# version 02
# baseline (speedup 1.0000x reference)
import contextlib
import numpy as np
import concourse.bass as bass
import concourse.mybir as mybir
from concourse.ap import AP
from concourse.bass_utils import run_bass_kernel_spmd

F32 = mybir.dt.float32
BF16 = mybir.dt.bfloat16
AF = mybir.ActivationFunctionType
ALU = mybir.AluOpType
AX = mybir.AxisListType

NCORES = 8
T = 4096
D = 1024
EPS = 1e-5

ENGINES = ("sync", "tensor", "vector", "scalar", "gpsimd")


def mkap(ap, offset_delta, pattern):
    return AP(ap.tensor, ap.offset + offset_delta, pattern)


class Ctx:

    def __init__(self, name="k"):
        self.nc = bass.Bass("TRN2", target_bir_lowering=False)
        self.name = name
        self.q = {e: [] for e in ENGINES}
        self._es = contextlib.ExitStack()
        self._sems = {}
        self._cnt = {}
        self._waited = {e: {} for e in ENGINES}
        self._block = None

    def dram(self, name, shape, dtype, kind="Internal"):
        return self.nc.dram_tensor(name, list(shape), dtype, kind=kind).ap()

    def inp(self, name, shape, dtype=F32):
        return self.dram(name, shape, dtype, kind="ExternalInput")

    def outp(self, name, shape, dtype=F32):
        return self.dram(name, shape, dtype, kind="ExternalOutput")

    def sbuf(self, name, shape, dtype, side="left"):
        if not hasattr(self, "_cms"):
            self._cms = {}
        cm = self.nc.sbuf_tensor(name, list(shape), dtype, side=side)
        h = cm.__enter__()
        self._cms[name] = (cm, side)
        return h

    def sbuf_free(self, *names):
        order = list(self._cms.keys())
        for n in sorted(names, key=lambda n: -order.index(n)):
            cm, _ = self._cms.pop(n)
            cm.__exit__(None, None, None)

    def psum(self, name, shape, dtype=F32):
        return self._es.enter_context(self.nc.psum_tensor(name, list(shape), dtype))

    def sem(self, name):
        if name not in self._sems:
            self._sems[name] = self._es.enter_context(self.nc.semaphore(name))
            self._cnt[name] = 0
        return self._sems[name]

    def _emit_waits(self, eng_name, deps):
        w = self._waited[eng_name]
        best = {}
        for (s, v) in deps:
            if v > best.get(s, 0):
                best[s] = v
        todo = []
        for s, v in best.items():
            if v > w.get(s, 0):
                w[s] = v
                todo.append((self._sems[s], v))
        if not todo:
            return None

        def f(eng):
            for (h, v) in todo:
                eng.wait_ge(h, v)
        return f

    def op(self, eng, fn, deps=(), n=1):
        sname = "P" + eng
        self.sem(sname)
        waits = self._emit_waits(eng, deps)
        self._cnt[sname] += 1
        tok = (sname, self._cnt[sname])
        sem = self._sems[sname]

        def f(engine):
            if waits:
                waits(engine)
            ins = fn(engine)
            assert ins is not None, f"op on {eng} returned None"
            ins.then_inc(sem, 1)
        self.q[eng].append(f)
        return tok

    def dma(self, qeng, out_ap, in_ap, deps=(), transpose=False, group=None, slow=False):
        sname = group if group is not None else "D" + qeng
        self.sem(sname)
        waits = self._emit_waits(qeng, deps)
        self._cnt[sname] += 16
        tok = (sname, self._cnt[sname])
        sem = self._sems[sname]

        def f(eng):
            if waits:
                waits(eng)
            if transpose:
                eng.dma_start(out=out_ap, in_=in_ap, transpose=True).then_inc(sem, 16)
            elif slow:
                eng.dma_start(out=out_ap, in_=in_ap,
                              allow_slow_non_contiguous=True).then_inc(sem, 16)
            else:
                eng.dma_start(out=out_ap, in_=in_ap).then_inc(sem, 16)
        self.q[qeng].append(f)
        return tok

    def constcol(self, value, name=None):
        key = ("cc", float(value))
        if not hasattr(self, "_consts"):
            self._consts = {}
        if key not in self._consts:
            nm = name or f"cc{len(self._consts)}"
            t = self.sbuf(nm, [128, 1], F32, side="right")
            tok = self.op("gpsimd", lambda g, t=t, value=value: g.memset(t[:], float(value)))
            self._consts[key] = (t, tok)
        return self._consts[key]

    def gtok(self, group):
        return (group, self._cnt.get(group, 0))

    def raw(self, eng, fn, deps=()):
        waits = self._emit_waits(eng, deps)

        def f(engine):
            if waits:
                waits(engine)
            fn(engine)
        self.q[eng].append(f)

    def wait_all(self, eng, toks):
        self.raw(eng, lambda e: None, deps=toks)

    def barrier(self):
        toks = [(s, v) for s, v in self._cnt.items() if v > 0]
        for e in ENGINES:
            self.wait_all(e, toks)

    def finish(self):
        if hasattr(self, "_cms"):
            for cm, _ in reversed(list(self._cms.values())):
                cm.__exit__(None, None, None)
            self._cms = {}
        blk = self._es.enter_context(self.nc.Block())
        for e in ENGINES:
            fns = self.q[e]
            if not fns:
                continue

            def runner(engine, fns=fns):
                for f in fns:
                    f(engine)
            getattr(blk, e)(runner)
        self._es.close()
        return self.nc


def ceil_div(a, b):
    return -(-a // b)



def shard_tm(x):
    return [np.ascontiguousarray(x[c // 2, (c % 2) * T:(c % 2 + 1) * T, :])
            for c in range(NCORES)]


def host_seg8():
    m = np.zeros((128, 8), np.float32)
    for h in range(8):
        m[16 * h:16 * h + 16, h] = 1.0
    import ml_dtypes
    return m.astype(ml_dtypes.bfloat16)


def _l1f_ins(c, prev, inputs, i):
    b, h = c // 2, c % 2
    if h == 0:
        halo = np.zeros((D, 513), np.float32)
    else:
        halo = prev[2 * b][:, T - 513:]
    xa = np.ascontiguousarray(np.concatenate([halo, prev[c]], axis=1))
    tabq, tabk = host_l1f_tables(h, T, inputs["L1_ffn_q_scale"][i],
                                 inputs["L1_ffn_k_scale"][i])
    W1 = inputs["L1_ffn_W1"][i]
    w1a = np.zeros((D, 2816), np.float32)
    w1a[:, :2730] = W1[:, :2730]
    w1g = np.zeros((D, 2816), np.float32)
    w1g[:, :2730] = W1[:, 2730:]
    w2 = np.zeros((2816, D), np.float32)
    w2[:2730] = inputs["L1_ffn_W2"][i]
    b1 = inputs["L1_ffn_b1"][i]
    b1a = np.zeros(2816, np.float32)
    b1a[:2730] = b1[:2730]
    b1g = np.zeros(2816, np.float32)
    b1g[:2730] = b1[2730:]
    return {"xa": xa, "wqkv": inputs["L1_ffn_Wqkv"][i], "wo": inputs["L1_ffn_Wo"][i],
            "lng": inputs["L1_ffn_ln_g"][i], "lnb": inputs["L1_ffn_ln_b"][i],
            "w1a": w1a, "w1g": w1g, "w2": w2, "b1a": b1a, "b1g": b1g,
            "b2": inputs["L1_ffn_b2"][i],
            "tabq": np.ascontiguousarray(tabq), "tabk": np.ascontiguousarray(tabk),
            "wmq": host_l1f_masks(h),
            "rolec": np.full((128, 1), 0.0 if h == 0 else 1.0, np.float32)}


def kernel(**inputs):
    import ml_dtypes
    BF = ml_dtypes.bfloat16
    cores = list(range(NCORES))
    Sn = 2 * T
    xs = shard_tm(inputs["L0_attn_x"])

    nc_prep = build_prep_ln()
    r = run_bass_kernel_spmd(nc_prep, [
        {"x": xs[c], "g": inputs["L0_attn_ln_g"], "b": inputs["L0_attn_ln_b"]}
        for c in cores], core_ids=cores)
    prep = r.results

    xcm = [np.ascontiguousarray(xs[c].T) for c in cores]

    def gau_in(c):
        b, h = c // 2, c % 2
        na_slot = np.concatenate([prep[2 * b + (1 - h)]["na"], prep[c]["na"]], 0)
        return {"na": np.ascontiguousarray(na_slot.T), "x": xcm[c],
                "mur": prep[c]["mu"][None, :], "r1r": prep[c]["r1"][None, :],
                "wh": inputs["L0_attn_Wh"], "bh": inputs["L0_attn_bh"],
                "wqk": inputs["L0_attn_Wqk"], "bqk": inputs["L0_attn_bqk"],
                "gamma": inputs["L0_attn_gamma"], "beta": inputs["L0_attn_beta"],
                "wo": inputs["L0_attn_Wo"], "bo": inputs["L0_attn_bo"],
                "mwide": host_l0_masks(h, Sn)}
    r = run_bass_kernel_spmd(build_l0_attn(), [gau_in(c) for c in cores], core_ids=cores)
    x1cm = [r.results[c]["xo"] for c in cores]
    x1tm = [np.ascontiguousarray(v.T) for v in x1cm]

    r = run_bass_kernel_spmd(nc_prep, [
        {"x": x1tm[c], "g": inputs["L0_ffn_ln1_w"], "b": inputs["L0_ffn_ln1_b"]}
        for c in cores], core_ids=cores)
    prep2 = r.results

    def moa_in(c):
        b, h = c // 2, c % 2
        na_g = np.concatenate([prep2[2 * b]["na"], prep2[2 * b + 1]["na"]], 0)
        order, bandm, maskK, rl, rr = host_moa_layout(h, T, Sn)
        SL = Sn + 128
        slot = np.zeros((SL, D), BF)
        col = 0
        for rng_ in order:
            if rng_ is None:
                col += 128
                continue
            a, bnd = rng_
            slot[col:col + (bnd - a)] = na_g[a:bnd]
            col += bnd - a
        return {"xn": np.ascontiguousarray(slot.T),
                "xnlo": np.ascontiguousarray(prep2[c]["nalo"].T),
                "x": x1cm[c],
                "mur": prep2[c]["mu"][None, :], "r1r": prep2[c]["r1"][None, :],
                "gate_w": inputs["L0_ffn_attn_gate_w"],
                "q_w": inputs["L0_ffn_q_w"],
                "q_b": np.ascontiguousarray(inputs["L0_ffn_q_b"].T),
                "k_w": inputs["L0_ffn_k_w"], "k_b": inputs["L0_ffn_k_b"],
                "v_w": inputs["L0_ffn_v_w"], "v_b": inputs["L0_ffn_v_b"],
                "o_w": inputs["L0_ffn_o_w"], "o_b": inputs["L0_ffn_o_b"],
                "rel_t": np.ascontiguousarray(inputs["L0_ffn_rel_table"].T),
                "bandm": bandm, "maskK": maskK, "roleL": rl, "roleR": rr}
    r = run_bass_kernel_spmd(build_moa(), [moa_in(c) for c in cores], core_ids=cores)
    x2cm = [r.results[c]["x2o"] for c in cores]

    r = run_bass_kernel_spmd(build_moe(), [
        {"x2": x2cm[c], "x": x1cm[c],
         "ln_w": inputs["L0_ffn_ln2_w"], "ln_b": inputs["L0_ffn_ln2_b"],
         "gate_w": inputs["L0_ffn_ff_gate_w"], "w1": inputs["L0_ffn_ff_w1"],
         "b1": inputs["L0_ffn_ff_b1"], "w2": inputs["L0_ffn_ff_w2"],
         "b2": inputs["L0_ffn_ff_b2"]} for c in cores], core_ids=cores)
    x2res = [r.results[c]["xo"] for c in cores]

    nc_glue_nb = build_glue(add_base=False, emit_bf16=True)
    r = run_bass_kernel_spmd(nc_glue_nb, [{"y": x2res[c]} for c in cores], core_ids=cores)
    xi_bf = [r.results[c]["ybf"] for c in cores]

    maa = np.ascontiguousarray(np.stack(
        [inputs["L1_attn_time_maa_x"].reshape(D), inputs["L1_attn_time_maa_w"].reshape(D),
         inputs["L1_attn_time_maa_k"].reshape(D), inputs["L1_attn_time_maa_v"].reshape(D),
         inputs["L1_attn_time_maa_r"].reshape(D)], 0))
    seg8 = host_seg8()

    def rwkv_a_in(c):
        b, h = c // 2, c % 2
        halo = np.zeros((D, 1), BF) if h == 0 else xi_bf[2 * b][:, T - 1:T]
        xin = np.ascontiguousarray(np.concatenate([halo, xi_bf[c]], axis=1))
        return {"xin": xin, "maa": maa, "tmw1": inputs["L1_attn_time_maa_w1"],
                "tmw2": inputs["L1_attn_time_maa_w2"],
                "decay_u": inputs["L1_attn_time_decay"].reshape(D),
                "dw1": inputs["L1_attn_time_decay_w1"], "dw2": inputs["L1_attn_time_decay_w2"],
                "wr": inputs["L1_attn_W_r"], "wk": inputs["L1_attn_W_k"],
                "wv": inputs["L1_attn_W_v"],
                "u_in": np.ascontiguousarray(inputs["L1_attn_time_faaaa"].reshape(D)),
                "seg8i": seg8}
    r = run_bass_kernel_spmd(build_rwkv_a(), [rwkv_a_in(c) for c in cores], core_ids=cores)
    ra = r.results

    def rwkv_b_in(c):
        b, h = c // 2, c % 2
        sprev = np.zeros((D, 16), np.float32) if h == 0 else ra[2 * b]["se"]
        return {"y1": ra[c]["y1"], "rc": ra[c]["rc"], "sprev": sprev, "x": x2res[c],
                "ln_g": inputs["L1_attn_ln_g"], "ln_b": inputs["L1_attn_ln_b"],
                "wo": inputs["L1_attn_W_o"], "seg8i": seg8}
    r = run_bass_kernel_spmd(build_rwkv_b(), [rwkv_b_in(c) for c in cores], core_ids=cores)
    x3cm = [r.results[c]["xo"] for c in cores]

    r = run_bass_kernel_spmd(nc_glue_nb, [{"y": x3cm[c]} for c in cores], core_ids=cores)
    prev = [r.results[c]["xo"] for c in cores]

    nc_l1f = build_l1f_layer()
    for i in range(6):
        r = run_bass_kernel_spmd(nc_l1f, [_l1f_ins(c, prev, inputs, i) for c in cores],
                                 core_ids=cores)
        prev = [r.results[c]["xo2"] for c in cores]

    r = run_bass_kernel_spmd(build_glue(add_base=True), [
        {"y": prev[c], "x": x3cm[c]} for c in cores], core_ids=cores)
    out = np.empty((4, Sn, D), np.float32)
    for c in cores:
        out[c // 2, (c % 2) * T:(c % 2 + 1) * T, :] = r.results[c]["xo"].T
    return out



def build_prep_ln(Tn=T, Dn=D):
    c = Ctx("prep_ln")
    x = c.inp("x", [Tn, Dn])
    g = c.inp("g", [Dn])
    b = c.inp("b", [Dn])
    na_o = c.outp("na", [Tn, Dn], BF16)
    nalo_o = c.outp("nalo", [Tn, Dn], BF16)
    mu_o = c.outp("mu", [Tn])
    r1_o = c.outp("r1", [Tn])

    nt = Tn // 128
    xt = c.sbuf("xt", [128, nt * Dn], F32)
    sq = c.sbuf("sq", [128, Dn], F32)
    mu = c.sbuf("mu_s", [128, nt], F32)
    ssq = c.sbuf("ssq", [128, nt], F32)
    var = c.sbuf("var", [128, nt], F32)
    r1 = c.sbuf("r1_s", [128, nt], F32)
    rho = c.sbuf("rho", [128, nt], F32)
    gb = c.sbuf("gb", [128, Dn], F32)
    bb = c.sbuf("bb", [128, Dn], F32)
    grow = c.sbuf("grow", [1, Dn], F32)
    brow = c.sbuf("brow", [1, Dn], F32)
    ones_row = c.sbuf("ones_row", [1, 128], F32)
    na_ab = [c.sbuf("na_a", [128, Dn], BF16), c.sbuf("na_b", [128, Dn], BF16)]
    lo_ab = [c.sbuf("lo_a", [128, Dn], BF16), c.sbuf("lo_b", [128, Dn], BF16)]
    tmpf = c.sbuf("tmpf", [128, Dn], F32)
    tmp = c.sbuf("tmp", [128, Dn], F32)
    psb = [c.psum(f"psb{i}", [128, 512]) for i in range(2)]

    c.dma("sync", grow[:], g.rearrange("(one d) -> one d", one=1), group="gi")
    c.dma("sync", brow[:], b.rearrange("(one d) -> one d", one=1), group="gi")
    c.dma("sync", xt[:].rearrange("p (n d) -> p n d", d=Dn), x.rearrange("(n p) d -> p n d", p=128), group="gi")
    t_one = c.op("gpsimd", lambda e: e.memset(ones_row[:], 1.0))
    gi = c.gtok("gi")
    bco = []
    prev = [t_one, gi]
    for j in range(Dn // 512):
        tk1 = c.op("tensor", lambda t, j=j: t.matmul(psb[0][:, :], ones_row[:], grow[:, j * 512:(j + 1) * 512],
                                                     start=True, stop=True), deps=prev)
        v1 = c.op("vector", lambda v, j=j: v.tensor_copy(gb[:, j * 512:(j + 1) * 512], psb[0][:]), deps=[tk1])
        tk2 = c.op("tensor", lambda t, j=j: t.matmul(psb[1][:, :], ones_row[:], brow[:, j * 512:(j + 1) * 512],
                                                     start=True, stop=True), deps=prev)
        v2 = c.op("vector", lambda v, j=j: v.tensor_copy(bb[:, j * 512:(j + 1) * 512], psb[1][:]), deps=[tk2])
        prev = [v1, v2, gi, t_one]
        bco = [v1, v2]

    st_toks = []
    prev_sq = None
    prev_w = []
    for i in range(nt):
        xi = xt[:, i * Dn:(i + 1) * Dn]
        rs = c.op("vector", lambda v, xi=xi, i=i: v.reduce_sum(mu[:, i:i + 1], xi, axis=AX.X), deps=[gi])
        sm = c.op("vector", lambda v, i=i: v.tensor_scalar(mu[:, i:i + 1], mu[:, i:i + 1], 1.0 / Dn, None, ALU.mult), deps=[rs])
        qq = c.op("scalar", lambda s, xi=xi, i=i: s.activation(sq[:], xi, AF.Square, accum_out=ssq[:, i:i + 1]),
                  deps=[gi] + ([prev_sq] if prev_sq else []))
        prev_sq = qq
        v1 = c.op("vector", lambda v, i=i: v.tensor_tensor(var[:, i:i + 1], mu[:, i:i + 1], mu[:, i:i + 1], ALU.mult), deps=[sm])
        v2 = c.op("vector", lambda v, i=i: v.scalar_tensor_tensor(var[:, i:i + 1], ssq[:, i:i + 1], 1.0 / Dn,
                                                                  var[:, i:i + 1], ALU.mult, ALU.subtract), deps=[qq, v1])
        epsc, epst = c.constcol(EPS)
        eps2c, eps2t = c.constcol(EPS * (1.0 + EPS))
        s1 = c.op("scalar", lambda s, i=i, epsc=epsc: s.activation(r1[:, i:i + 1], var[:, i:i + 1], AF.Sqrt, bias=epsc[:, 0:1]), deps=[v2, epst])
        s2 = c.op("vector", lambda v, i=i: v.reciprocal(r1[:, i:i + 1], r1[:, i:i + 1]), deps=[s1])
        s3 = c.op("scalar", lambda s, i=i, eps2c=eps2c: s.activation(rho[:, i:i + 1], var[:, i:i + 1], AF.Sqrt,
                                                        scale=float(1.0 + EPS), bias=eps2c[:, 0:1]), deps=[v2, eps2t])
        s4 = c.op("vector", lambda v, i=i: v.reciprocal(rho[:, i:i + 1], rho[:, i:i + 1]), deps=[s3])
        w1 = c.op("vector", lambda v, xi=xi, i=i: v.tensor_scalar(tmp[:], xi, mu[:, i:i + 1], rho[:, i:i + 1],
                                                                  ALU.subtract, ALU.mult), deps=[s4, sm] + prev_w)
        w2 = c.op("vector", lambda v: v.tensor_tensor(tmp[:], tmp[:], gb[:], ALU.mult), deps=[w1] + bco)
        nai = na_ab[i % 2]
        loi = lo_ab[i % 2]
        grp = "stA" if i % 2 == 0 else "stB"
        war = [c.gtok(grp)] if i >= 2 else []
        w3f = c.op("vector", lambda v: v.tensor_tensor(tmpf[:], tmp[:], bb[:], ALU.add), deps=[w2] + war + prev_w)
        w3 = c.op("vector", lambda v, nai=nai: v.tensor_copy(nai[:], tmpf[:]), deps=[w3f])
        w4 = c.op("vector", lambda v, nai=nai, loi=loi: v.tensor_tensor(loi[:], tmpf[:], nai[:], ALU.subtract),
                  deps=[w3])
        prev_w = [w4]
        c.dma("gpsimd", na_o[i * 128:(i + 1) * 128, :], nai[:], deps=[w3], group=grp)
        c.dma("gpsimd", nalo_o[i * 128:(i + 1) * 128, :], loi[:], deps=[w4], group=grp)
        st_toks.append((sm, s2, i))
    for (sm, s2, i) in st_toks:
        c.dma("gpsimd", mu_o[i * 128:(i + 1) * 128].rearrange("(p one) -> p one", one=1),
              mu[:, i:i + 1], deps=[sm], group="st")
        c.dma("gpsimd", r1_o[i * 128:(i + 1) * 128].rearrange("(p one) -> p one", one=1),
              r1[:, i:i + 1], deps=[s2], group="st")
    c.wait_all("gpsimd", [c.gtok("st"), c.gtok("stA"), c.gtok("stB")])
    return c.finish()



def build_l0_attn(Tn=T, Sn=2 * T, nglob=8192):
    HID = 2048
    Jn = 5
    NQQ = Tn // 512
    NKB = Sn // 128
    HS = Sn // 2
    assert HS == Tn

    c = Ctx("l0attn")
    na = c.inp("na", [D, Sn], BF16)
    x = c.inp("x", [D, Tn])
    mur = c.inp("mur", [1, Tn])
    r1r = c.inp("r1r", [1, Tn])
    wh = c.inp("wh", [D, 2 * HID])
    bh = c.inp("bh", [2 * HID])
    wqk = c.inp("wqk", [D, 128])
    bqk = c.inp("bqk", [128])
    gamma = c.inp("gamma", [2, 128])
    beta = c.inp("beta", [2, 128])
    wo = c.inp("wo", [HID, D])
    bo = c.inp("bo", [D])
    mwide = c.inp("mwide", [Jn, 128, 512], BF16)
    xo = c.outp("xo", [D, Tn])

    v_hbm = c.dram("v_hbm", [Sn, HID], BF16)
    g_hbm = c.dram("g_hbm", [Tn, HID], BF16)
    z_hbm = c.dram("z_hbm", [Tn, HID], BF16)

    na_sb = c.sbuf("na_sb", [128, 8 * HS], BF16)
    whv = c.sbuf("whv", [128, 8 * HID], BF16)
    bh_vb = c.sbuf("bh_vb", [128, HID], F32)
    bh_row = c.sbuf("bh_row", [1, 2 * HID], F32)
    onesr = c.sbuf("onesr", [1, 128], F32, side="right")
    onesc = c.sbuf("onesc", [128, 1], F32, side="right")
    sc1 = c.sbuf("sc1", [128, 2048], F32, side="right")
    sc2 = c.sbuf("sc2", [128, 2048], F32, side="right")
    wqk_sb = c.sbuf("wqk_sb", [128, 8 * 128], BF16)
    bqk_c = c.sbuf("bqk_c", [128, 1], F32)
    gb_c = c.sbuf("gb_c", [128, 4], F32)
    q_sb = c.sbuf("q_sb", [128, Sn], BF16, side="right")
    k_sb = c.sbuf("k_sb", [128, Sn], BF16, side="right")
    vt_sb = [c.sbuf(f"vt{i}", [128, HID], BF16) for i in range(2)]
    m_sb = c.sbuf("m_sb", [128, Jn * 512], BF16, side="right")
    PA = c.psum("PA", [128, 2048])
    PB = c.psum("PB", [128, 2048])
    ps_v = [PA, PB]
    ps_qk = PB[:, 0:512]
    ps_b = PB[:, 512:1024]

    t_o1 = c.op("gpsimd", lambda g: g.memset(onesr[:], 1.0))
    t_o2 = c.op("gpsimd", lambda g: g.memset(onesc[:], 1.0))

    c.dma("sync", bh_row[:], bh.rearrange("(one n) -> one n", one=1), group="small")
    c.dma("sync", bqk_c[:], bqk.rearrange("(p one) -> p one", one=1), group="small")
    c.dma("sync", gb_c[:, 0:1], gamma[0:1, :].rearrange("one p -> p one"), group="small")
    c.dma("sync", gb_c[:, 1:2], beta[0:1, :].rearrange("one p -> p one"), group="small")
    c.dma("sync", gb_c[:, 2:3], gamma[1:2, :].rearrange("one p -> p one"), group="small")
    c.dma("sync", gb_c[:, 3:4], beta[1:2, :].rearrange("one p -> p one"), group="small")
    c.dma("sync", m_sb[:].rearrange("p (j n) -> p j n", n=512), mwide.rearrange("j p n -> p j n"), group="small")
    for cc in range(8):
        c.dma("gpsimd", wqk_sb[:, cc * 128:(cc + 1) * 128], wqk[cc * 128:(cc + 1) * 128, :], group="small2")
    sm = [c.gtok("small"), c.gtok("small2")]

    def bh_bcast(part):
        toks = []
        for j in range(HID // 512):
            tk = c.op("tensor", lambda t, j=j, part=part: t.matmul(
                ps_b, onesr[:], bh_row[:, part * HID + j * 512: part * HID + (j + 1) * 512],
                start=True, stop=True), deps=[t_o1] + sm + toks)
            toks.append(c.op("vector", lambda v, j=j: v.tensor_copy(bh_vb[:, j * 512:(j + 1) * 512], ps_b),
                             deps=[tk]))
        return toks

    def phase_a_half(half, wsel, out_hbm, out_off, bh_dep, qk=False):
        c.barrier()
        c.dma("sync", na_sb[:].rearrange("p (cc t) -> p cc t", t=HS),
              na[:, half * HS:(half + 1) * HS].rearrange("(cc p) t -> p cc t", p=128),
              group="lna")
        lna = [c.gtok("lna")]
        c.dma("gpsimd", whv[:].rearrange("p (cc n) -> p cc n", n=HID),
              wh[:, wsel * HID:(wsel + 1) * HID].rearrange("(cc p) n -> p cc n", p=128),
              group="lwh")
        lw = [c.gtok("lwh")]
        prev_evac = {0: [], 1: []}
        for tt in range(HS // 128):
            pv = ps_v[tt % 2][:]
            mmdep = lna + lw + bh_dep + prev_evac[tt % 2]
            mm = None
            for cc in range(8):
                st = na_sb[:, cc * HS + tt * 128: cc * HS + (tt + 1) * 128]
                c.raw("tensor", lambda t, st=st: t.ldweights(st), deps=mmdep if cc == 0 else [])
                for nn in range(HID // 512):
                    mm = c.op("tensor", lambda t, st=st, cc=cc, nn=nn, pv=pv: t.matmul(
                        pv[:, nn * 512:(nn + 1) * 512], st,
                        whv[:, cc * HID + nn * 512: cc * HID + (nn + 1) * 512],
                        start=(cc == 0), stop=(cc == 7)))
            vt = vt_sb[tt % 2]
            scp = sc1 if tt % 2 == 0 else sc2
            e1 = c.op("vector", lambda v, pv=pv, scp=scp: v.tensor_tensor(scp[:, 0:HID], pv, bh_vb[:], ALU.add),
                      deps=[mm] + bh_dep + prev_evac[tt % 2])
            e2 = c.op("scalar", lambda s, vt=vt, scp=scp: s.activation(vt[:], scp[:, 0:HID], AF.Silu),
                      deps=[e1] + prev_evac[tt % 2])
            c.dma("gpsimd", out_hbm[out_off + tt * 128: out_off + (tt + 1) * 128, :], vt[:],
                  deps=[e2], group=f"vst{half}{wsel}{tt % 2}")
            prev_evac[tt % 2] = [e1, e2, c.gtok(f"vst{half}{wsel}{tt % 2}")]
        if qk:
            c.barrier()
            prev = []
            for tcol in range(HS // 512):
                mm = None
                for cc in range(8):
                    wst = wqk_sb[:, cc * 128:(cc + 1) * 128]
                    c.raw("tensor", lambda t, wst=wst: t.ldweights(wst),
                          deps=(lna + sm + prev) if cc == 0 else [])
                    mm = c.op("tensor", lambda t, wst=wst, cc=cc, tcol=tcol: t.matmul(
                        ps_qk, wst, na_sb[:, cc * HS + tcol * 512: cc * HS + (tcol + 1) * 512],
                        start=(cc == 0), stop=(cc == 7)))
                sil = c.op("scalar", lambda s: s.activation(sc2[:, 0:512], ps_qk, AF.Silu,
                                                            bias=bqk_c[:, 0:1]), deps=[mm] + prev + sm)
                col0 = half * HS + tcol * 512
                kaf = c.op("vector", lambda v, col0=col0: v.tensor_scalar(
                    k_sb[:, col0:col0 + 512], sc2[:, 0:512], gb_c[:, 2:3], gb_c[:, 3:4], ALU.mult, ALU.add),
                    deps=[sil] + sm)
                qaf = c.op("vector", lambda v, col0=col0: v.tensor_scalar(
                    q_sb[:, col0:col0 + 512], sc2[:, 0:512], gb_c[:, 0:1], gb_c[:, 1:2], ALU.mult, ALU.add),
                    deps=[sil])
                prev = [sil, kaf, qaf]

    bhd = bh_bcast(0)
    phase_a_half(0, 0, v_hbm, 0, bhd, qk=True)
    phase_a_half(1, 0, v_hbm, HS, bhd, qk=True)
    c.barrier()
    bhd_g = bh_bcast(1)
    phase_a_half(1, 1, g_hbm, 0, bhd_g)
    c.barrier()

    c.sbuf_free("na_sb", "whv", "bh_vb", "bh_row", "wqk_sb", "bqk_c", "gb_c", "vt0", "vt1")
    at_sb = c.sbuf("at_sb", [128, NKB * 512], BF16)
    vblk = [c.sbuf(f"vblk{i}", [128, 8 * 1024], BF16) for i in range(2)]
    gate_q = c.sbuf("gate_q", [128, 4 * HID], BF16)
    zin_t = c.sbuf("zin_t", [128, 1024], BF16)
    ps_s = PA[:, 0:512]
    ps_att = [PA[:, 0:1024], PA[:, 1024:2048], PB[:, 0:1024], PB[:, 1024:2048]]
    kb_end = lambda qq: Sn // 256 + 4 * qq + 4

    for qq in range(NQQ):
        c.barrier()
        prev_s = []
        at_toks = []
        for kb in range(kb_end(qq)):
            lhs = k_sb[:, kb * 128:(kb + 1) * 128]
            rhs = q_sb[:, HS + qq * 512: HS + (qq + 1) * 512]
            mm = c.op("tensor", lambda t, lhs=lhs, rhs=rhs: (
                t.ldweights(lhs), t.matmul(ps_s, lhs, rhs, start=True, stop=True))[1],
                deps=prev_s)
            rl = c.op("scalar", lambda s: s.activation(sc1[:, 0:512], ps_s, AF.Relu,
                                                       scale=float(1.0 / nglob)), deps=[mm] + prev_s)
            sq = c.op("vector", lambda v: v.tensor_tensor(sc2[:, 0:512], sc1[:, 0:512], sc1[:, 0:512],
                                                          ALU.mult), deps=[rl] + prev_s)
            if kb < Sn // 256:
                Jc = 4
            else:
                dp = kb - Sn // 256 - 4 * qq
                Jc = dp if dp >= 0 else None
            if Jc is None:
                atk = c.op("vector", lambda v, kb=kb: v.tensor_copy(
                    at_sb[:, kb * 512:(kb + 1) * 512], sc2[:, 0:512]), deps=[sq])
            else:
                atk = c.op("vector", lambda v, kb=kb, Jc=Jc: v.tensor_tensor(
                    at_sb[:, kb * 512:(kb + 1) * 512], sc2[:, 0:512], m_sb[:, Jc * 512:(Jc + 1) * 512],
                    ALU.mult), deps=[sq])
            prev_s = [rl, sq, atk]
            at_toks.append(atk)
        lg = c.dma("sync", gate_q[:].rearrange("p (n m) -> p n m", m=HID),
                   g_hbm[qq * 512:(qq + 1) * 512, :].rearrange("(n p) m -> p n m", p=128),
                   group="gq")
        lgt = [c.gtok("gq")]
        lvb = {0: [], 1: []}
        for hh in range(2):
            ke = kb_end(qq)
            nblk = ceil_div(ke, 8)
            first = True
            mmtok = None
            for kbb in range(nblk):
                kb0, kb1 = kbb * 8, min(kbb * 8 + 8, ke)
                vb = vblk[kbb % 2]
                c.dma("sync", vb[:, 0:(kb1 - kb0) * 1024].rearrange("p (n m) -> p n m", m=1024),
                      v_hbm[kb0 * 128:kb1 * 128, hh * 1024:(hh + 1) * 1024].rearrange("(n p) m -> p n m", p=128),
                      deps=lvb[kbb % 2], group=f"vb{kbb % 2}")
                lvt = [c.gtok(f"vb{kbb % 2}")]
                for kb in range(kb0, kb1):
                    for qi in range(4):
                        lhs = at_sb[:, kb * 512 + qi * 128: kb * 512 + (qi + 1) * 128]
                        deps = (at_toks + lvt) if (kb == kb0 and qi == 0) else []
                        c.raw("tensor", lambda t, lhs=lhs: t.ldweights(lhs), deps=deps)
                        for h2 in range(2):
                            mmtok = c.op("tensor", lambda t, lhs=lhs, vb=vb, kb=kb, kb0=kb0, h2=h2, qi=qi, first=first, last=(kb == ke - 1): t.matmul(
                                ps_att[qi][:, h2 * 512:(h2 + 1) * 512], lhs,
                                vb[:, (kb - kb0) * 1024 + h2 * 512: (kb - kb0) * 1024 + h2 * 512 + 512],
                                start=first, stop=last))
                    first = False
                lvb[kbb % 2] = [mmtok]
            evs = []
            for qi in range(4):
                ev = c.op("vector", lambda v, qi=qi, hh=hh: v.tensor_tensor(
                    zin_t[:], ps_att[qi], gate_q[:, qi * HID + hh * 1024: qi * HID + hh * 1024 + 1024],
                    ALU.mult), deps=[mmtok] + lgt + [c.gtok("zst")])
                c.dma("gpsimd", z_hbm[qq * 512 + qi * 128: qq * 512 + (qi + 1) * 128,
                                      hh * 1024:(hh + 1) * 1024], zin_t[:], deps=[ev], group="zst")
                evs.append(ev)
            c.wait_all("tensor", evs)
        c.wait_all("vector", [c.gtok("zst")])
    c.barrier()

    c.sbuf_free("at_sb", "vblk0", "vblk1", "gate_q", "zin_t", "m_sb", "k_sb", "q_sb")
    wo_sb = c.sbuf("wo_sb", [128, 16 * 1024], BF16)
    Tn2 = Tn // 2
    zc_sb = c.sbuf("zc_sb", [128, 16 * Tn2], BF16)
    z_sb = c.sbuf("z_sb", [128, 8 * Tn], BF16, side="right")
    QW = min(512, Tn // 2)
    ps_z = [PA[:, i * QW:(i + 1) * QW] for i in range(4)]
    for cc16 in range(16):
        c.dma("gpsimd", wo_sb[:, cc16 * 1024:(cc16 + 1) * 1024],
              wo[cc16 * 128:(cc16 + 1) * 128, :], group="wol")
    wol = [c.gtok("wol")]
    for qh in range(2):
        for hc in range(16):
            c.dma("sync", zc_sb[:, hc * Tn2:(hc + 1) * Tn2],
                  z_hbm[qh * Tn2:(qh + 1) * Tn2, hc * 128:(hc + 1) * 128],
                  transpose=True, group=f"zc{qh}")
        lzt = [c.gtok(f"zc{qh}")]
        zdone = []
        for occ in range(8):
            mm = None
            for hc in range(16):
                wst = wo_sb[:, hc * 1024 + occ * 128: hc * 1024 + occ * 128 + 128]
                c.raw("tensor", lambda t, wst=wst: t.ldweights(wst),
                      deps=(lzt + wol + zdone) if hc == 0 else [])
                for qcol in range(Tn2 // QW):
                    mm = c.op("tensor", lambda t, wst=wst, hc=hc, qcol=qcol: t.matmul(
                        ps_z[qcol], wst, zc_sb[:, hc * Tn2 + qcol * QW: hc * Tn2 + (qcol + 1) * QW],
                        start=(hc == 0), stop=(hc == 15)))
            zdone = []
            for qcol in range(Tn2 // QW):
                zdone.append(c.op("vector", lambda v, occ=occ, qh=qh, qcol=qcol: v.tensor_copy(
                    z_sb[:, occ * Tn + qh * Tn2 + qcol * QW: occ * Tn + qh * Tn2 + qcol * QW + QW],
                    ps_z[qcol]), deps=[mm]))
        c.barrier()

    c.sbuf_free("zc_sb", "wo_sb")
    bo_c = c.sbuf("bo_c", [128, 8], F32)
    mu_row = c.sbuf("mu_rowD", [1, Tn], F32)
    r1_row = c.sbuf("r1_rowD", [1, Tn], F32)
    yrow = c.sbuf("yrow", [1, 512], F32)
    y2row = c.sbuf("y2row", [1, 512], F32)
    mb = c.sbuf("mb", [128, 512], F32)
    rb = c.sbuf("rb", [128, 512], F32)
    myb = c.sbuf("myb", [128, 512], F32)
    ryb = c.sbuf("ryb", [128, 512], F32)
    y0 = c.sbuf("y0", [128, 8 * 512], F32)
    xin_t = c.sbuf("xin_t", [128, 512], F32)
    xcc = c.sbuf("xcc", [128, 8 * 512], F32)
    xot = c.sbuf("xot", [128, 512], F32)
    sqq = c.sbuf("sqq", [128, 8 * 512], F32)
    ps_st = PB[0:1, 0:512]
    ps_st2 = PB[0:1, 512:1024]
    ps_bc = PA[:, 0:512]

    for occ in range(8):
        c.dma("sync", bo_c[:, occ:occ + 1], bo[occ * 128:(occ + 1) * 128].rearrange("(p one) -> p one", one=1),
              group="sm3")
    c.dma("sync", mu_row[:], mur[:], group="sm3")
    c.dma("sync", r1_row[:], r1r[:], group="sm3")
    sm3 = [c.gtok("sm3")]

    for qcol in range(Tn // 512):
        qs = qcol * 512
        c.barrier()
        lx = [c.dma("sync", xcc[:].rearrange("p (cc n) -> p cc n", n=512),
                    x[:, qs:qs + 512].rearrange("(cc p) n -> p cc n", p=128), group="lxe")]
        lxt = [c.gtok("lxe")]
        b1 = c.op("tensor", lambda t, qs=qs: t.matmul(ps_bc, onesr[:], mu_row[:, qs:qs + 512],
                                                      start=True, stop=True), deps=sm3 + [t_o1])
        b1c = c.op("vector", lambda v: v.tensor_copy(mb[:], ps_bc), deps=[b1])
        b2 = c.op("tensor", lambda t, qs=qs: t.matmul(ps_bc, onesr[:], r1_row[:, qs:qs + 512],
                                                      start=True, stop=True), deps=[b1c])
        b2c = c.op("vector", lambda v: v.tensor_copy(rb[:], ps_bc), deps=[b2])
        ytoks = []
        for cc in range(8):
            t1 = c.op("vector", lambda v, cc=cc: v.tensor_tensor(
                xin_t[:], xcc[:, cc * 512:(cc + 1) * 512], mb[:], ALU.subtract), deps=lxt + [b1c] + ytoks[-1:])
            t2 = c.op("vector", lambda v: v.tensor_tensor(xin_t[:], xin_t[:], rb[:], ALU.mult), deps=[t1, b2c])
            t3 = c.op("vector", lambda v, cc=cc, qs=qs: v.scalar_tensor_tensor(
                y0[:, cc * 512:(cc + 1) * 512], z_sb[:, cc * Tn + qs: cc * Tn + qs + 512], bo_c[:, cc:cc + 1],
                xin_t[:], ALU.add, ALU.add), deps=[t2] + sm3)
            ytoks.append(t3)
        sqs = []
        for cc in range(8):
            sqs.append(c.op("scalar", lambda s, cc=cc: s.activation(sqq[:, cc * 512:(cc + 1) * 512],
                                                                    y0[:, cc * 512:(cc + 1) * 512], AF.Square),
                            deps=[ytoks[cc]]))
        st1 = None
        st2 = None
        for cc in range(8):
            st1 = c.op("tensor", lambda t, cc=cc: t.matmul(ps_st, onesc[:], y0[:, cc * 512:(cc + 1) * 512],
                                                           start=(cc == 0), stop=(cc == 7)),
                       deps=[ytoks[cc], t_o2] + ([st1] if st1 else []))
            st2 = c.op("tensor", lambda t, cc=cc: t.matmul(ps_st2, onesc[:], sqq[:, cc * 512:(cc + 1) * 512],
                                                           start=(cc == 0), stop=(cc == 7)),
                       deps=[sqs[cc]] + ([st2] if st2 else []))
        epsc, epst = c.constcol(EPS)
        e1 = c.op("vector", lambda v: v.tensor_scalar(y2row[:], ps_st2, 1.0 / D, None, ALU.mult), deps=[st2])
        e2 = c.op("vector", lambda v: v.tensor_scalar(yrow[:], ps_st, 1.0 / D, None, ALU.mult), deps=[st1])
        e3 = c.op("vector", lambda v: v.tensor_tensor(myb[0:1, 0:512], yrow[:], yrow[:], ALU.mult), deps=[e2])
        e4 = c.op("vector", lambda v: v.tensor_tensor(y2row[:], y2row[:], myb[0:1, 0:512], ALU.subtract),
                  deps=[e1, e3])
        e5 = c.op("scalar", lambda s: s.activation(y2row[:], y2row[:], AF.Sqrt, bias=epsc[0:1, 0:1]),
                  deps=[e4, epst])
        e6 = c.op("vector", lambda v: v.reciprocal(y2row[:], y2row[:]), deps=[e5])
        b3 = c.op("tensor", lambda t: t.matmul(ps_bc, onesr[:], yrow[:], start=True, stop=True),
                  deps=[e6, b2c])
        b3c = c.op("vector", lambda v: v.tensor_copy(myb[:], ps_bc), deps=[b3])
        b4 = c.op("tensor", lambda t: t.matmul(ps_bc, onesr[:], y2row[:], start=True, stop=True), deps=[b3c])
        b4c = c.op("vector", lambda v: v.tensor_copy(ryb[:], ps_bc), deps=[b4])
        sto = []
        for cc in range(8):
            u1 = c.op("vector", lambda v, cc=cc: v.tensor_tensor(xot[:], y0[:, cc * 512:(cc + 1) * 512],
                                                                 myb[:], ALU.subtract),
                      deps=[b3c, b4c] + sto[-1:])
            u2 = c.op("vector", lambda v: v.tensor_tensor(xot[:], xot[:], ryb[:], ALU.mult), deps=[u1])
            u3 = c.op("vector", lambda v, cc=cc: v.tensor_tensor(xot[:], xot[:], xcc[:, cc * 512:(cc + 1) * 512],
                                                                 ALU.add), deps=[u2])
            sto.append(c.dma("gpsimd", xo[cc * 128:(cc + 1) * 128, qs:qs + 512], xot[:], deps=[u3], group="xst"))
    c.wait_all("gpsimd", [c.gtok("xst")])
    return c.finish()


def host_l0_masks(h, Sn):
    import ml_dtypes
    m = np.zeros((5, 128, 512), dtype=np.float32)
    tril = (np.arange(128)[:, None] <= np.arange(128)[None, :]).astype(np.float32)
    for dp in range(4):
        for sub in range(4):
            dd = dp - sub
            blk = np.ones((128, 128), np.float32) if dd < 0 else (tril if dd == 0 else 0.0)
            m[dp, :, sub * 128:(sub + 1) * 128] = blk
    m[4] = 1.0 if h == 1 else 0.0
    return m.astype(ml_dtypes.bfloat16)



def build_moa(Tn=T, Sn=2 * T, debug=False):
    EA, HD = 8, 128
    NQT = Tn // 128
    NQQ = Tn // 512
    SL = Sn + 128
    NKB = SL // 128
    SC = float(HD ** -0.5)
    MB = 30.0
    KPOST = Tn // 128 + 1

    c = Ctx("moa")
    xn = c.inp("xn", [D, SL], BF16)
    x = c.inp("x", [D, Tn])
    mur = c.inp("mur", [1, Tn])
    r1r = c.inp("r1r", [1, Tn])
    gate_w = c.inp("gate_w", [D, EA])
    q_w = c.inp("q_w", [EA, D, HD])
    q_b = c.inp("q_b", [HD, EA])
    k_w = c.inp("k_w", [D, HD])
    k_b = c.inp("k_b", [HD])
    v_w = c.inp("v_w", [D, HD])
    v_b = c.inp("v_b", [HD])
    o_w = c.inp("o_w", [EA, HD, D])
    o_b = c.inp("o_b", [EA, D])
    rel_t = c.inp("rel_t", [HD, 129])
    xnlo = c.inp("xnlo", [D, Tn], BF16)
    bandm = c.inp("bandm", [3, 4, 128, 768], BF16)
    maskK = c.inp("maskK", [2, 4, 128, 768], BF16)
    roleL = c.inp("roleL", [1, 1])
    roleR = c.inp("roleR", [1, 1])
    x2o = c.outp("x2o", [D, Tn])

    rel_hbm = c.dram("rel_hbm", [EA, (Tn + 256) * 129 + 2048], BF16)
    st_hbm = c.dram("st_hbm", [Tn, 128], BF16)

    onesr = c.sbuf("onesr", [1, 128], F32, side="right")
    onescb = c.sbuf("onescb", [128, 1], BF16, side="right")
    k_cm = c.sbuf("k_cm", [128, SL], BF16, side="right")
    v_tm = c.sbuf("v_tm", [128, NKB * 128], BF16, side="right")
    clcr = c.sbuf("clcr", [128, EA * NQT * 2], F32, side="right")
    strows = c.sbuf("strows", [128, Tn], BF16, side="right")
    grows = c.sbuf("grows", [8, Tn], BF16, side="right")
    rlc = c.sbuf("rlc", [128, 2], F32, side="right")
    q_hbm = c.dram("q_hbm", [EA, 128, Tn], BF16)
    ow_sb = c.sbuf("ow_sb", [128, EA * D], BF16, side="right")
    ob_sb = c.sbuf("ob_sb", [8, D], BF16, side="right")
    sc1 = c.sbuf("sc1", [128, 768], F32, side="right")
    sc2 = c.sbuf("sc2", [128, 768], F32, side="right")
    scE = c.sbuf("scE", [128, 512], BF16, side="right")
    PA = c.psum("PA", [128, 2048])
    PB = c.psum("PB", [128, 2048])

    onesrb = c.sbuf("onesrb", [1, 128], BF16, side="right")
    t_o1 = c.op("gpsimd", lambda g: g.memset(onesr[:], 1.0))
    t_o2 = c.op("gpsimd", lambda g: g.memset(onescb[:], 1.0))
    t_o3 = c.op("gpsimd", lambda g: g.memset(onesrb[:], 1.0))
    negMc, negMt = c.constcol(-MB)

    xn_sb = c.sbuf("xn_sb", [128, 8 * (Tn + 128)], BF16)
    wsmall = c.sbuf("wsmall", [128, 8 * 264], BF16)
    kvb = c.sbuf("kvb", [128, 2], F32)
    qwb = c.sbuf("qwb", [128, 8 * EA * HD], BF16)
    qbb = c.sbuf("qbb", [128, EA], F32)
    relT = c.sbuf("relT", [128, 132], BF16)
    zro = c.sbuf("zro", [128, 132], BF16)
    for cc in range(8):
        c.dma("gpsimd", wsmall[:, cc * 264:cc * 264 + 128], k_w[cc * 128:(cc + 1) * 128, :], group="wl")
        c.dma("gpsimd", wsmall[:, cc * 264 + 128:cc * 264 + 256], v_w[cc * 128:(cc + 1) * 128, :], group="wl")
        c.dma("gpsimd", wsmall[:, cc * 264 + 256:cc * 264 + 264], gate_w[cc * 128:(cc + 1) * 128, :], group="wl")
        for e in range(EA):
            c.dma("gpsimd", qwb[:, (e * 8 + cc) * 128:(e * 8 + cc + 1) * 128],
                  q_w[e, cc * 128:(cc + 1) * 128, :], group="wl")
    for e in range(EA):
        c.dma("gpsimd", ow_sb[:, e * D:(e + 1) * D], o_w[e], group="wl")
    c.dma("gpsimd", ob_sb[:], o_b[:], group="wl")
    c.dma("gpsimd", kvb[:, 0:1], k_b.rearrange("(p one) -> p one", one=1), group="wl2")
    vbrow = c.sbuf("vbrow", [1, 128], F32, side="right")
    vbb = c.sbuf("vbb", [128, 128], F32, side="right")
    c.dma("sync", vbrow[:], v_b.rearrange("(one n) -> one n", one=1), group="wl2f")
    c.dma("gpsimd", qbb[:], q_b[:], group="wl2")
    c.dma("gpsimd", relT[:, 0:129], rel_t[:], group="wl2")
    wl = [c.gtok("wl"), c.gtok("wl2"), c.gtok("wl2f")]
    zt = c.op("vector", lambda v: v.memset(zro[:], 0.0))
    vb1 = c.op("tensor", lambda t: t.matmul(PB[:, 1024:1152], onesr[:], vbrow[:], start=True, stop=True),
               deps=[t_o1] + wl)
    vb2 = c.op("vector", lambda v: v.tensor_copy(vbb[:], PB[:, 1024:1152]), deps=[vb1])

    def colchunks(width):
        out, off = [], 0
        while off < width:
            w = min(512, width - off)
            out.append((off, w))
            off += w
        return out

    for half in range(2):
        h0 = 0 if half == 0 else Tn
        hw = Tn if half == 0 else SL - Tn
        c.barrier()
        c.dma("sync", xn_sb[:, 0:8 * hw].rearrange("p (cc t) -> p cc t", t=hw),
              xn[:, h0:h0 + hw].rearrange("(cc p) t -> p cc t", p=128), group="lxn")
        prev = []
        for (off, w) in colchunks(hw):
            mm = None
            for cc in range(8):
                wst = wsmall[:, cc * 264:cc * 264 + 128]
                c.raw("tensor", lambda t, wst=wst: t.ldweights(wst),
                      deps=(wl + [c.gtok("lxn")] + prev) if cc == 0 else [])
                mm = c.op("tensor", lambda t, wst=wst, cc=cc, off=off, w=w, hw=hw: t.matmul(
                    PA[:, 0:w], wst, xn_sb[:, cc * hw + off: cc * hw + off + w],
                    start=(cc == 0), stop=(cc == 7)))
            prev = [c.op("vector", lambda v, h0=h0, off=off, w=w: v.tensor_scalar(
                k_cm[:, h0 + off: h0 + off + w], PA[:, 0:w], kvb[:, 0:1], None, ALU.add),
                deps=[mm] + wl + prev)]
        prev = []
        for tt in range(hw // 128):
            mm = None
            for cc in range(8):
                st = xn_sb[:, cc * hw + tt * 128: cc * hw + (tt + 1) * 128]
                c.raw("tensor", lambda t, st=st: t.ldweights(st), deps=prev if cc == 0 else [])
                mm = c.op("tensor", lambda t, st=st, cc=cc: t.matmul(
                    PA[:, 512:640], st, wsmall[:, cc * 264 + 128:cc * 264 + 256],
                    start=(cc == 0), stop=(cc == 7)))
            ti = h0 // 128 + tt
            prev = [c.op("vector", lambda v, ti=ti: v.tensor_tensor(
                v_tm[:, ti * 128:(ti + 1) * 128], PA[:, 512:640], vbb[:], ALU.add),
                deps=[mm, vb2] + prev)]
    c.barrier()
    c.dma("sync", xn_sb[:, 0:8 * Tn].rearrange("p (cc t) -> p cc t", t=Tn),
          xn[:, 128:128 + Tn].rearrange("(cc p) t -> p cc t", p=128), group="lxo")
    lxo = [c.gtok("lxo")]
    for e in range(EA):
        prev = lxo
        for tcol in range(Tn // 512):
            mm = None
            for cc in range(8):
                wst = qwb[:, (e * 8 + cc) * 128:(e * 8 + cc + 1) * 128]
                c.raw("tensor", lambda t, wst=wst: t.ldweights(wst), deps=prev if cc == 0 else [])
                mm = c.op("tensor", lambda t, wst=wst, cc=cc, tcol=tcol: t.matmul(
                    PA[:, 0:512], wst, xn_sb[:, cc * Tn + tcol * 512: cc * Tn + (tcol + 1) * 512],
                    start=(cc == 0), stop=(cc == 7)))
            ev = c.op("vector", lambda v, e=e, tcol=tcol: v.tensor_scalar(
                scE[:, 0:512], PA[:, 0:512],
                qbb[:, e:e + 1], SC, ALU.add, ALU.mult), deps=[mm] + prev)
            c.dma("gpsimd", q_hbm[e, :, tcol * 512:(tcol + 1) * 512], scE[:, 0:512], deps=[ev], group="qst")
            prev = [ev, c.gtok("qst")]
        c.barrier()
    for e in range(EA):
        c.dma("gpsimd", rel_hbm[e, 0:128 * 129].rearrange("(p n) -> p n", n=129), zro[:, 0:129],
              deps=[zt], group="rz")
        c.dma("gpsimd", rel_hbm[e, (Tn + 128) * 129:(Tn + 256) * 129].rearrange("(p n) -> p n", n=129),
              zro[:, 0:129], deps=[zt], group="rz")
    qtile = c.sbuf("qtile", [128, 512], BF16, side="right")
    for e in range(EA):
        for qt in range(NQT):
            c.barrier()
            c.dma("sync", qtile[:, 0:128], q_hbm[e, :, qt * 128:(qt + 1) * 128], group="qld")
            lhs = qtile[:, 0:128]
            mm = c.op("tensor", lambda t, lhs=lhs: (t.ldweights(lhs), t.matmul(
                PB[:, 0:129], lhs, relT[:, 0:129], start=True, stop=True))[1], deps=[c.gtok("qld")])
            rv = c.op("vector", lambda v: v.tensor_copy(sc1[:, 0:129], PB[:, 0:129]), deps=[mm])
            bb = c.op("vector", lambda v: v.tensor_copy(scE[:, 0:129], sc1[:, 0:129]), deps=[rv])
            c.dma("gpsimd", rel_hbm[e, (128 + qt * 128) * 129:(128 + (qt + 1) * 128) * 129]
                  .rearrange("(p n) -> p n", n=129), scE[:, 0:129], deps=[bb], group="rst")
            c.op("vector", lambda v, e=e, qt=qt: v.tensor_copy(
                clcr[:, (e * NQT + qt) * 2:(e * NQT + qt) * 2 + 1], sc1[:, 0:1]), deps=[rv])
            c.op("vector", lambda v, e=e, qt=qt: v.tensor_copy(
                clcr[:, (e * NQT + qt) * 2 + 1:(e * NQT + qt) * 2 + 2], sc1[:, 128:129]), deps=[rv])
    c.barrier()
    rl_row = c.sbuf("rl_row", [1, 2], F32, side="right")
    c.dma("sync", rl_row[:, 0:1], roleL[:], group="rlr")
    c.dma("sync", rl_row[:, 1:2], roleR[:], group="rlr")
    rlrt = [c.gtok("rlr")]
    rb1 = c.op("tensor", lambda t: t.matmul(PB[:, 1152:1153], onesr[:], rl_row[:, 0:1], start=True, stop=True),
               deps=[t_o1] + rlrt)
    rb2 = c.op("vector", lambda v: v.tensor_copy(rlc[:, 0:1], PB[:, 1152:1153]), deps=[rb1])
    rb3 = c.op("tensor", lambda t: t.matmul(PB[:, 1152:1153], onesr[:], rl_row[:, 1:2], start=True, stop=True),
               deps=[rb2])
    rb4 = c.op("vector", lambda v: v.tensor_copy(rlc[:, 1:2], PB[:, 1152:1153]), deps=[rb3])
    logit_s = c.sbuf("logit_s", [128, EA], F32)
    gwf = c.sbuf("gwf", [128, 8 * EA], F32)
    gwh = c.sbuf("gwh", [128, 8 * EA], BF16)
    gwl = c.sbuf("gwl", [128, 8 * EA], BF16)
    gwt = c.sbuf("gwt", [128, 8 * EA], F32)
    lo_t = c.sbuf("lo_t", [128, 8 * 128], BF16)
    for cc in range(8):
        c.dma("sync", gwf[:, cc * EA:(cc + 1) * EA], gate_w[cc * 128:(cc + 1) * 128, :], group="gwl")
    gwtk = [c.gtok("gwl")]
    sA = c.op("vector", lambda v: v.tensor_copy(gwh[:], gwf[:]), deps=gwtk)
    sB = c.op("vector", lambda v: v.tensor_copy(gwt[:], gwh[:]), deps=[sA])
    sC = c.op("vector", lambda v: v.tensor_tensor(gwt[:], gwf[:], gwt[:], ALU.subtract), deps=[sB])
    sD = c.op("vector", lambda v: v.tensor_copy(gwl[:], gwt[:]), deps=[sC])
    m1c = c.sbuf("m1c", [128, 1], F32)
    m2c = c.sbuf("m2c", [128, 1], F32)
    msk1 = c.sbuf("msk1", [128, EA], F32)
    msk2 = c.sbuf("msk2", [128, EA], F32)
    g1c = c.sbuf("g1c", [128, 1], F32)
    stt_sb = c.sbuf("stt_sb", [128, 48], F32)
    for qt in range(NQT):
        c.barrier()
        c.dma("sync", lo_t[:].rearrange("p (cc n) -> p cc n", n=128),
              xnlo[:, qt * 128:(qt + 1) * 128].rearrange("(cc p) n -> p cc n", p=128), group="lo")
        lot = [c.gtok("lo")]
        mm = None
        for cc in range(8):
            st = xn_sb[:, cc * Tn + qt * 128: cc * Tn + (qt + 1) * 128]
            stl = lo_t[:, cc * 128:(cc + 1) * 128]
            c.raw("tensor", lambda t, st=st: t.ldweights(st), deps=[sD] + lot if cc == 0 else [])
            c.op("tensor", lambda t, st=st, cc=cc: t.matmul(
                PA[:, 0:EA], st, gwh[:, cc * EA:(cc + 1) * EA],
                start=(cc == 0), stop=False))
            c.op("tensor", lambda t, st=st, cc=cc: t.matmul(
                PA[:, 0:EA], st, gwl[:, cc * EA:(cc + 1) * EA], start=False, stop=False))
            c.raw("tensor", lambda t, stl=stl: t.ldweights(stl))
            mm = c.op("tensor", lambda t, stl=stl, cc=cc: t.matmul(
                PA[:, 0:EA], stl, gwh[:, cc * EA:(cc + 1) * EA],
                start=False, stop=(cc == 7)))
        lg = c.op("vector", lambda v: v.tensor_copy(logit_s[:], PA[:, 0:EA]), deps=[mm])
        r1m = c.op("vector", lambda v: v.reduce_max(m1c[:], logit_s[:], axis=AX.X), deps=[lg])
        q1 = c.op("vector", lambda v: v.tensor_scalar(msk1[:], logit_s[:], m1c[:, 0:1], None, ALU.is_equal),
                  deps=[r1m])
        q2 = c.op("vector", lambda v: v.scalar_tensor_tensor(sc1[:, 0:EA], msk1[:], -1e30, logit_s[:],
                                                             ALU.mult, ALU.add), deps=[q1])
        r2m = c.op("vector", lambda v: v.reduce_max(m2c[:], sc1[:, 0:EA], axis=AX.X), deps=[q2])
        q3 = c.op("vector", lambda v: v.tensor_scalar(msk2[:], sc1[:, 0:EA], m2c[:, 0:1], None, ALU.is_equal),
                  deps=[r2m])
        d12 = c.op("vector", lambda v: v.tensor_tensor(g1c[:], m1c[:], m2c[:], ALU.subtract), deps=[r2m, q1])
        sg = c.op("scalar", lambda s: s.activation(g1c[:], g1c[:], AF.Sigmoid), deps=[d12])
        z0 = c.op("vector", lambda v: v.memset(stt_sb[:, 8:48], 0.0), deps=[lg])
        gA = c.op("vector", lambda v: v.tensor_scalar(stt_sb[:, 0:EA], msk1[:], g1c[:, 0:1], None, ALU.mult),
                  deps=[sg, q3, z0])
        gB = c.op("vector", lambda v: v.tensor_scalar(sc1[:, 0:EA], msk2[:], g1c[:, 0:1], None, ALU.mult),
                  deps=[sg, gA])
        gC = c.op("vector", lambda v: v.tensor_tensor(sc1[:, 0:EA], msk2[:], sc1[:, 0:EA], ALU.subtract),
                  deps=[gB])
        gD = c.op("vector", lambda v: v.tensor_tensor(stt_sb[:, 0:EA], stt_sb[:, 0:EA], sc1[:, 0:EA], ALU.add),
                  deps=[gC])
        eo = [gD]
        for e in range(EA):
            eo.append(c.op("scalar", lambda s, e=e, qt=qt: s.activation(
                stt_sb[:, 16 + e:17 + e], clcr[:, (e * NQT + qt) * 2:(e * NQT + qt) * 2 + 1],
                AF.Exp), deps=[sg, z0]))
            eo.append(c.op("scalar", lambda s, e=e, qt=qt: s.activation(
                stt_sb[:, 24 + e:25 + e], clcr[:, (e * NQT + qt) * 2 + 1:(e * NQT + qt) * 2 + 2],
                AF.Exp), deps=[sg, z0]))
        p1 = c.op("vector", lambda v: v.tensor_scalar(stt_sb[:, 32:40], stt_sb[:, 16:24], rlc[:, 0:1],
                                                      None, ALU.mult), deps=eo + [rb2])
        p2 = c.op("vector", lambda v: v.tensor_scalar(stt_sb[:, 40:48], stt_sb[:, 24:32], rlc[:, 1:2],
                                                      None, ALU.mult), deps=[p1, rb4])
        p3 = c.op("vector", lambda v: v.tensor_tensor(stt_sb[:, 8:16], stt_sb[:, 32:40], stt_sb[:, 40:48],
                                                      ALU.add), deps=[p2])
        cv = c.op("vector", lambda v: v.tensor_copy(scE[:, 0:48], stt_sb[:, 0:48]), deps=[p3])
        c.dma("gpsimd", st_hbm[qt * 128:(qt + 1) * 128, 0:48], scE[:, 0:48], deps=[cv], group="sst")
    c.barrier()
    c.dma("sync", strows[:], st_hbm[:, :], transpose=True, group="str")
    strt = [c.gtok("str")]
    for e in range(EA):
        c.dma("sync", grows[e:e + 1, :], strows[e:e + 1, :], deps=strt, group="rd")
    rd = [c.gtok("rd")]
    c.barrier()
    c.sbuf_free("stt_sb", "g1c", "msk2", "msk1", "m2c", "m1c", "logit_s",
                "zro", "relT", "qbb", "qwb", "kvb", "wsmall", "xn_sb",
                "gwf", "gwh", "gwl", "gwt", "lo_t")

    Bsb = c.sbuf("Bsb", [128, 4 * 768], F32)
    stage0 = c.sbuf("stage0", [1, 6 * 512], BF16)
    q_sl = c.sbuf("q_sl", [128, 512], BF16)
    bm_sb = c.sbuf("bm_sb", [128, 12 * 768], BF16)
    km_sb = c.sbuf("km_sb", [128, 8 * 768], BF16)
    fct = c.sbuf("fct", [128, 5 * 512], F32)
    gzb = c.sbuf("gzb", [128, 512], F32)
    zr = c.sbuf("zr", [1, 512], F32)
    ctxs = c.sbuf("ctxs", [128, EA * 512], BF16)
    xcc = c.sbuf("xcc", [128, 8 * 512], F32)
    mb = c.sbuf("mb", [128, 512], F32)
    rb = c.sbuf("rb", [128, 512], F32)
    x2t = c.sbuf("x2t", [128, 512], F32)
    scF = c.sbuf("scF", [128, 512], F32)
    c.dma("sync", bm_sb[:].rearrange("p (k s n) -> p k s n", s=4, n=768),
          bandm.rearrange("k s p n -> p k s n"), group="bm")
    c.dma("sync", km_sb[:].rearrange("p (k s n) -> p k s n", s=4, n=768),
          maskK.rearrange("k s p n -> p k s n"), group="bm")
    bmt = [c.gtok("bm")]
    ident_f = c.sbuf("ident_f", [128, 128], F32)
    ii1 = c.op("gpsimd", lambda g: g.iota(sc1[:, 0:128], [[1, 128]], channel_multiplier=0,
                                          allow_small_or_imprecise_dtypes=True))
    ii2 = c.op("gpsimd", lambda g: g.iota(sc2[:, 0:128], [[0, 128]], channel_multiplier=1,
                                          allow_small_or_imprecise_dtypes=True))
    c.op("vector", lambda v: v.tensor_tensor(ident_f[:], sc1[:, 0:128], sc2[:, 0:128], ALU.is_equal),
         deps=[ii1, ii2])
    mu_row = c.sbuf("mu_rowM", [1, Tn], F32)
    r1_row = c.sbuf("r1_rowM", [1, Tn], F32)
    c.dma("sync", mu_row[:], mur[:], group="smu")
    c.dma("sync", r1_row[:], r1r[:], group="smu")
    smu = [c.gtok("smu")]
    ps_sc = PA[:, 0:512]
    ps_ctx = PA[:, 512:1024]
    ps_Z = PB[0:1, 0:512]
    ps_bc = PB[:, 512:1024]
    c.barrier()

    for qq in range(NQQ):
        q0 = 128 + qq * 512
        zone0 = q0 - 128
        kz0 = zone0 // 128
        kz1 = kz0 + 6
        for e in range(EA):
            c.barrier()
            for j, slot in enumerate([e, 8 + e, 16 + e, 24 + e, 32 + e, 40 + e]):
                c.dma("sync", stage0[:, j * 512:(j + 1) * 512],
                      strows[slot:slot + 1, qq * 512:(qq + 1) * 512], group="stg")
            c.dma("sync", q_sl[:], q_hbm[e, :, qq * 512:(qq + 1) * 512], group="stg")
            stg = [c.gtok("stg")]
            fprev = []
            for j in range(5):
                bmm = c.op("tensor", lambda t, j=j: t.matmul(
                    ps_bc, onesrb[:], stage0[:, (j + 1) * 512:(j + 2) * 512],
                    start=True, stop=True), deps=[t_o3] + stg + fprev)
                fprev = [c.op("vector", lambda v, j=j: v.tensor_copy(fct[:, j * 512:(j + 1) * 512], ps_bc),
                              deps=[bmm])]
            btoks = []
            for sub in range(4):
                qa = q0 + sub * 128
                off = (qa - 128 + 128) * 129 + (zone0 - qa) + 64
                bl = c.dma("gpsimd", sc1[:, 0:768], mkap(rel_hbm[e], off, [[128, 128], [1, 768]]),
                           deps=btoks[-1:], group="bld")
                b1 = c.op("vector", lambda v, sub=sub: v.tensor_tensor(
                    sc2[:, 0:768], sc1[:, 0:768], bm_sb[:, (0 * 4 + sub) * 768:(0 * 4 + sub + 1) * 768],
                    ALU.mult), deps=[c.gtok("bld")] + bmt)
                qt = qq * 4 + sub
                b2 = c.op("vector", lambda v, sub=sub, e=e, qt=qt: v.scalar_tensor_tensor(
                    sc2[:, 0:768], bm_sb[:, (1 * 4 + sub) * 768:(1 * 4 + sub + 1) * 768],
                    clcr[:, (e * NQT + qt) * 2:(e * NQT + qt) * 2 + 1],
                    sc2[:, 0:768], ALU.mult, ALU.add), deps=[b1])
                b3 = c.op("vector", lambda v, sub=sub, e=e, qt=qt: v.scalar_tensor_tensor(
                    Bsb[:, sub * 768:(sub + 1) * 768], bm_sb[:, (2 * 4 + sub) * 768:(2 * 4 + sub + 1) * 768],
                    clcr[:, (e * NQT + qt) * 2 + 1:(e * NQT + qt) * 2 + 2],
                    sc2[:, 0:768], ALU.mult, ALU.add), deps=[b2])
                if qq == 0:
                    b3 = c.op("vector", lambda v, sub=sub: v.scalar_tensor_tensor(
                        Bsb[:, sub * 768:(sub + 1) * 768], km_sb[:, (0 * 4 + sub) * 768:(0 * 4 + sub + 1) * 768],
                        -1e30, Bsb[:, sub * 768:(sub + 1) * 768], ALU.mult, ALU.add), deps=[b3] + bmt)
                if qq == NQQ - 1:
                    b3 = c.op("vector", lambda v, sub=sub: v.scalar_tensor_tensor(
                        Bsb[:, sub * 768:(sub + 1) * 768], km_sb[:, (1 * 4 + sub) * 768:(1 * 4 + sub + 1) * 768],
                        -1e30, Bsb[:, sub * 768:(sub + 1) * 768], ALU.mult, ALU.add), deps=[b3] + bmt)
                btoks.append(b3)
            prevE = []
            for kb in range(NKB):
                lhs = k_cm[:, kb * 128:(kb + 1) * 128]
                inzone = kz0 <= kb < kz1
                mm = c.op("tensor", lambda t, lhs=lhs, inzone=inzone: (
                    t.ldweights(lhs), t.matmul(
                        ps_sc, lhs, q_sl[:],
                        start=True, stop=(not inzone)))[1], deps=prevE + btoks + fprev + stg)
                if inzone:
                    for sub in range(4):
                        mm = c.op("tensor", lambda t, kb=kb, sub=sub, kz0=kz0: t.matmul(
                            ps_sc[:, sub * 128:(sub + 1) * 128],
                            Bsb[:, sub * 768 + (kb - kz0) * 128: sub * 768 + (kb - kz0) * 128 + 128],
                            ident_f[:], start=False, stop=(sub == 3), is_transpose=True))
                    Etok = c.op("scalar", lambda s: s.activation(scE[:], ps_sc, AF.Exp, bias=negMc[:, 0:1]),
                                deps=[mm, negMt])
                else:
                    ex = c.op("scalar", lambda s: s.activation(scF[:], ps_sc, AF.Exp, bias=negMc[:, 0:1]),
                              deps=[mm, negMt])
                    if kb == 0:
                        fsel = 3
                    elif kb == KPOST:
                        fsel = 4
                    elif kb < kz0:
                        fsel = 1
                    elif kb <= KPOST:
                        fsel = 2
                    else:
                        fsel = 0
                    Etok = c.op("vector", lambda v, fsel=fsel: v.tensor_tensor(
                        scE[:], scF[:], fct[:, fsel * 512:(fsel + 1) * 512], ALU.mult), deps=[ex])
                cm = c.op("tensor", lambda t, kb=kb: (t.ldweights(v_tm[:, kb * 128:(kb + 1) * 128]), t.matmul(
                    ps_ctx, v_tm[:, kb * 128:(kb + 1) * 128], scE[:], start=(kb == 0), stop=(kb == NKB - 1)))[1],
                    deps=[Etok])
                zm = c.op("tensor", lambda t, kb=kb: (t.ldweights(onescb[:]), t.matmul(
                    ps_Z, onescb[:], scE[:], start=(kb == 0), stop=(kb == NKB - 1)))[1], deps=[t_o2])
                prevE = [cm, zm]
            rz = c.op("vector", lambda v: v.reciprocal(zr[:], ps_Z), deps=prevE)
            gz = c.op("vector", lambda v: v.tensor_tensor(
                zr[:], zr[:], stage0[:, 0:512], ALU.mult), deps=[rz])
            bz = c.op("tensor", lambda t: t.matmul(ps_bc, onesr[:], zr[:], start=True, stop=True), deps=[gz])
            bzc = c.op("vector", lambda v: v.tensor_copy(gzb[:], ps_bc), deps=[bz])
            c.op("vector", lambda v, e=e: v.tensor_tensor(ctxs[:, e * 512:(e + 1) * 512], ps_ctx, gzb[:],
                                                          ALU.mult), deps=[bzc])
        c.barrier()
        b1 = c.op("tensor", lambda t, qq=qq: t.matmul(ps_bc, onesr[:],
                  mu_row[:, qq * 512:(qq + 1) * 512], start=True, stop=True), deps=smu)
        b1c = c.op("vector", lambda v: v.tensor_copy(mb[:], ps_bc), deps=[b1])
        b2 = c.op("tensor", lambda t, qq=qq: t.matmul(ps_bc, onesr[:],
                  r1_row[:, qq * 512:(qq + 1) * 512], start=True, stop=True), deps=[b1c])
        b2c = c.op("vector", lambda v: v.tensor_copy(rb[:], ps_bc), deps=[b2])
        c.dma("sync", xcc[:].rearrange("p (cc n) -> p cc n", n=512),
              x[:, qq * 512:(qq + 1) * 512].rearrange("(cc p) n -> p cc n", p=128), group="lxq")
        lxt = [c.gtok("lxq")]
        evtoks = []
        for ccg in range(2):
            mms = []
            for cc4 in range(4):
                cc = ccg * 4 + cc4
                mm = None
                for e in range(EA):
                    lhs = ow_sb[:, e * D + cc * 128: e * D + cc * 128 + 128]
                    c.raw("tensor", lambda t, lhs=lhs: t.ldweights(lhs),
                          deps=evtoks if (cc4 == 0 and e == 0) else [])
                    mm = c.op("tensor", lambda t, lhs=lhs, e=e, cc4=cc4: t.matmul(
                        PA[:, cc4 * 512:(cc4 + 1) * 512], lhs, ctxs[:, e * 512:(e + 1) * 512],
                        start=(e == 0), stop=False))
                obl = ob_sb[:, cc * 128:(cc + 1) * 128]
                c.raw("tensor", lambda t, obl=obl: t.ldweights(obl))
                mm = c.op("tensor", lambda t, obl=obl, cc4=cc4, qq=qq: t.matmul(
                    PA[:, cc4 * 512:(cc4 + 1) * 512], obl, grows[:, qq * 512:(qq + 1) * 512],
                    start=False, stop=True))
                mms.append(mm)
            for cc4 in range(4):
                cc = ccg * 4 + cc4
                t1 = c.op("vector", lambda v, cc=cc: v.tensor_tensor(
                    x2t[:], xcc[:, cc * 512:(cc + 1) * 512], mb[:], ALU.subtract),
                    deps=[mms[cc4], b1c, b2c] + lxt + [c.gtok("x2s")])
                t2 = c.op("vector", lambda v: v.tensor_tensor(x2t[:], x2t[:], rb[:], ALU.mult), deps=[t1])
                t3 = c.op("vector", lambda v, cc4=cc4: v.tensor_tensor(
                    x2t[:], x2t[:], PA[:, cc4 * 512:(cc4 + 1) * 512], ALU.add), deps=[t2])
                evtoks = [t3]
                c.dma("gpsimd", x2o[cc * 128:(cc + 1) * 128, qq * 512:(qq + 1) * 512], x2t[:],
                      deps=[t3], group="x2s")
        c.barrier()
    if debug:
        dq = c.outp("dbg_q", [128, EA * Tn], BF16)
        dk = c.outp("dbg_k", [128, SL], BF16)
        dv = c.outp("dbg_v", [128, NKB * 128], BF16)
        dc = c.outp("dbg_ctx", [128, EA * 512], BF16)
        dg = c.outp("dbg_rows", [1, 48 * Tn], BF16)
        c.dma("gpsimd", dq, q_all[:], group="x2s")
        c.dma("gpsimd", dk, k_cm[:], group="x2s")
        c.dma("gpsimd", dv, v_tm[:], group="x2s")
        c.dma("gpsimd", dc, ctxs[:], group="x2s")
        c.dma("gpsimd", dg, rows0[:], group="x2s")
    c.wait_all("gpsimd", [c.gtok("x2s")])
    return c.finish()


def host_moa_layout(h, Tn, Sn):
    import ml_dtypes
    HSn = Sn // 2
    own0 = h * HSn
    par0 = (1 - h) * HSn
    if h == 0:
        pre = None
        post = (par0, par0 + 128)
        rest = (par0 + 128, par0 + HSn)
    else:
        pre = (par0 + HSn - 128, par0 + HSn)
        post = None
        rest = (par0, par0 + HSn - 128)
    j = np.arange(768)[None, :]
    qr = np.arange(128)[:, None]
    bandm = np.zeros((3, 4, 128, 768), np.float32)
    for sub in range(4):
        d = j - 128 - 128 * sub - qr
        bandm[0, sub] = (np.abs(d) <= 64)
        bandm[1, sub] = (d < -64)
        bandm[2, sub] = (d > 64)
    maskK = np.zeros((2, 4, 128, 768), np.float32)
    if h == 0:
        maskK[0, :, :, 0:128] = 1.0
    if h == 1:
        maskK[1, :, :, 640:768] = 1.0
    bf = ml_dtypes.bfloat16
    return ((pre, (own0, own0 + Tn), post, rest),
            bandm.astype(bf), maskK.astype(bf),
            np.array([[1.0 if h == 1 else 0.0]], np.float32),
            np.array([[1.0 if h == 0 else 0.0]], np.float32))



def build_moe(Tn=T):
    EF, FH = 16, 512
    NQT = Tn // 128
    NQQ = Tn // 512
    c = Ctx("moe")
    x2 = c.inp("x2", [D, Tn])
    x = c.inp("x", [D, Tn])
    ln_w = c.inp("ln_w", [D])
    ln_b = c.inp("ln_b", [D])
    gate_w = c.inp("gate_w", [D, EF])
    w1 = c.inp("w1", [EF, D, FH])
    b1 = c.inp("b1", [EF, FH])
    w2 = c.inp("w2", [EF, FH, D])
    b2 = c.inp("b2", [EF, D])
    xo = c.outp("xo", [D, Tn])

    w1bf = c.dram("w1bf", [EF, D, FH], BF16)
    w2bf = c.dram("w2bf", [EF, FH, D], BF16)
    st_hbm = c.dram("st2_hbm", [Tn, 128], BF16)

    onesr = c.sbuf("onesr", [1, 128], F32, side="right")
    onesc = c.sbuf("onesc", [128, 1], F32, side="right")
    xn2h = c.sbuf("xn2h", [128, 8 * Tn], BF16, side="right")
    strows = c.sbuf("strows", [128, Tn], BF16, side="right")
    g16 = c.sbuf("g16", [16, Tn], BF16, side="right")
    b2sb = c.sbuf("b2sb", [16, D], BF16, side="right")
    b1c = c.sbuf("b1c", [128, EF * 4], F32, side="right")
    wbc = c.sbuf("wbc", [128, 16], F32, side="right")
    sc1 = c.sbuf("sc1", [128, 512], F32, side="right")
    sc2 = c.sbuf("sc2", [128, 512], F32, side="right")
    scE = c.sbuf("scE", [128, 512], BF16, side="right")
    mb = c.sbuf("mb", [128, 512], F32, side="right")
    rb = c.sbuf("rb", [128, 512], F32, side="right")
    stage0 = c.sbuf("stage0", [1, 512], BF16, side="right")
    PA = c.psum("PA", [128, 2048])
    PB = c.psum("PB", [128, 2048])
    t_o1 = c.op("gpsimd", lambda g: g.memset(onesr[:], 1.0))
    t_o2 = c.op("gpsimd", lambda g: g.memset(onesc[:], 1.0))
    epsc, epst = c.constcol(EPS)

    for e in range(EF):
        c.dma("gpsimd", w1bf[e], w1[e], group="wcast")
        c.dma("gpsimd", w2bf[e], w2[e], group="wcast")
    c.dma("gpsimd", b2sb[:], b2[:], group="wl")
    for e in range(EF):
        for hc in range(4):
            c.dma("sync", b1c[:, e * 4 + hc:e * 4 + hc + 1],
                  b1[e, hc * 128:(hc + 1) * 128].rearrange("(p one) -> p one", one=1), group="wl2")
    for cc in range(8):
        c.dma("sync", wbc[:, cc:cc + 1], ln_w[cc * 128:(cc + 1) * 128].rearrange("(p one) -> p one", one=1),
              group="wl2")
        c.dma("sync", wbc[:, 8 + cc:9 + cc], ln_b[cc * 128:(cc + 1) * 128].rearrange("(p one) -> p one", one=1),
              group="wl2")
    wl = [c.gtok("wl"), c.gtok("wl2")]

    xn2l_hbm = c.dram("xn2l_hbm", [D, Tn], BF16)
    lo_t2 = c.sbuf("lo_t2", [128, 512], BF16)
    x2cc = c.sbuf("x2cc", [128, 8 * 512], F32)
    murow = c.sbuf("murow", [1, Tn], F32)
    rrow = c.sbuf("rrow", [1, Tn], F32)
    sqrow = c.sbuf("sqrow", [1, Tn], F32)
    tf = c.sbuf("tf", [128, 512], F32)
    for qq in range(NQQ):
        c.barrier()
        c.dma("sync", x2cc[:].rearrange("p (cc n) -> p cc n", n=512),
              x2[:, qq * 512:(qq + 1) * 512].rearrange("(cc p) n -> p cc n", p=128), group="lx2")
        lxt = [c.gtok("lx2")]
        s1 = None
        s2 = None
        sqs = []
        for cc in range(8):
            q1 = c.op("scalar", lambda s, cc=cc: s.activation(sc1[:], x2cc[:, cc * 512:(cc + 1) * 512],
                                                              AF.Square),
                      deps=lxt + sqs[-1:] + ([s2] if s2 else []))
            sq = c.op("vector", lambda v: v.tensor_copy(scE[:], sc1[:]), deps=[q1])
            s1 = c.op("tensor", lambda t, cc=cc: t.matmul(PB[0:1, 0:512], onesc[:],
                                                          x2cc[:, cc * 512:(cc + 1) * 512],
                                                          start=(cc == 0), stop=(cc == 7)),
                      deps=[t_o2] + lxt + ([s1] if s1 else []))
            s2 = c.op("tensor", lambda t, cc=cc: t.matmul(PB[0:1, 512:1024], onesc[:], sc1[:],
                                                          start=(cc == 0), stop=(cc == 7)),
                      deps=[q1] + ([s2] if s2 else []))
            sqs = [sq]
        e1 = c.op("vector", lambda v, qq=qq: v.tensor_scalar(murow[:, qq * 512:(qq + 1) * 512],
                                                             PB[0:1, 0:512], 1.0 / D, None, ALU.mult), deps=[s1])
        e2 = c.op("vector", lambda v, qq=qq: v.tensor_scalar(sqrow[:, qq * 512:(qq + 1) * 512],
                                                             PB[0:1, 512:1024], 1.0 / D, None, ALU.mult), deps=[s2])
        e3 = c.op("vector", lambda v, qq=qq: v.tensor_tensor(sc1[0:1, 0:512],
                                                             murow[:, qq * 512:(qq + 1) * 512],
                                                             murow[:, qq * 512:(qq + 1) * 512], ALU.mult),
                  deps=[e1, e2])
        e4 = c.op("vector", lambda v, qq=qq: v.tensor_tensor(sqrow[:, qq * 512:(qq + 1) * 512],
                                                             sqrow[:, qq * 512:(qq + 1) * 512],
                                                             sc1[0:1, 0:512], ALU.subtract), deps=[e3])
        e5 = c.op("scalar", lambda s, qq=qq: s.activation(sqrow[:, qq * 512:(qq + 1) * 512],
                                                          sqrow[:, qq * 512:(qq + 1) * 512], AF.Sqrt,
                                                          bias=epsc[0:1, 0:1]), deps=[e4, epst])
        e6 = c.op("vector", lambda v, qq=qq: v.reciprocal(rrow[:, qq * 512:(qq + 1) * 512],
                                                          sqrow[:, qq * 512:(qq + 1) * 512]), deps=[e5])
        bm1 = c.op("tensor", lambda t, qq=qq: t.matmul(PB[:, 1024:1536], onesr[:],
                                                       murow[:, qq * 512:(qq + 1) * 512], start=True, stop=True),
                   deps=[e6, t_o1])
        bc1 = c.op("vector", lambda v: v.tensor_copy(mb[:], PB[:, 1024:1536]), deps=[bm1])
        bm2 = c.op("tensor", lambda t, qq=qq: t.matmul(PB[:, 1024:1536], onesr[:],
                                                       rrow[:, qq * 512:(qq + 1) * 512], start=True, stop=True),
                   deps=[bc1])
        bc2 = c.op("vector", lambda v: v.tensor_copy(rb[:], PB[:, 1024:1536]), deps=[bm2])
        prevf = []
        for cc in range(8):
            f1 = c.op("vector", lambda v, cc=cc: v.tensor_tensor(tf[:], x2cc[:, cc * 512:(cc + 1) * 512],
                                                                 mb[:], ALU.subtract), deps=[bc1, bc2] + prevf)
            f2 = c.op("vector", lambda v: v.tensor_tensor(tf[:], tf[:], rb[:], ALU.mult), deps=[f1])
            f3 = c.op("vector", lambda v, cc=cc: v.tensor_scalar(tf[:], tf[:], wbc[:, cc:cc + 1],
                                                          wbc[:, 8 + cc:9 + cc],
                                                          ALU.mult, ALU.add), deps=[f2] + wl)
            f4 = c.op("vector", lambda v, cc=cc, qq=qq: v.tensor_copy(
                xn2h[:, cc * Tn + qq * 512: cc * Tn + qq * 512 + 512], tf[:]), deps=[f3])
            f5 = c.op("vector", lambda v, cc=cc, qq=qq: v.tensor_tensor(
                lo_t2[:], tf[:],
                xn2h[:, cc * Tn + qq * 512: cc * Tn + qq * 512 + 512], ALU.subtract),
                deps=[f4] + [c.gtok("lo2s")])
            c.dma("gpsimd", xn2l_hbm[cc * 128:(cc + 1) * 128, qq * 512:(qq + 1) * 512], lo_t2[:],
                  deps=[f5], group="lo2s")
            prevf = [f5]
    c.barrier()

    gwf = c.sbuf("gwf", [128, 8 * EF], F32)
    gwh = c.sbuf("gwh", [128, 8 * EF], BF16)
    gwl = c.sbuf("gwl", [128, 8 * EF], BF16)
    gwt = c.sbuf("gwt", [128, 8 * EF], F32)
    logit_s = c.sbuf("logit_s", [128, EF], F32)
    m1c = c.sbuf("m1c", [128, 1], F32)
    m2c = c.sbuf("m2c", [128, 1], F32)
    msk1 = c.sbuf("msk1", [128, EF], F32)
    msk2 = c.sbuf("msk2", [128, EF], F32)
    g1c = c.sbuf("g1c", [128, 1], F32)
    stt_sb = c.sbuf("stt_sb", [128, EF], F32)
    for cc in range(8):
        c.dma("sync", gwf[:, cc * EF:(cc + 1) * EF], gate_w[cc * 128:(cc + 1) * 128, :], group="gwl")
    gwtk = [c.gtok("gwl")]
    sA = c.op("vector", lambda v: v.tensor_copy(gwh[:], gwf[:]), deps=gwtk)
    sB = c.op("vector", lambda v: v.tensor_copy(gwt[:], gwh[:]), deps=[sA])
    sC = c.op("vector", lambda v: v.tensor_tensor(gwt[:], gwf[:], gwt[:], ALU.subtract), deps=[sB])
    sD = c.op("vector", lambda v: v.tensor_copy(gwl[:], gwt[:]), deps=[sC])
    lo_g = c.sbuf("lo_g", [128, 8 * 128], BF16)
    for qt in range(NQT):
        c.barrier()
        c.dma("sync", lo_g[:].rearrange("p (cc n) -> p cc n", n=128),
              xn2l_hbm[:, qt * 128:(qt + 1) * 128].rearrange("(cc p) n -> p cc n", p=128), group="lo2l")
        lo2t = [c.gtok("lo2l")]
        mm = None
        for cc in range(8):
            sh = xn2h[:, cc * Tn + qt * 128: cc * Tn + (qt + 1) * 128]
            sl = lo_g[:, cc * 128:(cc + 1) * 128]
            c.raw("tensor", lambda t, sh=sh: t.ldweights(sh), deps=([sD] + lo2t) if cc == 0 else [])
            c.op("tensor", lambda t, sh=sh, cc=cc: t.matmul(
                PA[:, 0:EF], sh, gwh[:, cc * EF:(cc + 1) * EF], start=(cc == 0), stop=False))
            c.op("tensor", lambda t, sh=sh, cc=cc: t.matmul(
                PA[:, 0:EF], sh, gwl[:, cc * EF:(cc + 1) * EF], start=False, stop=False))
            c.raw("tensor", lambda t, sl=sl: t.ldweights(sl))
            mm = c.op("tensor", lambda t, sl=sl, cc=cc: t.matmul(
                PA[:, 0:EF], sl, gwh[:, cc * EF:(cc + 1) * EF], start=False, stop=(cc == 7)))
        lg = c.op("vector", lambda v: v.tensor_copy(logit_s[:], PA[:, 0:EF]), deps=[mm])
        r1m = c.op("vector", lambda v: v.reduce_max(m1c[:], logit_s[:], axis=AX.X), deps=[lg])
        q1 = c.op("vector", lambda v: v.tensor_scalar(msk1[:], logit_s[:], m1c[:, 0:1], None, ALU.is_equal),
                  deps=[r1m])
        q2 = c.op("vector", lambda v: v.scalar_tensor_tensor(sc1[:, 0:EF], msk1[:], -1e30, logit_s[:],
                                                             ALU.mult, ALU.add), deps=[q1])
        r2m = c.op("vector", lambda v: v.reduce_max(m2c[:], sc1[:, 0:EF], axis=AX.X), deps=[q2])
        q3 = c.op("vector", lambda v: v.tensor_scalar(msk2[:], sc1[:, 0:EF], m2c[:, 0:1], None, ALU.is_equal),
                  deps=[r2m])
        d12 = c.op("vector", lambda v: v.tensor_tensor(g1c[:], m1c[:], m2c[:], ALU.subtract), deps=[r2m, q1])
        sg = c.op("scalar", lambda s: s.activation(g1c[:], g1c[:], AF.Sigmoid), deps=[d12])
        gA = c.op("vector", lambda v: v.tensor_scalar(stt_sb[:], msk1[:], g1c[:, 0:1], None, ALU.mult),
                  deps=[sg, q3])
        gB = c.op("vector", lambda v: v.tensor_scalar(sc1[:, 0:EF], msk2[:], g1c[:, 0:1], None, ALU.mult),
                  deps=[sg, gA])
        gC = c.op("vector", lambda v: v.tensor_tensor(sc1[:, 0:EF], msk2[:], sc1[:, 0:EF], ALU.subtract),
                  deps=[gB])
        gD = c.op("vector", lambda v: v.tensor_tensor(stt_sb[:], stt_sb[:], sc1[:, 0:EF], ALU.add),
                  deps=[gC])
        cv = c.op("vector", lambda v: v.tensor_copy(scE[:, 0:EF], stt_sb[:]), deps=[gD])
        c.dma("gpsimd", st_hbm[qt * 128:(qt + 1) * 128, 0:EF], scE[:, 0:EF], deps=[cv], group="sst")
    c.barrier()
    c.dma("sync", strows[:], st_hbm[:, :], transpose=True, group="str")
    strt = [c.gtok("str")]
    c.dma("sync", g16[:], strows[0:16, :], deps=strt, group="g16l")
    g16t = [c.gtok("g16l")]
    c.sbuf_free("stt_sb", "g1c", "msk2", "msk1", "m2c", "m1c", "logit_s", "gwt", "gwl", "gwh", "gwf",
                "lo_g", "lo_t2", "x2cc", "murow", "rrow", "sqrow", "tf")

    x2cc2 = c.sbuf("x2cc2", [128, 8 * 512], F32)
    w1sb = [c.sbuf(f"w1sb{i}", [128, 8 * FH], BF16) for i in range(2)]
    w2sb = [c.sbuf(f"w2sb{i}", [128, 4 * D], BF16) for i in range(2)]
    h_hbm = c.dram("h_hbm", [EF, 4 * 128, 512], BF16)
    hsb = c.sbuf("hsb", [128, 4 * 512], BF16)
    hld = [c.sbuf(f"hld{i}", [128, 4 * 512], BF16) for i in range(2)]
    onesrb = c.sbuf("onesrb", [1, 128], BF16)
    g2b = c.sbuf("g2b", [128, 512], F32)
    y0 = c.sbuf("y0", [128, 8 * 512], F32)
    xcc = c.sbuf("xcc", [128, 8 * 512], F32)
    sqq = c.sbuf("sqq", [128, 8 * 512], BF16)
    onescb2 = c.sbuf("onescb2", [128, 1], BF16)
    yrow = c.sbuf("yrow", [1, 512], F32)
    y2row = c.sbuf("y2row", [1, 512], F32)
    myb = c.sbuf("myb", [128, 512], F32)
    ryb = c.sbuf("ryb", [128, 512], F32)
    xot = c.sbuf("xot", [128, 512], F32)
    t_o4 = c.op("gpsimd", lambda g: g.memset(onesrb[:], 1.0))
    t_o5 = c.op("gpsimd", lambda g: g.memset(onescb2[:], 1.0))
    wcast = [c.gtok("wcast")]
    w1war = [[], []]
    w2war = [[], []]
    gbwar = []
    hwar = []
    for qq in range(NQQ):
        c.barrier()
        for e in range(EF):
            c.dma("sync", w1sb[e % 2][:].rearrange("p (cc n) -> p cc n", n=FH),
                  w1bf[e].rearrange("(cc p) n -> p cc n", p=128),
                  deps=wcast + w1war[e % 2], group=f"w1l{e % 2}")
            lwt = [c.gtok(f"w1l{e % 2}")]
            c.dma("sync", stage0[:], strows[e:e + 1, qq * 512:(qq + 1) * 512],
                  deps=strt + [("Ptensor", c._cnt.get("Ptensor", 0))], group="stg")
            sgt = [c.gtok("stg")]
            gb1 = c.op("tensor", lambda t: t.matmul(PB[:, 1024:1536], onesrb[:], stage0[:],
                                                    start=True, stop=True), deps=sgt + [t_o4] + gbwar)
            gb2 = c.op("vector", lambda v: v.tensor_copy(g2b[:], PB[:, 1024:1536]), deps=[gb1] + hwar)
            gbwar = [gb2]
            mm = None
            for hc in range(4):
                for cc in range(8):
                    wst = w1sb[e % 2][:, cc * FH + hc * 128: cc * FH + (hc + 1) * 128]
                    c.raw("tensor", lambda t, wst=wst: t.ldweights(wst),
                          deps=(lwt + hwar) if (hc == 0 and cc == 0) else [])
                    mm = c.op("tensor", lambda t, wst=wst, cc=cc, hc=hc, qq=qq: t.matmul(
                        PA[:, hc * 512:(hc + 1) * 512], wst,
                        xn2h[:, cc * Tn + qq * 512: cc * Tn + qq * 512 + 512],
                        start=(cc == 0), stop=(cc == 7)))
            w1war[e % 2] = [mm]
            prevh = []
            for hc in range(4):
                r1 = c.op("scalar", lambda s, hc=hc, e=e: s.activation(
                    sc1[:], PA[:, hc * 512:(hc + 1) * 512], AF.Relu,
                    bias=b1c[:, e * 4 + hc:e * 4 + hc + 1]), deps=[mm] + wl + prevh)
                h1 = c.op("vector", lambda v, e=e, hc=hc: v.tensor_tensor(
                    hsb[:, hc * 512:(hc + 1) * 512], sc1[:], g2b[:], ALU.mult),
                    deps=[r1, gb2] + ([c.gtok("hst")] if hc == 0 else []))
                prevh = [r1, h1]
                w1war[e % 2] = w1war[e % 2] + [h1]
            hwar = prevh
            c.dma("gpsimd", h_hbm[e].rearrange("(hc p) n -> p hc n", p=128),
                  hsb[:].rearrange("p (hc n) -> p hc n", n=512), deps=prevh, group="hst")
        c.barrier()
        lx2 = c.dma("sync", x2cc2[:].rearrange("p (cc n) -> p cc n", n=512),
                    x2[:, qq * 512:(qq + 1) * 512].rearrange("(cc p) n -> p cc n", p=128), group="lx2b")
        lx = c.dma("sync", xcc[:].rearrange("p (cc n) -> p cc n", n=512),
                   x[:, qq * 512:(qq + 1) * 512].rearrange("(cc p) n -> p cc n", p=128), group="lx2b")
        lxt = [c.gtok("lx2b")]
        PBANK = [PA[:, 0:512], PA[:, 512:1024], PA[:, 1024:1536], PA[:, 1536:2048],
                 PB[:, 0:512], PB[:, 512:1024], PB[:, 1024:1536], PB[:, 1536:2048]]
        mm = None
        hwl = [c.gtok("hst")]
        for e in range(EF):
            c.dma("sync", w2sb[e % 2][:].rearrange("p (hc n) -> p hc n", n=D),
                  w2bf[e].rearrange("(hc p) n -> p hc n", p=128),
                  deps=wcast + w2war[e % 2], group=f"w2l{e % 2}")
            lw2t = [c.gtok(f"w2l{e % 2}")]
            c.dma("sync", hld[e % 2][:].rearrange("p (hc n) -> p hc n", n=512),
                  h_hbm[e].rearrange("(hc p) n -> p hc n", p=128),
                  deps=hwl + w2war[e % 2], group=f"hl{e % 2}")
            hlt = [c.gtok(f"hl{e % 2}")]
            for cc in range(8):
                for hc in range(4):
                    wst = w2sb[e % 2][:, hc * D + cc * 128: hc * D + cc * 128 + 128]
                    c.raw("tensor", lambda t, wst=wst: t.ldweights(wst),
                          deps=(lw2t + hlt) if (cc == 0 and hc == 0) else [])
                    mm = c.op("tensor", lambda t, wst=wst, e=e, hc=hc, cc=cc: t.matmul(
                        PBANK[cc], wst, hld[e % 2][:, hc * 512:(hc + 1) * 512],
                        start=(e == 0 and hc == 0), stop=False))
            w2war[e % 2] = [mm]
        mms = []
        for cc in range(8):
            bb = b2sb[:, cc * 128:(cc + 1) * 128]
            c.raw("tensor", lambda t, bb=bb: t.ldweights(bb))
            mm = c.op("tensor", lambda t, bb=bb, cc=cc, qq=qq: t.matmul(
                PBANK[cc], bb, g16[:, qq * 512:(qq + 1) * 512], start=False, stop=True), deps=g16t)
            mms.append(mm)
        ytoks = []
        for cc in range(8):
            ytoks.append(c.op("vector", lambda v, cc=cc: v.tensor_tensor(
                y0[:, cc * 512:(cc + 1) * 512], x2cc2[:, cc * 512:(cc + 1) * 512],
                PBANK[cc], ALU.add), deps=[mms[cc]] + lxt))
        sqs = []
        st1 = None
        st2 = None
        for cc in range(8):
            sqs.append(c.op("scalar", lambda s, cc=cc: s.activation(sqq[:, cc * 512:(cc + 1) * 512],
                                                                    y0[:, cc * 512:(cc + 1) * 512], AF.Square),
                            deps=[ytoks[cc]]))
        for cc in range(8):
            st1 = c.op("tensor", lambda t, cc=cc: t.matmul(PA[0:1, 0:512], onesc[:],
                                                           y0[:, cc * 512:(cc + 1) * 512],
                                                           start=(cc == 0), stop=(cc == 7)),
                       deps=[t_o2] + ytoks + ([st1] if st1 else []))
            st2 = c.op("tensor", lambda t, cc=cc: t.matmul(PA[0:1, 512:1024], onescb2[:],
                                                           sqq[:, cc * 512:(cc + 1) * 512],
                                                           start=(cc == 0), stop=(cc == 7)),
                       deps=[sqs[cc], t_o5] + ([st2] if st2 else []))
        e1 = c.op("vector", lambda v: v.tensor_scalar(y2row[:], PA[0:1, 512:1024], 1.0 / D, None, ALU.mult),
                  deps=[st2])
        e2 = c.op("vector", lambda v: v.tensor_scalar(yrow[:], PA[0:1, 0:512], 1.0 / D, None, ALU.mult),
                  deps=[st1])
        e3 = c.op("vector", lambda v: v.tensor_tensor(myb[0:1, 0:512], yrow[:], yrow[:], ALU.mult), deps=[e2])
        e4 = c.op("vector", lambda v: v.tensor_tensor(y2row[:], y2row[:], myb[0:1, 0:512], ALU.subtract),
                  deps=[e1, e3])
        e5 = c.op("scalar", lambda s: s.activation(y2row[:], y2row[:], AF.Sqrt, bias=epsc[0:1, 0:1]),
                  deps=[e4, epst])
        e6 = c.op("vector", lambda v: v.reciprocal(y2row[:], y2row[:]), deps=[e5])
        b3 = c.op("tensor", lambda t: t.matmul(PA[:, 1024:1536], onesr[:], yrow[:], start=True, stop=True),
                  deps=[e6, t_o1] + ytoks)
        b3c = c.op("vector", lambda v: v.tensor_copy(myb[:], PA[:, 1024:1536]), deps=[b3])
        b4 = c.op("tensor", lambda t: t.matmul(PA[:, 1024:1536], onesr[:], y2row[:], start=True, stop=True),
                  deps=[b3c])
        b4c = c.op("vector", lambda v: v.tensor_copy(ryb[:], PA[:, 1024:1536]), deps=[b4])
        sto = []
        for cc in range(8):
            u1 = c.op("vector", lambda v, cc=cc: v.tensor_tensor(xot[:], y0[:, cc * 512:(cc + 1) * 512],
                                                                 myb[:], ALU.subtract),
                      deps=[b3c, b4c] + sto + [c.gtok("xst")])
            u2 = c.op("vector", lambda v: v.tensor_tensor(xot[:], xot[:], ryb[:], ALU.mult), deps=[u1])
            u3 = c.op("vector", lambda v, cc=cc: v.tensor_tensor(xot[:], xot[:], xcc[:, cc * 512:(cc + 1) * 512],
                                                                 ALU.add), deps=[u2])
            sto = [u3]
            c.dma("gpsimd", xo[cc * 128:(cc + 1) * 128, qq * 512:(qq + 1) * 512], xot[:], deps=[u3], group="xst")
    c.wait_all("gpsimd", [c.gtok("xst")])
    return c.finish()



def build_rwkv_a(Tn=T):
    HS = 16
    c = Ctx("rwkva")
    xin = c.inp("xin", [D, Tn + 1], BF16)
    maa = c.inp("maa", [5, D])
    tmw1 = c.inp("tmw1", [D, 128])
    tmw2 = c.inp("tmw2", [4, 32, D])
    decay_u = c.inp("decay_u", [D])
    dw1 = c.inp("dw1", [D, 64])
    dw2 = c.inp("dw2", [64, D])
    wr = c.inp("wr", [D, D])
    wk = c.inp("wk", [D, D])
    wv = c.inp("wv", [D, D])
    u_in = c.inp("u_in", [D])
    y1o = c.outp("y1", [D, Tn], BF16)
    rco = c.outp("rc", [D, Tn], BF16)
    seo = c.outp("se", [D, HS])

    mix_hbm = c.dram("mix_hbm", [4, D, Tn], BF16)
    r_hbm = c.dram("r_hbm", [D, Tn], BF16)
    k_hbm = c.dram("k_hbm", [D, Tn], BF16)
    v_hbm = c.dram("v_hbm", [D, Tn], BF16)
    wd_hbm = c.dram("wd_hbm", [D, Tn], F32)

    onesr = c.sbuf("onesr", [1, 128], F32, side="right")
    PA = c.psum("PA", [128, 2048])
    PB = c.psum("PB", [128, 2048])
    t_o1 = c.op("gpsimd", lambda g: g.memset(onesr[:], 1.0))

    xin_sb = c.sbuf("xin_sb", [128, 8 * (Tn + 1)], BF16)
    xx_sb = c.sbuf("xx_sb", [128, 8 * Tn], BF16)
    maac = c.sbuf("maac", [128, 5 * 8], F32)
    mg = [c.sbuf(f"mg{f}", [32, Tn], BF16) for f in range(4)]
    w1s = c.sbuf("w1s", [128, 8 * 128], BF16)
    w2s = c.sbuf("w2s", [32, 4 * D], BF16)
    TT1 = Tn + 1
    c.dma("sync", xin_sb[:].rearrange("p (cc t) -> p cc t", t=TT1),
          xin.rearrange("(cc p) t -> p cc t", p=128), group="li")
    for j in range(5):
        for cc in range(8):
            c.dma("sync", maac[:, j * 8 + cc:j * 8 + cc + 1],
                  maa[j, cc * 128:(cc + 1) * 128].rearrange("(p one) -> p one", one=1), group="li")
    for cc in range(8):
        c.dma("gpsimd", w1s[:, cc * 128:(cc + 1) * 128], tmw1[cc * 128:(cc + 1) * 128, :], group="li2")
    for f in range(4):
        c.dma("gpsimd", w2s[:, f * D:(f + 1) * D], tmw2[f], group="li2")
    li = [c.gtok("li"), c.gtok("li2")]
    xxt = []
    for cc in range(8):
        xxt.append(c.op("vector", lambda v, cc=cc: v.tensor_tensor(
            xx_sb[:, cc * Tn:(cc + 1) * Tn], xin_sb[:, cc * TT1: cc * TT1 + Tn],
            xin_sb[:, cc * TT1 + 1: cc * TT1 + Tn + 1], ALU.subtract), deps=li))
    xxx_t = c.sbuf("xxx_t", [128, 8 * 512], BF16)
    prevm = []
    for tcol in range(Tn // 512):
        xts = []
        for cc in range(8):
            xts.append(c.op("vector", lambda v, cc=cc, tcol=tcol: v.scalar_tensor_tensor(
                xxx_t[:, cc * 512:(cc + 1) * 512], xx_sb[:, cc * Tn + tcol * 512: cc * Tn + tcol * 512 + 512],
                maac[:, cc:cc + 1], xin_sb[:, cc * TT1 + tcol * 512 + 1: cc * TT1 + tcol * 512 + 513],
                ALU.mult, ALU.add), deps=[xxt[cc]] + prevm))
        mmk = None
        for cc in range(8):
            wst = w1s[:, cc * 128:(cc + 1) * 128]
            c.raw("tensor", lambda t, wst=wst: t.ldweights(wst), deps=(xts + li) if cc == 0 else [])
            mmk = c.op("tensor", lambda t, wst=wst, cc=cc: t.matmul(
                PA[:, 0:512], wst, xxx_t[:, cc * 512:(cc + 1) * 512], start=(cc == 0), stop=(cc == 7)))
        newm = []
        for f in range(4):
            newm.append(c.op("scalar", lambda s, tcol=tcol, f=f: s.activation(
                mg[f][:, tcol * 512:(tcol + 1) * 512], PA[32 * f:32 * f + 32, 0:512], AF.Tanh),
                deps=[mmk] + prevm))
        prevm = newm
    c.barrier()
    mixt = c.sbuf("mixt", [128, 512], F32)
    mixb = c.sbuf("mixb", [128, 512], BF16)
    for j in range(4):
        for tcol in range(Tn // 512):
            for cc in range(8):
                wst = w2s[0:32, j * D + cc * 128: j * D + (cc + 1) * 128]
                mmk = c.op("tensor", lambda t, wst=wst, j=j, tcol=tcol: (t.ldweights(wst), t.matmul(
                    PA[:, 512:1024], wst, mg[j][0:32, tcol * 512:(tcol + 1) * 512],
                    start=True, stop=True))[1], deps=prevm + [c.gtok("mxs")])
                a1 = c.op("vector", lambda v, j=j, cc=cc: v.tensor_scalar(
                    mixt[:], PA[:, 512:1024], maac[:, (j + 1) * 8 + cc:(j + 1) * 8 + cc + 1], None, ALU.add),
                    deps=[mmk])
                a2 = c.op("vector", lambda v, cc=cc, tcol=tcol: v.tensor_tensor(
                    mixt[:], mixt[:], xx_sb[:, cc * Tn + tcol * 512: cc * Tn + tcol * 512 + 512], ALU.mult),
                    deps=[a1])
                a3 = c.op("vector", lambda v, cc=cc, tcol=tcol: v.tensor_tensor(
                    mixb[:], mixt[:], xin_sb[:, cc * TT1 + tcol * 512 + 1: cc * TT1 + tcol * 512 + 513],
                    ALU.add), deps=[a2])
                c.dma("gpsimd", mix_hbm[j, cc * 128:(cc + 1) * 128, tcol * 512:(tcol + 1) * 512],
                      mixb[:], deps=[a3], group="mxs")
    c.barrier()
    c.sbuf_free("mixb", "mixt", "xxx_t", "w2s", "w1s", "mg3", "mg2", "mg1", "mg0", "xx_sb", "xin_sb", "maac")

    sc1 = c.sbuf("sc1", [128, 1024], F32, side="right")
    sc2 = c.sbuf("sc2", [128, Tn], F32, side="right")
    scb = c.sbuf("scb", [128, 1024], BF16, side="right")
    mx_sb = c.sbuf("mx_sb", [128, 8 * Tn], BF16)
    wbig = [c.sbuf(f"wbig{i}", [128, 8 * 512], BF16) for i in range(2)]
    mxl = [c.gtok("mxs")]

    def proj(mix_idx, w_in, out_h):
        c.barrier()
        c.dma("sync", mx_sb[:].rearrange("p (cc t) -> p cc t", t=Tn),
              mix_hbm[mix_idx].rearrange("(cc p) t -> p cc t", p=128), deps=mxl, group="lmx")
        lmx = [c.gtok("lmx")]
        wwar = {0: [], 1: []}
        prev = []
        for oc in range(2):
            c.dma("gpsimd", wbig[oc][:].rearrange("p (cc n) -> p cc n", n=512),
                  w_in[:, oc * 512:(oc + 1) * 512].rearrange("(cc p) n -> p cc n", p=128),
                  deps=wwar[oc], group=f"wld{oc}")
            lw = [c.gtok(f"wld{oc}")]
            for ob in range(4):
                for tcol in range(Tn // 512):
                    mmk = None
                    for cc in range(8):
                        wst = wbig[oc][:, cc * 512 + ob * 128: cc * 512 + (ob + 1) * 128]
                        c.raw("tensor", lambda t, wst=wst: t.ldweights(wst),
                              deps=(lw + lmx + prev) if cc == 0 else [])
                        mmk = c.op("tensor", lambda t, wst=wst, cc=cc, tcol=tcol: t.matmul(
                            PA[:, 0:512], wst, mx_sb[:, cc * Tn + tcol * 512: cc * Tn + tcol * 512 + 512],
                            start=(cc == 0), stop=(cc == 7)))
                    e1 = c.op("vector", lambda v: v.tensor_copy(scb[:, 0:512], PA[:, 0:512]),
                              deps=[mmk] + prev)
                    c.dma("gpsimd", out_h[(oc * 4 + ob) * 128:(oc * 4 + ob + 1) * 128,
                                          tcol * 512:(tcol + 1) * 512], scb[:, 0:512], deps=[e1], group="pst")
                    prev = [e1, c.gtok("pst")]
    proj(3, wr, r_hbm)
    proj(1, wk, k_hbm)
    proj(2, wv, v_hbm)
    t1cb = c.sbuf("t1cb", [64, Tn], BF16)
    dw1s = c.sbuf("dw1s", [128, 8 * 64], BF16)
    dw2us = c.sbuf("dw2us", [64, D], BF16)
    dcolu = c.sbuf("dcolu", [128, 8], F32)
    for cc in range(8):
        c.dma("gpsimd", dw1s[:, cc * 64:(cc + 1) * 64], dw1[cc * 128:(cc + 1) * 128, :], group="li4")
        c.dma("sync", dcolu[:, cc:cc + 1], decay_u[cc * 128:(cc + 1) * 128].rearrange("(p one) -> p one", one=1),
              group="li4b")
    c.dma("gpsimd", dw2us[:], dw2[:], group="li4")
    li4 = [c.gtok("li4"), c.gtok("li4b")]
    c.barrier()
    c.dma("sync", mx_sb[:].rearrange("p (cc t) -> p cc t", t=Tn),
          mix_hbm[0].rearrange("(cc p) t -> p cc t", p=128), group="lmx")
    lmx = [c.gtok("lmx")]
    prev = []
    for tcol in range(Tn // 512):
        mmk = None
        for cc in range(8):
            wst = dw1s[:, cc * 64:(cc + 1) * 64]
            c.raw("tensor", lambda t, wst=wst: t.ldweights(wst), deps=(li4 + lmx + prev) if cc == 0 else [])
            mmk = c.op("tensor", lambda t, wst=wst, cc=cc, tcol=tcol: t.matmul(
                PA[0:64, 0:512], wst, mx_sb[:, cc * Tn + tcol * 512: cc * Tn + tcol * 512 + 512],
                start=(cc == 0), stop=(cc == 7)))
        a1 = c.op("scalar", lambda s, tcol=tcol: s.activation(sc1[0:64, 0:512],
                                                              PA[0:64, 0:512], AF.Tanh), deps=[mmk] + prev)
        prev = [c.op("vector", lambda v, tcol=tcol: v.tensor_copy(t1cb[:, tcol * 512:(tcol + 1) * 512],
                                                                  sc1[0:64, 0:512]),
                     deps=[a1])]
    c.barrier()
    prevw = []
    for cc in range(8):
        for tcol in range(Tn // 512):
            mmk = c.op("tensor", lambda t, cc=cc, tcol=tcol: (t.ldweights(dw2us[:, cc * 128:(cc + 1) * 128]),
                       t.matmul(PA[:, 512:1024], dw2us[:, cc * 128:(cc + 1) * 128],
                                t1cb[0:64, tcol * 512:(tcol + 1) * 512], start=True, stop=True))[1],
                       deps=prev + prevw)
            a1 = c.op("scalar", lambda s, cc=cc: s.activation(sc2[:, 0:512], PA[:, 512:1024], AF.Exp,
                                                              bias=dcolu[:, cc:cc + 1]), deps=[mmk] + li4 + prevw)
            a2 = c.op("scalar", lambda s: s.activation(sc1[:, 0:512], sc2[:, 0:512], AF.Exp, scale=-1.0),
                      deps=[a1])
            c.dma("gpsimd", wd_hbm[cc * 128:(cc + 1) * 128, tcol * 512:(tcol + 1) * 512], sc1[:, 0:512],
                  deps=[a2], group="wds")
            prevw = [a2, c.gtok("wds")]
    c.barrier()
    c.sbuf_free("dcolu", "dw2us", "dw1s", "t1cb", "wbig1", "wbig0", "mx_sb")

    r_sb = c.sbuf("r_sb", [128, Tn], BF16)
    k_sb = c.sbuf("k_sb", [128, Tn], BF16)
    wd_sb = c.sbuf("wd_sb", [128, Tn], F32)
    ucol = c.sbuf("ucol", [128, 8], F32)
    vjb = [c.sbuf(f"vjb{i}", [128, Tn], BF16) for i in range(2)]
    kv_sb = c.sbuf("kv_sb", [128, Tn], F32)
    Ssc = c.sbuf("Ssc", [128, Tn + 1], F32)
    yt = c.sbuf("yt", [128, Tn], BF16)
    seg8i = c.inp("seg8i", [128, 8], BF16)
    seg8 = c.sbuf("seg8", [128, 8], BF16)
    rowb = c.sbuf("rowb", [8, Tn], BF16)
    cumw = c.sbuf("cumw", [128, Tn + 1], F32)
    rcb = c.sbuf("rcb", [128, Tn], BF16)
    sebuf = c.sbuf("sebuf", [128, HS], F32)
    zeros_t = c.sbuf("zeros_t", [128, Tn], BF16)
    c.dma("sync", seg8[:], seg8i[:], group="li6")
    for cc in range(8):
        c.dma("sync", ucol[:, cc:cc + 1], u_in[cc * 128:(cc + 1) * 128].rearrange("(p one) -> p one", one=1),
              group="li6")
    li6 = [c.gtok("li6")]
    zz = c.op("vector", lambda v: v.memset(zeros_t[:], 0.0))
    for cc in range(8):
        c.barrier()
        c.dma("sync", r_sb[:], r_hbm[cc * 128:(cc + 1) * 128, :], deps=[c.gtok("pst")], group="lsc")
        c.dma("sync", k_sb[:], k_hbm[cc * 128:(cc + 1) * 128, :], group="lsc")
        c.dma("sync", wd_sb[:], wd_hbm[cc * 128:(cc + 1) * 128, :], deps=[c.gtok("wds")], group="lsc")
        lsc = [c.gtok("lsc")]
        zc1 = c.op("vector", lambda v: v.memset(cumw[:, 0:1], 1.0), deps=lsc)
        zc2 = c.op("vector", lambda v: v.tensor_tensor_scan(cumw[:, 1:Tn + 1], wd_sb[:], zeros_t[:],
                                                            1.0, ALU.mult, ALU.add), deps=[zc1, zz] + lsc)
        zc3 = c.op("vector", lambda v: v.tensor_tensor(rcb[:], r_sb[:], cumw[:, 0:Tn], ALU.mult),
                   deps=[zc2])
        c.dma("gpsimd", rco[cc * 128:(cc + 1) * 128, :], rcb[:], deps=[zc3], group="rcs")
        prevj = []
        for j in range(HS):
            vb = vjb[j % 2]
            c.dma("sync", vb[:],
                  mkap(v_hbm[0:1, 0:1], (cc * 128 + j) * Tn, [[16 * Tn, 8], [0, 16], [1, Tn]]),
                  deps=[c.gtok("pst")] + prevj, group="lvj")
            lvt = [c.gtok("lvj")]
            kv1 = c.op("vector", lambda v, vb=vb: v.tensor_tensor(kv_sb[:], k_sb[:], vb[:], ALU.mult),
                       deps=lvt + lsc + prevj)
            zs = c.op("vector", lambda v: v.memset(Ssc[:, 0:1], 0.0), deps=prevj)
            sc = c.op("vector", lambda v: v.tensor_tensor_scan(Ssc[:, 1:Tn + 1], wd_sb[:], kv_sb[:],
                                                               0.0, ALU.mult, ALU.add), deps=[kv1, zs])
            tt1 = c.op("vector", lambda v, cc=cc: v.scalar_tensor_tensor(sc2[:], kv_sb[:], ucol[:, cc:cc + 1],
                                                                         Ssc[:, 0:Tn], ALU.mult, ALU.add),
                       deps=[sc] + li6)
            tt2 = c.op("vector", lambda v: v.tensor_tensor(yt[:], sc2[:], r_sb[:], ALU.mult), deps=[tt1])
            mms = []
            for tcol in range(Tn // 512):
                mk = c.op("tensor", lambda t, tcol=tcol: (t.ldweights(seg8[:]), t.matmul(
                    PB[0:8, 0:512], seg8[:], yt[:, tcol * 512:(tcol + 1) * 512], start=True, stop=True))[1],
                    deps=[tt2] + li6 + mms)
                mms.append(c.op("vector", lambda v, tcol=tcol: v.tensor_copy(
                    rowb[0:8, tcol * 512:(tcol + 1) * 512], PB[0:8, 0:512]), deps=[mk]))
            c.dma("gpsimd", mkap(y1o[0:1, 0:1], (cc * 128 + j) * Tn, [[16 * Tn, 8], [1, Tn]]),
                  rowb[0:8, :], deps=mms, group="yst")
            se1 = c.op("vector", lambda v, j=j: v.tensor_copy(sebuf[:, j:j + 1], Ssc[:, Tn:Tn + 1]),
                       deps=[sc])
            prevj = mms + [se1, c.gtok("yst"), kv1, tt2]
        c.dma("gpsimd", seo[cc * 128:(cc + 1) * 128, :], sebuf[:], deps=prevj, group="ses")
    c.wait_all("gpsimd", [c.gtok("yst"), c.gtok("ses"), c.gtok("rcs")])
    return c.finish()



def build_rwkv_b(Tn=T):
    HS = 16
    c = Ctx("rwkvb")
    y1 = c.inp("y1", [D, Tn], BF16)
    rc = c.inp("rc", [D, Tn], BF16)
    sprev = c.inp("sprev", [D, HS])
    x = c.inp("x", [D, Tn])
    ln_g = c.inp("ln_g", [D])
    ln_b = c.inp("ln_b", [D])
    wo = c.inp("wo", [D, D])
    seg8i = c.inp("seg8i", [128, 8], BF16)
    xo = c.outp("xo", [D, Tn])
    corr_hbm = c.dram("corr_hbm", [D, Tn], BF16)

    onesr = c.sbuf("onesr", [1, 128], F32, side="right")
    onesc = c.sbuf("onesc", [128, 1], F32, side="right")
    onescb = c.sbuf("onescb", [128, 1], BF16, side="right")
    sc1 = c.sbuf("sc1", [128, 512], F32, side="right")
    PA = c.psum("PA", [128, 2048])
    PB = c.psum("PB", [128, 2048])
    t_o1 = c.op("gpsimd", lambda g: g.memset(onesr[:], 1.0))
    t_o2 = c.op("gpsimd", lambda g: g.memset(onesc[:], 1.0))
    t_o3 = c.op("gpsimd", lambda g: g.memset(onescb[:], 1.0))
    epsc, epst = c.constcol(EPS)

    rc_sb = c.sbuf("rc_sb", [128, Tn], BF16)
    sp_sb = c.sbuf("sp_sb", [128, HS], F32)
    seg8 = c.sbuf("seg8", [128, 8], BF16)
    rowb = c.sbuf("rowb", [8, Tn], BF16)
    tmpb = c.sbuf("tmpb", [128, Tn], BF16)
    c.dma("sync", seg8[:], seg8i[:], group="li")
    li = [c.gtok("li")]
    for cc in range(8):
        c.barrier()
        c.dma("sync", rc_sb[:], rc[cc * 128:(cc + 1) * 128, :], group="lrc")
        c.dma("sync", sp_sb[:], sprev[cc * 128:(cc + 1) * 128, :], group="lrc")
        lrc = [c.gtok("lrc")]
        for j in range(HS):
            t1 = c.op("vector", lambda v, j=j: v.tensor_scalar(tmpb[:], rc_sb[:], sp_sb[:, j:j + 1], None,
                                                               ALU.mult), deps=lrc + [c.gtok("cst")])
            mms = []
            for tcol in range(Tn // 512):
                mk = c.op("tensor", lambda t, tcol=tcol: (t.ldweights(seg8[:]), t.matmul(
                    PB[0:8, 0:512], seg8[:], tmpb[:, tcol * 512:(tcol + 1) * 512], start=True, stop=True))[1],
                    deps=[t1] + li + mms)
                mms.append(c.op("vector", lambda v, tcol=tcol: v.tensor_copy(
                    rowb[0:8, tcol * 512:(tcol + 1) * 512], PB[0:8, 0:512]), deps=[mk]))
            c.dma("gpsimd", mkap(corr_hbm[0:1, 0:1], (cc * 128 + j) * Tn, [[16 * Tn, 8], [1, Tn]]),
                  rowb[0:8, :], deps=mms, group="cst")
    c.barrier()
    c.sbuf_free("tmpb", "rowb", "seg8", "sp_sb", "rc_sb")

    wosb = c.sbuf("wosb", [128, 8 * D], BF16)
    gbcol = c.sbuf("gbcol", [128, 16], F32)
    ycc = c.sbuf("ycc", [128, 8 * 512], F32)
    ynb = c.sbuf("ynb", [128, 8 * 512], BF16)
    zcc = c.sbuf("zcc", [128, 8 * 512], F32)
    xcc = c.sbuf("xcc", [128, 8 * 512], F32)
    sqq = c.sbuf("sqq", [128, 8 * 512], BF16)
    t1b = c.sbuf("t1b", [128, 512], BF16)
    t2b = c.sbuf("t2b", [128, 512], BF16)
    yrow = c.sbuf("yrow", [1, 512], F32)
    y2row = c.sbuf("y2row", [1, 512], F32)
    myb = c.sbuf("myb", [128, 512], F32)
    ryb = c.sbuf("ryb", [128, 512], F32)
    xot = c.sbuf("xot", [128, 512], F32)
    c.dma("gpsimd", wosb[:].rearrange("p (cc n) -> p cc n", n=D),
          wo.rearrange("(cc p) n -> p cc n", p=128), group="lwo")
    for cc in range(8):
        c.dma("sync", gbcol[:, cc:cc + 1], ln_g[cc * 128:(cc + 1) * 128].rearrange("(p one) -> p one", one=1),
              group="lwo2")
        c.dma("sync", gbcol[:, 8 + cc:9 + cc], ln_b[cc * 128:(cc + 1) * 128].rearrange("(p one) -> p one", one=1),
              group="lwo2")
    lwo = [c.gtok("lwo"), c.gtok("lwo2")]

    def ln_stats_rows(src_f32, srcq_bf, deps):
        st1 = None
        st2 = None
        for cc in range(8):
            st1 = c.op("tensor", lambda t, cc=cc: t.matmul(PB[0:1, 0:512], onesc[:],
                                                           src_f32[:, cc * 512:(cc + 1) * 512],
                                                           start=(cc == 0), stop=(cc == 7)),
                       deps=deps + [t_o2] + ([st1] if st1 else []))
            st2 = c.op("tensor", lambda t, cc=cc: t.matmul(PB[0:1, 512:1024], onescb[:],
                                                           srcq_bf[:, cc * 512:(cc + 1) * 512],
                                                           start=(cc == 0), stop=(cc == 7)),
                       deps=deps + [t_o3] + ([st2] if st2 else []))
        e1 = c.op("vector", lambda v: v.tensor_scalar(y2row[:], PB[0:1, 512:1024], 1.0 / D, None, ALU.mult),
                  deps=[st2])
        e2 = c.op("vector", lambda v: v.tensor_scalar(yrow[:], PB[0:1, 0:512], 1.0 / D, None, ALU.mult),
                  deps=[st1])
        e3 = c.op("vector", lambda v: v.tensor_tensor(myb[0:1, 0:512], yrow[:], yrow[:], ALU.mult), deps=[e2])
        e4 = c.op("vector", lambda v: v.tensor_tensor(y2row[:], y2row[:], myb[0:1, 0:512], ALU.subtract),
                  deps=[e1, e3])
        e5 = c.op("scalar", lambda s: s.activation(y2row[:], y2row[:], AF.Sqrt, bias=epsc[0:1, 0:1]),
                  deps=[e4, epst])
        e6 = c.op("vector", lambda v: v.reciprocal(y2row[:], y2row[:]), deps=[e5])
        b3 = c.op("tensor", lambda t: t.matmul(PB[:, 1024:1536], onesr[:], yrow[:], start=True, stop=True),
                  deps=[e6, t_o1])
        b3c = c.op("vector", lambda v: v.tensor_copy(myb[:], PB[:, 1024:1536]), deps=[b3])
        b4 = c.op("tensor", lambda t: t.matmul(PB[:, 1024:1536], onesr[:], y2row[:], start=True, stop=True),
                  deps=[b3c])
        b4c = c.op("vector", lambda v: v.tensor_copy(ryb[:], PB[:, 1024:1536]), deps=[b4])
        return [b3c, b4c]

    for qcol in range(Tn // 512):
        c.barrier()
        qs = qcol * 512
        adds = []
        for cc in range(8):
            c.dma("sync", t1b[:], y1[cc * 128:(cc + 1) * 128, qs:qs + 512], deps=adds, group="lyq")
            c.dma("sync", t2b[:], corr_hbm[cc * 128:(cc + 1) * 128, qs:qs + 512],
                  deps=[c.gtok("cst")], group="lyq")
            ly = [c.gtok("lyq")]
            a1 = c.op("vector", lambda v, cc=cc: v.tensor_tensor(ycc[:, cc * 512:(cc + 1) * 512],
                                                                 t1b[:], t2b[:], ALU.add), deps=ly)
            a2 = c.op("scalar", lambda s, cc=cc: s.activation(sqq[:, cc * 512:(cc + 1) * 512],
                                                              ycc[:, cc * 512:(cc + 1) * 512], AF.Square),
                      deps=[a1])
            adds += [a1, a2]
        bt = ln_stats_rows(ycc, sqq, adds)
        yns = []
        for cc in range(8):
            n1 = c.op("vector", lambda v, cc=cc: v.tensor_tensor(sc1[:, 0:512],
                                                                 ycc[:, cc * 512:(cc + 1) * 512], myb[:],
                                                                 ALU.subtract), deps=bt + yns[-1:])
            n2 = c.op("vector", lambda v: v.tensor_tensor(sc1[:, 0:512], sc1[:, 0:512], ryb[:], ALU.mult),
                      deps=[n1])
            yns.append(c.op("vector", lambda v, cc=cc: v.tensor_scalar(
                ynb[:, cc * 512:(cc + 1) * 512], sc1[:, 0:512], gbcol[:, cc:cc + 1],
                gbcol[:, 8 + cc:9 + cc], ALU.mult, ALU.add), deps=[n2] + lwo))
        PBANK = [PA[:, 0:512], PA[:, 512:1024], PA[:, 1024:1536], PA[:, 1536:2048],
                 PB[:, 0:512], PB[:, 512:1024], PB[:, 1024:1536], PB[:, 1536:2048]]
        mm = None
        for occ in range(8):
            for cc in range(8):
                wst = wosb[:, cc * D + occ * 128: cc * D + occ * 128 + 128]
                c.raw("tensor", lambda t, wst=wst: t.ldweights(wst),
                      deps=(yns + lwo + bt) if (occ == 0 and cc == 0) else [])
                mm = c.op("tensor", lambda t, wst=wst, occ=occ, cc=cc: t.matmul(
                    PBANK[occ], wst, ynb[:, cc * 512:(cc + 1) * 512], start=(cc == 0), stop=(cc == 7)))
        c.dma("sync", xcc[:].rearrange("p (cc n) -> p cc n", n=512),
              x[:, qs:qs + 512].rearrange("(cc p) n -> p cc n", p=128), group="lxq")
        lxt = [c.gtok("lxq")]
        ztoks = []
        sqs = []
        for occ in range(8):
            z1 = c.op("vector", lambda v, occ=occ: v.tensor_copy(zcc[:, occ * 512:(occ + 1) * 512],
                                                                 PBANK[occ]), deps=[mm] + sqs[-1:])
            ztoks.append(z1)
            sqs.append(c.op("scalar", lambda s, occ=occ: s.activation(sqq[:, occ * 512:(occ + 1) * 512],
                                                                      zcc[:, occ * 512:(occ + 1) * 512],
                                                                      AF.Square), deps=[z1]))
        bt2 = ln_stats_rows(zcc, sqq, ztoks + sqs)
        sto = []
        for cc in range(8):
            u1 = c.op("vector", lambda v, cc=cc: v.tensor_tensor(xot[:], zcc[:, cc * 512:(cc + 1) * 512],
                                                                 myb[:], ALU.subtract),
                      deps=bt2 + sto + [c.gtok("xst")])
            u2 = c.op("vector", lambda v: v.tensor_tensor(xot[:], xot[:], ryb[:], ALU.mult), deps=[u1])
            u3 = c.op("vector", lambda v, cc=cc: v.tensor_tensor(xot[:], xot[:], xcc[:, cc * 512:(cc + 1) * 512],
                                                                 ALU.add), deps=[u2] + lxt)
            sto = [u3]
            c.dma("gpsimd", xo[cc * 128:(cc + 1) * 128, qs:qs + 512], xot[:], deps=[u3], group="xst")
    c.wait_all("gpsimd", [c.gtok("xst")])
    return c.finish()



def build_l1f_layer(Tn=T):
    A = 513 + Tn
    KV = 512 + Tn
    NW = Tn // 256 + 1
    NT = KV // 128
    ATN = Tn + 128
    FH2 = 2816
    NHC = FH2 // 128
    c = Ctx("l1f")
    xa = c.inp("xa", [D, A])
    wqkv = c.inp("wqkv", [D, 3072])
    wo = c.inp("wo", [D, D])
    lng = c.inp("lng", [D])
    lnb = c.inp("lnb", [D])
    w1a = c.inp("w1a", [D, FH2])
    w1g = c.inp("w1g", [D, FH2])
    w2 = c.inp("w2", [FH2, D])
    b1a = c.inp("b1a", [FH2])
    b1g = c.inp("b1g", [FH2])
    b2 = c.inp("b2", [D])
    tabq = c.inp("tabq", [2, KV, 64])
    tabk = c.inp("tabk", [2, KV, 64])
    wmq = c.inp("wmq", [3, 256, 512])
    rolec = c.inp("rolec", [128, 1])
    xo2 = c.outp("xo2", [D, Tn])

    qn_h = c.dram("qn_h", [8, KV, 128], BF16)
    kn_h = c.dram("kn_h", [8, KV, 128], BF16)
    v_h = c.dram("v_h", [KV, D], BF16)
    qcm_h = c.dram("qcm_h", [D, KV], BF16)
    kcm_h = c.dram("kcm_h", [D, KV], BF16)
    at_h = c.dram("at_h", [8, ATN, 128], BF16)
    x2_h = c.dram("x2_h", [D, Tn + 1], F32)
    w1g_h = c.dram("w1g_h", [D, FH2], BF16)
    w2_h = c.dram("w2_h", [FH2, D], BF16)

    onesr = c.sbuf("onesr", [1, 128], F32, side="right")
    onesc = c.sbuf("onesc", [128, 1], F32, side="right")
    onescb = c.sbuf("onescb", [128, 1], BF16, side="right")
    idb = c.sbuf("idb", [128, 128], BF16, side="right")
    wmq_sb = c.sbuf("wmq_sb", [128, 3 * 1024], F32, side="right")
    role_sb = c.sbuf("role_sb", [128, 1], F32, side="right")
    t_o1 = c.op("gpsimd", lambda g: g.memset(onesr[:], 1.0))
    t_o2 = c.op("gpsimd", lambda g: g.memset(onesc[:], 1.0))
    t_o3 = c.op("gpsimd", lambda g: g.memset(onescb[:], 1.0))
    epsc, epst = c.constcol(EPS)
    sid1 = c.sbuf("sid1", [128, 128], F32, side="right")
    sid2 = c.sbuf("sid2", [128, 128], F32, side="right")
    ii1 = c.op("gpsimd", lambda g: g.iota(sid1[:], [[1, 128]], channel_multiplier=0,
                                          allow_small_or_imprecise_dtypes=True))
    ii2 = c.op("gpsimd", lambda g: g.iota(sid2[:], [[0, 128]], channel_multiplier=1,
                                          allow_small_or_imprecise_dtypes=True))
    idt = c.op("vector", lambda v: v.tensor_tensor(idb[:], sid1[:], sid2[:], ALU.is_equal),
               deps=[ii1, ii2])
    for s in range(3):
        for qsub in range(2):
            c.dma("sync", wmq_sb[:, s * 1024 + qsub * 512:(s * 1024 + qsub * 512) + 512],
                  wmq[s, qsub * 128:(qsub + 1) * 128, :], group="lwm")
    c.dma("sync", role_sb[:], rolec[:], group="lwm")
    lwm = [c.gtok("lwm")]

    c.dma("gpsimd", w1g_h[:, :], w1g[:, :], group="wcast")
    c.dma("gpsimd", w2_h[:, :], w2[:, :], group="wcast")

    wqs = c.sbuf("wqs", [128, 8 * 3072], BF16)
    HC = (NT // 2) * 128
    xab = c.sbuf("xab", [128, 8 * (HC + 1)], BF16)
    tbt = c.sbuf("tbt", [128, 4 * 64], F32)
    sqt = c.sbuf("sqt", [128, 1024], F32)
    ssq = c.sbuf("ssq", [128, 32], F32)
    qnt = c.sbuf("qnt", [128, 1024], F32)
    rot = c.sbuf("rot", [128, 1024], F32)
    tt1 = c.sbuf("tt1", [128, 1024], F32)
    qfb = c.sbuf("qfb", [128, 1024], BF16)
    kfb = c.sbuf("kfb", [128, 1024], BF16)
    vfb = c.sbuf("vfb", [128, 1024], BF16)
    PA = c.psum("PA", [128, 2048])
    PB = c.psum("PB", [128, 1024])
    PT = c.psum("PT", [128, 512], BF16)
    for cc in range(8):
        c.dma("gpsimd", wqs[:, cc * 3072:(cc + 1) * 3072], wqkv[cc * 128:(cc + 1) * 128, :],
              group="lw")
    lw = [c.gtok("lw")]

    def qk_post(src_ps, tb_base, outb, deps):
        v3 = src_ps.rearrange("p (h d) -> p h d", d=64)
        s1 = c.op("scalar", lambda s: s.activation(sqt[:], src_ps, AF.Square), deps=deps)
        s2 = c.op("vector", lambda v: v.reduce_sum(ssq[:, 0:16], sqt[:].rearrange(
            "p (h d) -> p h d", d=64), axis=AX.X), deps=[s1])
        s3 = c.op("scalar", lambda s: s.activation(ssq[:, 16:32], ssq[:, 0:16], AF.Sqrt), deps=[s2])
        s4 = c.op("vector", lambda v: v.tensor_scalar(ssq[:, 16:32], ssq[:, 16:32], 1e-12, None,
                                                      ALU.max), deps=[s3])
        s5 = c.op("vector", lambda v: v.reciprocal(ssq[:, 16:32], ssq[:, 16:32]), deps=[s4])
        rq = ssq[:, 16:32].unsqueeze(2).broadcast_to([128, 16, 64])
        s6 = c.op("vector", lambda v: v.tensor_tensor(qnt[:].rearrange("p (h d) -> p h d", d=64),
                                                      v3, rq, ALU.mult), deps=[s5])
        cs = tbt[:, tb_base:tb_base + 64].unsqueeze(1).broadcast_to([128, 16, 64])
        snL = mkap(tbt[:, 0:1], tb_base + 64, [[4 * 64, 128], [0, 16], [1, 32]])
        snR = mkap(tbt[:, 0:1], tb_base + 64 + 32, [[4 * 64, 128], [0, 16], [1, 32]])
        qnL = mkap(qnt[:, 0:1], 0, [[1024, 128], [64, 16], [1, 32]])
        qnR = mkap(qnt[:, 0:1], 32, [[1024, 128], [64, 16], [1, 32]])
        rotL = mkap(rot[:, 0:1], 0, [[1024, 128], [64, 16], [1, 32]])
        rotR = mkap(rot[:, 0:1], 32, [[1024, 128], [64, 16], [1, 32]])
        r1 = c.op("gpsimd", lambda g: g.tensor_tensor(rotL, qnR, snL, ALU.mult),
                  deps=[s6, c.gtok("tbl")])
        r2 = c.op("gpsimd", lambda g: g.tensor_tensor(rotR, qnL, snR, ALU.mult), deps=[s6])
        r3 = c.op("vector", lambda v: v.tensor_tensor(tt1[:].rearrange("p (h d) -> p h d", d=64),
                                                      qnt[:].rearrange("p (h d) -> p h d", d=64),
                                                      cs, ALU.mult), deps=[s6, c.gtok("tbl")])
        r4 = c.op("gpsimd", lambda g: g.tensor_tensor(outb[:], tt1[:], rot[:], ALU.add),
                  deps=[r1, r2, r3])
        return [r4, s1, s6]

    pa_free = []
    for half in range(2):
        c.barrier()
        cb = half * HC
        for cc in range(8):
            c.dma("gpsimd", xab[:, cc * (HC + 1):(cc + 1) * (HC + 1)],
                  xa[cc * 128:(cc + 1) * 128, cb:cb + HC + 1], group="lxa")
        lxa = [c.gtok("lxa")]
        for tt in range(half * (NT // 2), (half + 1) * (NT // 2)):
            kk = tt * 128
            c.dma("sync", tbt[:, 0:64], tabq[0, kk:kk + 128, :],
                  deps=[c.gtok("Pgpsimd"), c.gtok("Pvector")], group="tbl")
            c.dma("sync", tbt[:, 64:128], tabq[1, kk:kk + 128, :], group="tbl")
            c.dma("sync", tbt[:, 128:192], tabk[0, kk:kk + 128, :], group="tbl")
            c.dma("sync", tbt[:, 192:256], tabk[1, kk:kk + 128, :], group="tbl")
            tl = c.gtok("tbl")
            mmdep = list(pa_free)
            mm = None
            for cc in range(8):
                loc = cc * (HC + 1) + (kk - cb) + (1 if cc < 4 else 0)
                st = xab[:, loc:loc + 128]
                c.raw("tensor", lambda t, st=st: t.ldweights(st),
                      deps=(lxa + lw + mmdep) if cc == 0 else [])
                for nn in range(6):
                    dst = PA[:, nn * 512:(nn + 1) * 512] if nn < 4 else PB[:, (nn - 4) * 512:(nn - 3) * 512]
                    mm = c.op("tensor", lambda t, st=st, cc=cc, nn=nn, dst=dst: t.matmul(
                        dst, st, wqs[:, cc * 3072 + nn * 512:cc * 3072 + (nn + 1) * 512],
                        start=(cc == 0), stop=(cc == 7)))
            qd = qk_post(PA[:, 0:1024], 0, qfb, [mm, c.gtok("sq"), tl])
            kd = qk_post(PA[:, 1024:2048], 128, kfb, qd + [c.gtok("sk")])
            vd = c.op("scalar", lambda s: s.activation(vfb[:], PB[:, 0:1024], AF.Copy, bias=0.0),
                      deps=[mm, c.gtok("sv")] + kd)
            pa_free = [qd[1], qd[2], kd[1], kd[2], vd]
            c.dma("sync", mkap(qn_h[0:1, 0:1, 0:1], kk * 128,
                               [[128, 128], [KV * 128, 8], [1, 128]]),
                  qfb[:].rearrange("p (cc n) -> p cc n", n=128), deps=qd, group="sq")
            c.dma("sync", mkap(kn_h[0:1, 0:1, 0:1], kk * 128,
                               [[128, 128], [KV * 128, 8], [1, 128]]),
                  kfb[:].rearrange("p (cc n) -> p cc n", n=128), deps=kd, group="sk")
            c.dma("sync", v_h[kk:kk + 128, :], vfb[:], deps=[vd], group="sv")
    c.barrier()
    c.sbuf_free("xab", "tbt", "sqt", "ssq", "qnt", "rot", "tt1", "qfb", "kfb", "vfb", "wqs")

    tch = c.sbuf("tch", [128, KV], BF16)
    tch2 = c.sbuf("tch2", [128, KV], BF16)
    for cc2 in range(8):
        buf = tch if cc2 % 2 == 0 else tch2
        gq = "tq%d" % (cc2 % 2)
        c.dma("sync", buf[:], qn_h[cc2].rearrange("t p -> t p"), deps=[c.gtok(gq)],
              transpose=True, group=gq)
        c.dma("sync", qcm_h[cc2 * 128:(cc2 + 1) * 128, :], buf[:], deps=[c.gtok(gq)], group=gq)
    for cc2 in range(8):
        buf = tch if cc2 % 2 == 0 else tch2
        gq = "tq%d" % (cc2 % 2)
        c.dma("sync", buf[:], kn_h[cc2].rearrange("t p -> t p"), deps=[c.gtok(gq)],
              transpose=True, group=gq)
        c.dma("sync", kcm_h[cc2 * 128:(cc2 + 1) * 128, :], buf[:], deps=[c.gtok(gq)], group=gq)
    c.barrier()
    c.sbuf_free("tch", "tch2")

    qwin2 = c.sbuf("qwin2", [128, 2 * 8 * 256], BF16)
    kwin2 = c.sbuf("kwin2", [128, 2 * 8 * 512], BF16)
    vwin2 = c.sbuf("vwin2", [128, 2 * 4 * 1024], BF16)
    scq = c.sbuf("scq", [128, 2 * 512], F32)
    Eq = c.sbuf("Eq", [128, 2 * 512], BF16)
    ET = c.sbuf("ET", [128, 2 * 512], BF16)
    mcol = c.sbuf("mcol", [128, 4], F32)
    zcol = c.sbuf("zcol", [128, 4], F32)
    atb = c.sbuf("atb", [128, 2 * 1024], BF16)
    c.barrier()
    prevA = []
    jmm = {}
    for j in range(1, NW + 1):
        k0 = 256 * (j - 1)
        par = j % 2
        qwin = qwin2[:, par * 2048:(par + 1) * 2048]
        kwin = kwin2[:, par * 4096:(par + 1) * 4096]
        vwin = vwin2[:, par * 4096:(par + 1) * 4096]
        ldep = [jmm[j - 2]] if (j - 2) in jmm else []
        c.dma("sync", qwin.rearrange("p (hp n) -> p hp n", n=256),
              mkap(qcm_h[0:1, 0:1], k0 + 256, [[KV, 128], [128 * KV, 8], [1, 256]]),
              deps=ldep, group="lq%d" % par)
        c.dma("sync", kwin.rearrange("p (hp n) -> p hp n", n=512),
              mkap(kcm_h[0:1, 0:1], k0, [[KV, 128], [128 * KV, 8], [1, 512]]),
              group="lq%d" % par)
        c.dma("sync", vwin.rearrange("p (kt n) -> p kt n", n=1024),
              mkap(v_h[0:1, 0:1], k0 * 1024, [[1024, 128], [128 * 1024, 4], [1, 1024]]),
              group="lq%d" % par)
        lq = [c.gtok("lq%d" % par)]
        msel = 0 if j == 1 else (1 if j == 2 else 2)
        for h in range(16):
            hp, hr = h // 2, 64 * (h % 2)
            mmq = None
            for qsub in range(2):
                lhs = qwin[hr:hr + 64, hp * 256 + qsub * 128:hp * 256 + qsub * 128 + 128]
                c.raw("tensor", lambda t, lhs=lhs, hr=hr: t.ldweights(lhs, tile_position=(hr, 0)),
                      deps=lq + prevA)
                mmq = c.op("tensor", lambda t, lhs=lhs, qsub=qsub, hr=hr, hp=hp, kwin=kwin:
                           t.matmul(
                    PA[:, qsub * 512:(qsub + 1) * 512], lhs,
                    kwin[hr:hr + 64, hp * 512:(hp + 1) * 512], start=True, stop=True,
                    tile_position=(hr, 0)))
            prevA = []
            for qsub in range(2):
                a1 = c.op("vector", lambda v, qsub=qsub, msel=msel: v.scalar_tensor_tensor(
                    scq[:, qsub * 512:(qsub + 1) * 512], PA[:, qsub * 512:(qsub + 1) * 512], 8.0,
                    wmq_sb[:, msel * 1024 + qsub * 512:msel * 1024 + qsub * 512 + 512],
                    ALU.mult, ALU.add), deps=[mmq] + lwm)
                prevA.append(a1)
                a2 = c.op("vector", lambda v, qsub=qsub: v.reduce_max(
                    mcol[:, qsub:qsub + 1], scq[:, qsub * 512:(qsub + 1) * 512], axis=AX.X),
                    deps=[a1])
                a3 = c.op("vector", lambda v, qsub=qsub: v.tensor_scalar(
                    mcol[:, 2 + qsub:3 + qsub], mcol[:, qsub:qsub + 1], -1.0, None, ALU.mult),
                    deps=[a2])
                a4 = c.op("scalar", lambda s, qsub=qsub: s.activation(
                    Eq[:, qsub * 512:(qsub + 1) * 512], scq[:, qsub * 512:(qsub + 1) * 512],
                    AF.Exp, bias=mcol[:, 2 + qsub:3 + qsub],
                    accum_out=zcol[:, qsub:qsub + 1]), deps=[a3, c.gtok("Ptensor")])
                mmt = None
                for kt in range(4):
                    src = Eq[:, qsub * 512 + kt * 128:qsub * 512 + kt * 128 + 128]
                    c.raw("tensor", lambda t, src=src: t.ldweights(src),
                          deps=[a4, idt, c.gtok("Pscalar")])
                    mmt = c.op("tensor", lambda t, src=src, kt=kt: t.matmul(
                        PT[:, kt * 128:(kt + 1) * 128], src, idb[:], start=True, stop=True,
                        is_transpose=True))
                et = c.op("scalar", lambda s, qsub=qsub: s.activation(
                    ET[:, qsub * 512:(qsub + 1) * 512], PT[:], AF.Copy, bias=0.0), deps=[mmt])
                mva = None
                for kt in range(4):
                    lhs = ET[:, qsub * 512 + kt * 128:qsub * 512 + kt * 128 + 128]
                    c.raw("tensor", lambda t, lhs=lhs: t.ldweights(lhs), deps=[et])
                    mva = c.op("tensor", lambda t, lhs=lhs, kt=kt, qsub=qsub, h=h, vwin=vwin:
                               t.matmul(
                        PB[:, qsub * 512:qsub * 512 + 64], lhs,
                        vwin[:, kt * 1024 + h * 64:kt * 1024 + h * 64 + 64],
                        start=(kt == 0), stop=(kt == 3)))
                rz = c.op("vector", lambda v, qsub=qsub: v.reciprocal(
                    zcol[:, 2 + qsub:3 + qsub], zcol[:, qsub:qsub + 1]), deps=[a4])
                prevA.append(c.op("vector", lambda v, qsub=qsub, h=h: v.tensor_scalar(
                    atb[:, qsub * 1024 + h * 64:qsub * 1024 + h * 64 + 64],
                    PB[:, qsub * 512:qsub * 512 + 64], zcol[:, 2 + qsub:3 + qsub], None,
                    ALU.mult), deps=[mva, rz, c.gtok("sat")]))
        jmm[j] = c.gtok("Ptensor")
        row0 = 256 * j - 384
        for qsub in range(2):
            if j == 1 and qsub == 0:
                continue
            c.dma("sync", mkap(at_h[0:1, 0:1, 0:1], (row0 + qsub * 128) * 128,
                               [[128, 128], [ATN * 128, 8], [1, 128]]),
                  atb[:, qsub * 1024:(qsub + 1) * 1024].rearrange("p (cc n) -> p cc n", n=128),
                  deps=prevA, group="sat")
    c.barrier()
    c.sbuf_free("qwin2", "kwin2", "vwin2", "scq", "Eq", "ET", "mcol", "zcol", "atb")

    atcm = c.sbuf("atcm", [128, 8 * ATN], BF16)
    wos = c.sbuf("wos", [128, 8 * 1024], BF16)
    x2t = c.sbuf("x2t", [128, 512], F32)
    xrt = c.sbuf("xrt", [128, 512], F32)
    for cc2 in range(8):
        c.dma("sync", atcm[:, cc2 * ATN:(cc2 + 1) * ATN], at_h[cc2].rearrange("t p -> t p"),
              transpose=True, group="lat")
        c.dma("gpsimd", wos[:, cc2 * 1024:(cc2 + 1) * 1024], wo[cc2 * 128:(cc2 + 1) * 128, :],
              group="lwo")
    lat = [c.gtok("lat"), c.gtok("lwo")]
    PBANK = [PA[:, 0:512], PA[:, 512:1024], PA[:, 1024:1536], PA[:, 1536:2048],
             PB[:, 0:512], PB[:, 512:1024]]
    chunks = [(i * 512, 512) for i in range(ATN // 512)]
    if ATN % 512:
        chunks.append(((ATN // 512) * 512, ATN % 512))
    for (ts, tw) in chunks:
        c.barrier()
        mm = None
        etoks = []
        for occ in range(8):
            pb = PBANK[occ % 6]
            for cc in range(8):
                wst = wos[:, cc * 1024 + occ * 128:cc * 1024 + occ * 128 + 128]
                c.raw("tensor", lambda t, wst=wst: t.ldweights(wst),
                      deps=lat + ([etoks[occ - 6]] if occ >= 6 and cc == 0 else []))
                mm = c.op("tensor", lambda t, wst=wst, pb=pb, cc=cc, ts=ts, tw=tw: t.matmul(
                    pb[:, 0:tw], wst, atcm[:, cc * ATN + ts:cc * ATN + ts + tw],
                    start=(cc == 0), stop=(cc == 7)))
            lo = max(ts, 127)
            sh = lo - ts
            vw = ts + tw - lo
            c.dma("sync", xrt[:, 0:tw], xa[occ * 128:(occ + 1) * 128, ts + 385:ts + 385 + tw],
                  deps=[c.gtok("Pvector")], group="lxr")
            e1 = c.op("vector", lambda v, pb=pb, tw=tw: v.tensor_tensor(
                x2t[:, 0:tw], pb[:, 0:tw], xrt[:, 0:tw], ALU.add),
                deps=[mm, c.gtok("lxr"), c.gtok("sx2")])
            if ts == 0:
                e1 = c.op("vector", lambda v: v.tensor_scalar(
                    x2t[:, 127:128], x2t[:, 127:128], role_sb[:, 0:1], None, ALU.mult),
                    deps=[e1] + lwm)
            etoks.append(e1)
            c.dma("sync", x2_h[occ * 128:(occ + 1) * 128, lo - 127:lo - 127 + vw],
                  x2t[:, sh:sh + vw], deps=[e1], group="sx2")
    c.barrier()
    c.sbuf_free("atcm", "wos", "x2t", "xrt")

    w1as = c.sbuf("w1as", [128, 8 * FH2], BF16)
    w1gb = c.sbuf("w1gb", [128, 2 * 1024], BF16)
    w2s = c.sbuf("w2s", [128, NHC * 1024], BF16)
    bcols = c.sbuf("bcols", [128, 2 * NHC + 24], F32)
    xcc = c.sbuf("xcc", [128, 8 * 512], F32)
    xov4 = c.sbuf("xov4", [128, 4 * 512], F32)
    sqq = c.sbuf("sqq", [128, 8 * 512], BF16)
    xnb = c.sbuf("xnb", [128, 8 * 512], BF16)
    hbuf = c.sbuf("hbuf", [128, NHC * 512], BF16)
    sc1 = c.sbuf("sc1", [128, 512], F32)
    sc2 = c.sbuf("sc2", [128, 512], F32)
    yrow = c.sbuf("yrow", [1, 512], F32)
    y2row = c.sbuf("y2row", [1, 512], F32)
    myb = c.sbuf("myb", [128, 512], F32)
    ryb = c.sbuf("ryb", [128, 512], F32)
    xot = c.sbuf("xot", [128, 512], F32)
    for cc in range(8):
        c.dma("gpsimd", w1as[:, cc * FH2:(cc + 1) * FH2], w1a[cc * 128:(cc + 1) * 128, :],
              group="lw1")
    c.dma("sync", w2s[:].rearrange("p (hc n) -> p hc n", n=1024),
          w2_h.rearrange("(hc p) n -> p hc n", p=128), deps=[c.gtok("wcast")], group="lw2s")
    c.dma("sync", bcols[:, 0:NHC], b1a.rearrange("(hc p) -> p hc", p=128), group="lb", slow=True)
    c.dma("sync", bcols[:, NHC:2 * NHC], b1g.rearrange("(hc p) -> p hc", p=128), group="lb",
          slow=True)
    c.dma("sync", bcols[:, 2 * NHC:2 * NHC + 8], b2.rearrange("(cc p) -> p cc", p=128),
          group="lb", slow=True)
    c.dma("sync", bcols[:, 2 * NHC + 8:2 * NHC + 16], lng.rearrange("(cc p) -> p cc", p=128),
          group="lb", slow=True)
    c.dma("sync", bcols[:, 2 * NHC + 16:2 * NHC + 24], lnb.rearrange("(cc p) -> p cc", p=128),
          group="lb", slow=True)
    lb = [c.gtok("lb"), c.gtok("lw1"), c.gtok("wcast")]
    lw2s = c.gtok("lw2s")

    def ln_stats(deps):
        st1 = None
        st2 = None
        for cc in range(8):
            st1 = c.op("tensor", lambda t, cc=cc: t.matmul(
                PB[0:1, 0:512], onesc[:], xcc[:, cc * 512:(cc + 1) * 512],
                start=(cc == 0), stop=(cc == 7)), deps=deps + [t_o2] + ([st1] if st1 else []))
            st2 = c.op("tensor", lambda t, cc=cc: t.matmul(
                PB[0:1, 512:1024], onescb[:], sqq[:, cc * 512:(cc + 1) * 512],
                start=(cc == 0), stop=(cc == 7)), deps=deps + [t_o3] + ([st2] if st2 else []))
        e1 = c.op("vector", lambda v: v.tensor_scalar(y2row[:], PB[0:1, 512:1024], 1.0 / D, None,
                                                      ALU.mult), deps=[st2])
        e2 = c.op("vector", lambda v: v.tensor_scalar(yrow[:], PB[0:1, 0:512], 1.0 / D, None,
                                                      ALU.mult), deps=[st1])
        e3 = c.op("vector", lambda v: v.tensor_tensor(myb[0:1, 0:512], yrow[:], yrow[:], ALU.mult),
                  deps=[e2])
        e4 = c.op("vector", lambda v: v.tensor_tensor(y2row[:], y2row[:], myb[0:1, 0:512],
                                                      ALU.subtract), deps=[e1, e3])
        e5 = c.op("scalar", lambda s: s.activation(y2row[:], y2row[:], AF.Sqrt,
                                                   bias=epsc[0:1, 0:1]), deps=[e4, epst])
        e6 = c.op("vector", lambda v: v.reciprocal(y2row[:], y2row[:]), deps=[e5])
        b3 = c.op("tensor", lambda t: t.matmul(PB[:, 0:512], onesr[:], yrow[:], start=True,
                                               stop=True), deps=[e6, t_o1])
        b3c = c.op("vector", lambda v: v.tensor_copy(myb[:], PB[:, 0:512]), deps=[b3])
        b4 = c.op("tensor", lambda t: t.matmul(PB[:, 0:512], onesr[:], y2row[:], start=True,
                                               stop=True), deps=[b3c])
        b4c = c.op("vector", lambda v: v.tensor_copy(ryb[:], PB[:, 0:512]), deps=[b4])
        return [b3c, b4c]

    for qcol in range(Tn // 512):
        c.barrier()
        qs = qcol * 512
        lds = []
        for cc in range(8):
            off = (1 + qs) if cc < 4 else qs
            c.dma("sync", xcc[:, cc * 512:(cc + 1) * 512],
                  x2_h[cc * 128:(cc + 1) * 128, off:off + 512], group="lx2")
            if cc >= 4:
                c.dma("sync", xov4[:, (cc - 4) * 512:(cc - 3) * 512],
                      x2_h[cc * 128:(cc + 1) * 128, 1 + qs:1 + qs + 512], group="lx2")
        lx2 = [c.gtok("lx2")]
        sqs = []
        for cc in range(8):
            sqs.append(c.op("scalar", lambda s, cc=cc: s.activation(
                sqq[:, cc * 512:(cc + 1) * 512], xcc[:, cc * 512:(cc + 1) * 512], AF.Square),
                deps=lx2))
        bt = ln_stats(lx2 + sqs)
        yns = []
        for cc in range(8):
            n1 = c.op("vector", lambda v, cc=cc: v.tensor_tensor(
                sc1[:, 0:512], xcc[:, cc * 512:(cc + 1) * 512], myb[:], ALU.subtract),
                deps=bt + yns[-1:])
            n2 = c.op("vector", lambda v: v.tensor_tensor(sc1[:, 0:512], sc1[:, 0:512], ryb[:],
                                                          ALU.mult), deps=[n1])
            yns.append(c.op("vector", lambda v, cc=cc: v.tensor_scalar(
                xnb[:, cc * 512:(cc + 1) * 512], sc1[:, 0:512],
                bcols[:, 2 * NHC + 8 + cc:2 * NHC + 9 + cc],
                bcols[:, 2 * NHC + 16 + cc:2 * NHC + 17 + cc], ALU.mult, ALU.add),
                deps=[n2] + lb))
        hts = []
        hgmm = []
        for hc in range(NHC):
            par = hc % 2
            gld = c.dma("sync", w1gb[:, par * 1024:(par + 1) * 1024].rearrange(
                "p (cc n) -> p cc n", n=128),
                mkap(w1g_h[0:1, 0:1], hc * 128, [[FH2, 128], [128 * FH2, 8], [1, 128]]),
                deps=[hgmm[hc - 2]] if hc >= 2 else [], group="hg%d" % par)
            mm = None
            for cc in range(8):
                wst = w1as[:, cc * FH2 + hc * 128:cc * FH2 + hc * 128 + 128]
                c.raw("tensor", lambda t, wst=wst: t.ldweights(wst),
                      deps=(yns + lb + hts[-2:]) if cc == 0 else [])
                mm = c.op("tensor", lambda t, wst=wst, cc=cc: t.matmul(
                    PA[:, 0:512], wst, xnb[:, cc * 512:(cc + 1) * 512],
                    start=(cc == 0), stop=(cc == 7)))
            for cc in range(8):
                wst = w1gb[:, par * 1024 + cc * 128:par * 1024 + cc * 128 + 128]
                c.raw("tensor", lambda t, wst=wst: t.ldweights(wst),
                      deps=[gld] if cc == 0 else [])
                mm = c.op("tensor", lambda t, wst=wst, cc=cc: t.matmul(
                    PA[:, 512:1024], wst, xnb[:, cc * 512:(cc + 1) * 512],
                    start=(cc == 0), stop=(cc == 7)))
            g1 = c.op("scalar", lambda s, hc=hc: s.activation(
                sc2[:, 0:512], PA[:, 512:1024], AF.Gelu, bias=bcols[:, NHC + hc:NHC + hc + 1]),
                deps=[mm] + lb)
            g2 = c.op("vector", lambda v, hc=hc: v.tensor_scalar(
                sc1[:, 0:512], PA[:, 0:512], bcols[:, hc:hc + 1], None, ALU.add),
                deps=[mm] + lb)
            hgmm.append(mm)
            hts.append(c.op("gpsimd", lambda g, hc=hc: g.tensor_tensor(
                hbuf[:, hc * 512:(hc + 1) * 512], sc1[:, 0:512], sc2[:, 0:512], ALU.mult),
                deps=[g1, g2]))
        mm2 = None
        for hc in range(NHC):
            for occ in range(6):
                wst = w2s[:, hc * 1024 + occ * 128:hc * 1024 + occ * 128 + 128]
                c.raw("tensor", lambda t, wst=wst: t.ldweights(wst),
                      deps=[lw2s, hts[-1], c.gtok("Pscalar"), c.gtok("Pvector")]
                      if (hc == 0 and occ == 0) else [])
                mm2 = c.op("tensor", lambda t, wst=wst, occ=occ, hc=hc: t.matmul(
                    PA[:, occ * 512:(occ + 1) * 512] if occ < 4
                    else PB[:, (occ - 4) * 512:(occ - 3) * 512], wst,
                    hbuf[:, hc * 512:(hc + 1) * 512], start=(hc == 0), stop=(hc == NHC - 1)))
        sto = []
        for occ in range(8):
            if occ >= 6:
                continue
            pb = PA[:, occ * 512:(occ + 1) * 512] if occ < 4 else PB[:, (occ - 4) * 512:(occ - 3) * 512]
            res = xcc[:, occ * 512:(occ + 1) * 512] if occ < 4 else xov4[:, (occ - 4) * 512:(occ - 3) * 512]
            u1 = c.op("vector", lambda v, pb=pb, occ=occ: v.tensor_scalar(
                xot[:], pb, bcols[:, 2 * NHC + occ:2 * NHC + occ + 1], None, ALU.add),
                deps=[mm2] + sto + lb + [c.gtok("sxo")])
            u2 = c.op("vector", lambda v, res=res: v.tensor_tensor(xot[:], xot[:], res, ALU.add),
                      deps=[u1])
            sto = [u2]
            c.dma("gpsimd", xo2[occ * 128:(occ + 1) * 128, qs:qs + 512], xot[:], deps=[u2],
                  group="sxo")
        c.barrier()
        mm3 = None
        for hc in range(NHC):
            for oi, occ in enumerate((6, 7)):
                wst = w2s[:, hc * 1024 + occ * 128:hc * 1024 + occ * 128 + 128]
                c.raw("tensor", lambda t, wst=wst: t.ldweights(wst), deps=[])
                mm3 = c.op("tensor", lambda t, wst=wst, oi=oi, hc=hc: t.matmul(
                    PA[:, oi * 512:(oi + 1) * 512], wst, hbuf[:, hc * 512:(hc + 1) * 512],
                    start=(hc == 0), stop=(hc == NHC - 1)))
        sto = []
        for oi, occ in enumerate((6, 7)):
            pb = PA[:, oi * 512:(oi + 1) * 512]
            res = xov4[:, (occ - 4) * 512:(occ - 3) * 512]
            u1 = c.op("vector", lambda v, pb=pb, occ=occ: v.tensor_scalar(
                xot[:], pb, bcols[:, 2 * NHC + occ:2 * NHC + occ + 1], None, ALU.add),
                deps=[mm3] + sto + lb + [c.gtok("sxo")])
            u2 = c.op("vector", lambda v, res=res: v.tensor_tensor(xot[:], xot[:], res, ALU.add),
                      deps=[u1])
            sto = [u2]
            c.dma("gpsimd", xo2[occ * 128:(occ + 1) * 128, qs:qs + 512], xot[:], deps=[u2],
                  group="sxo")
    c.wait_all("gpsimd", [c.gtok("sxo")])
    return c.finish()


def host_l1f_tables(h, Tn, qs64, ks64, n=8192):
    KV = 512 + Tn
    own0 = h * Tn
    p = own0 - 512 + np.arange(KV, dtype=np.float64)
    half = np.arange(0, 64, 2, dtype=np.float64)
    invf = 1.0 / 10000.0 ** (half / 64.0)
    f = p[:, None] * invf[None, :]
    freqs = np.concatenate([f, f], 1)
    cos, sin = np.cos(freqs), np.sin(freqs)
    bs = (half + 0.4 * 64) / (1.4 * 64)
    power = (p - n // 2) / (256 // 2)
    sc = bs[None, :] ** power[:, None]
    scale = np.concatenate([sc, sc], 1)
    sign = np.concatenate([-np.ones(32), np.ones(32)])[None, :]
    roll = np.concatenate([np.arange(32, 64), np.arange(0, 32)]).astype(int)
    q64 = qs64.astype(np.float64)
    k64 = ks64.astype(np.float64)
    tabq = np.stack([q64[None, :] * cos * scale,
                     q64[roll][None, :] * sin * scale * sign]).astype(np.float32)
    tabk = np.stack([k64[None, :] * cos / scale,
                     k64[roll][None, :] * sin / scale * sign]).astype(np.float32)
    return tabq, tabk


def host_l1f_masks(h):
    qr = np.arange(256)[:, None]
    kr = np.arange(512)[None, :]
    causal = (qr <= kr) & (kr <= qr + 256)
    gen = np.where(causal, 0.0, -1e30).astype(np.float32)
    first = np.where(causal & (kr >= 256), 0.0, -1e30).astype(np.float32)
    zero = np.zeros((256, 512), np.float32)
    if h == 0:
        return np.ascontiguousarray(np.stack([zero, first, gen]))
    return np.ascontiguousarray(np.stack([gen, gen, gen]))



def build_glue(Tn=T, add_base=True, emit_bf16=False):
    c = Ctx("glue")
    y = c.inp("y", [D, Tn])
    x = c.inp("x", [D, Tn]) if add_base else None
    xo = c.outp("xo", [D, Tn])
    ybf = c.outp("ybf", [D, Tn], BF16) if emit_bf16 else None
    onesr = c.sbuf("onesr", [1, 128], F32, side="right")
    onesc = c.sbuf("onesc", [128, 1], F32, side="right")
    onescb = c.sbuf("onescb", [128, 1], BF16, side="right")
    t_o1 = c.op("gpsimd", lambda g: g.memset(onesr[:], 1.0))
    t_o2 = c.op("gpsimd", lambda g: g.memset(onesc[:], 1.0))
    t_o3 = c.op("gpsimd", lambda g: g.memset(onescb[:], 1.0))
    epsc, epst = c.constcol(EPS)
    PB = c.psum("PB", [128, 1024])
    ycc = c.sbuf("ycc", [128, 8 * 512], F32)
    xcc = c.sbuf("xcc", [128, 8 * 512], F32)
    sqq = c.sbuf("sqq", [128, 8 * 512], BF16)
    yrow = c.sbuf("yrow", [1, 512], F32)
    y2row = c.sbuf("y2row", [1, 512], F32)
    myb = c.sbuf("myb", [128, 512], F32)
    ryb = c.sbuf("ryb", [128, 512], F32)
    xot = c.sbuf("xot", [128, 512], F32)

    for qcol in range(Tn // 512):
        c.barrier()
        qs = qcol * 512
        c.dma("sync", ycc[:].rearrange("p (cc n) -> p cc n", n=512),
              y[:, qs:qs + 512].rearrange("(cc p) n -> p cc n", p=128), group="ly")
        if add_base:
            c.dma("sync", xcc[:].rearrange("p (cc n) -> p cc n", n=512),
                  x[:, qs:qs + 512].rearrange("(cc p) n -> p cc n", p=128), group="ly")
        ly = [c.gtok("ly")]
        sqs = []
        for cc in range(8):
            sqs.append(c.op("scalar", lambda s, cc=cc: s.activation(
                sqq[:, cc * 512:(cc + 1) * 512], ycc[:, cc * 512:(cc + 1) * 512], AF.Square),
                deps=ly))
        st1 = None
        st2 = None
        for cc in range(8):
            st1 = c.op("tensor", lambda t, cc=cc: t.matmul(
                PB[0:1, 0:512], onesc[:], ycc[:, cc * 512:(cc + 1) * 512],
                start=(cc == 0), stop=(cc == 7)), deps=ly + [t_o2] + ([st1] if st1 else []))
            st2 = c.op("tensor", lambda t, cc=cc: t.matmul(
                PB[0:1, 512:1024], onescb[:], sqq[:, cc * 512:(cc + 1) * 512],
                start=(cc == 0), stop=(cc == 7)),
                deps=[sqs[cc], t_o3] + ([st2] if st2 else []))
        e1 = c.op("vector", lambda v: v.tensor_scalar(y2row[:], PB[0:1, 512:1024], 1.0 / D, None,
                                                      ALU.mult), deps=[st2])
        e2 = c.op("vector", lambda v: v.tensor_scalar(yrow[:], PB[0:1, 0:512], 1.0 / D, None,
                                                      ALU.mult), deps=[st1])
        e3 = c.op("vector", lambda v: v.tensor_tensor(myb[0:1, 0:512], yrow[:], yrow[:], ALU.mult),
                  deps=[e2])
        e4 = c.op("vector", lambda v: v.tensor_tensor(y2row[:], y2row[:], myb[0:1, 0:512],
                                                      ALU.subtract), deps=[e1, e3])
        e5 = c.op("scalar", lambda s: s.activation(y2row[:], y2row[:], AF.Sqrt,
                                                   bias=epsc[0:1, 0:1]), deps=[e4, epst])
        e6 = c.op("vector", lambda v: v.reciprocal(y2row[:], y2row[:]), deps=[e5])
        b3 = c.op("tensor", lambda t: t.matmul(PB[:, 0:512], onesr[:], yrow[:], start=True,
                                               stop=True), deps=[e6, t_o1])
        b3c = c.op("vector", lambda v: v.tensor_copy(myb[:], PB[:, 0:512]), deps=[b3])
        b4 = c.op("tensor", lambda t: t.matmul(PB[:, 0:512], onesr[:], y2row[:], start=True,
                                               stop=True), deps=[b3c])
        b4c = c.op("vector", lambda v: v.tensor_copy(ryb[:], PB[:, 0:512]), deps=[b4])
        sto = []
        for cc in range(8):
            u1 = c.op("vector", lambda v, cc=cc: v.tensor_tensor(
                xot[:], ycc[:, cc * 512:(cc + 1) * 512], myb[:], ALU.subtract),
                deps=[b3c, b4c] + sto + [c.gtok("sxo")])
            u2 = c.op("vector", lambda v: v.tensor_tensor(xot[:], xot[:], ryb[:], ALU.mult),
                      deps=[u1])
            if add_base:
                u2 = c.op("vector", lambda v, cc=cc: v.tensor_tensor(
                    xot[:], xot[:], xcc[:, cc * 512:(cc + 1) * 512], ALU.add), deps=[u2])
            sto = [u2]
            c.dma("gpsimd", xo[cc * 128:(cc + 1) * 128, qs:qs + 512], xot[:], deps=[u2],
                  group="sxo")
            if emit_bf16:
                c.dma("gpsimd", ybf[cc * 128:(cc + 1) * 128, qs:qs + 512], xot[:], deps=[u2],
                      group="sxo")
    c.wait_all("gpsimd", [c.gtok("sxo")])
    return c.finish()


# version 03 (final)
# speedup vs baseline: 1.0000x; 1.0000x over previous
import contextlib
import numpy as np
import concourse.bass as bass
import concourse.mybir as mybir
from concourse.ap import AP
from concourse.bass_utils import run_bass_kernel_spmd

F32 = mybir.dt.float32
BF16 = mybir.dt.bfloat16
AF = mybir.ActivationFunctionType
ALU = mybir.AluOpType
AX = mybir.AxisListType

NCORES = 8
T = 4096
D = 1024
EPS = 1e-5

ENGINES = ("sync", "tensor", "vector", "scalar", "gpsimd")


def mkap(ap, offset_delta, pattern):
    return AP(ap.tensor, ap.offset + offset_delta, pattern)


class Ctx:

    def __init__(self, name="k"):
        self.nc = bass.Bass("TRN2", target_bir_lowering=False)
        self.name = name
        self.q = {e: [] for e in ENGINES}
        self._es = contextlib.ExitStack()
        self._sems = {}
        self._cnt = {}
        self._waited = {e: {} for e in ENGINES}
        self._block = None

    def dram(self, name, shape, dtype, kind="Internal"):
        return self.nc.dram_tensor(name, list(shape), dtype, kind=kind).ap()

    def inp(self, name, shape, dtype=F32):
        return self.dram(name, shape, dtype, kind="ExternalInput")

    def outp(self, name, shape, dtype=F32):
        return self.dram(name, shape, dtype, kind="ExternalOutput")

    def sbuf(self, name, shape, dtype, side="left"):
        if not hasattr(self, "_cms"):
            self._cms = {}
        cm = self.nc.sbuf_tensor(name, list(shape), dtype, side=side)
        h = cm.__enter__()
        self._cms[name] = (cm, side)
        return h

    def sbuf_free(self, *names):
        order = list(self._cms.keys())
        for n in sorted(names, key=lambda n: -order.index(n)):
            cm, _ = self._cms.pop(n)
            cm.__exit__(None, None, None)

    def psum(self, name, shape, dtype=F32):
        return self._es.enter_context(self.nc.psum_tensor(name, list(shape), dtype))

    def sem(self, name):
        if name not in self._sems:
            self._sems[name] = self._es.enter_context(self.nc.semaphore(name))
            self._cnt[name] = 0
        return self._sems[name]

    def _emit_waits(self, eng_name, deps):
        w = self._waited[eng_name]
        best = {}
        for (s, v) in deps:
            if v > best.get(s, 0):
                best[s] = v
        todo = []
        for s, v in best.items():
            if v > w.get(s, 0):
                w[s] = v
                todo.append((self._sems[s], v))
        if not todo:
            return None

        def f(eng):
            for (h, v) in todo:
                eng.wait_ge(h, v)
        return f

    def op(self, eng, fn, deps=(), n=1):
        sname = "P" + eng
        self.sem(sname)
        waits = self._emit_waits(eng, deps)
        self._cnt[sname] += 1
        tok = (sname, self._cnt[sname])
        sem = self._sems[sname]

        def f(engine):
            if waits:
                waits(engine)
            ins = fn(engine)
            assert ins is not None, f"op on {eng} returned None"
            ins.then_inc(sem, 1)
        self.q[eng].append(f)
        return tok

    def dma(self, qeng, out_ap, in_ap, deps=(), transpose=False, group=None, slow=False):
        sname = group if group is not None else "D" + qeng
        self.sem(sname)
        waits = self._emit_waits(qeng, deps)
        self._cnt[sname] += 16
        tok = (sname, self._cnt[sname])
        sem = self._sems[sname]

        def f(eng):
            if waits:
                waits(eng)
            if transpose:
                eng.dma_start(out=out_ap, in_=in_ap, transpose=True).then_inc(sem, 16)
            elif slow:
                eng.dma_start(out=out_ap, in_=in_ap,
                              allow_slow_non_contiguous=True).then_inc(sem, 16)
            else:
                eng.dma_start(out=out_ap, in_=in_ap).then_inc(sem, 16)
        self.q[qeng].append(f)
        return tok

    def constcol(self, value, name=None):
        key = ("cc", float(value))
        if not hasattr(self, "_consts"):
            self._consts = {}
        if key not in self._consts:
            nm = name or f"cc{len(self._consts)}"
            t = self.sbuf(nm, [128, 1], F32, side="right")
            tok = self.op("gpsimd", lambda g, t=t, value=value: g.memset(t[:], float(value)))
            self._consts[key] = (t, tok)
        return self._consts[key]

    def gtok(self, group):
        return (group, self._cnt.get(group, 0))

    def raw(self, eng, fn, deps=()):
        waits = self._emit_waits(eng, deps)

        def f(engine):
            if waits:
                waits(engine)
            fn(engine)
        self.q[eng].append(f)

    def wait_all(self, eng, toks):
        self.raw(eng, lambda e: None, deps=toks)

    def barrier(self):
        toks = [(s, v) for s, v in self._cnt.items() if v > 0]
        for e in ENGINES:
            self.wait_all(e, toks)

    def finish(self):
        if hasattr(self, "_cms"):
            for cm, _ in reversed(list(self._cms.values())):
                cm.__exit__(None, None, None)
            self._cms = {}
        blk = self._es.enter_context(self.nc.Block())
        for e in ENGINES:
            fns = self.q[e]
            if not fns:
                continue

            def runner(engine, fns=fns):
                for f in fns:
                    f(engine)
            getattr(blk, e)(runner)
        self._es.close()
        return self.nc


def ceil_div(a, b):
    return -(-a // b)



def shard_tm(x):
    return [np.ascontiguousarray(x[c // 2, (c % 2) * T:(c % 2 + 1) * T, :])
            for c in range(NCORES)]


def host_seg8():
    m = np.zeros((128, 8), np.float32)
    for h in range(8):
        m[16 * h:16 * h + 16, h] = 1.0
    import ml_dtypes
    return m.astype(ml_dtypes.bfloat16)


def _l1f_ins(c, prev, inputs, i):
    b, h = c // 2, c % 2
    if h == 0:
        halo = np.zeros((D, 513), np.float32)
    else:
        halo = prev[2 * b][:, T - 513:]
    xa = np.ascontiguousarray(np.concatenate([halo, prev[c]], axis=1))
    tabq, tabk = host_l1f_tables(h, T, inputs["L1_ffn_q_scale"][i],
                                 inputs["L1_ffn_k_scale"][i])
    W1 = inputs["L1_ffn_W1"][i]
    w1a = np.zeros((D, 2816), np.float32)
    w1a[:, :2730] = W1[:, :2730]
    w1g = np.zeros((D, 2816), np.float32)
    w1g[:, :2730] = W1[:, 2730:]
    w2 = np.zeros((2816, D), np.float32)
    w2[:2730] = inputs["L1_ffn_W2"][i]
    b1 = inputs["L1_ffn_b1"][i]
    b1a = np.zeros(2816, np.float32)
    b1a[:2730] = b1[:2730]
    b1g = np.zeros(2816, np.float32)
    b1g[:2730] = b1[2730:]
    return {"xa": xa, "wqkv": inputs["L1_ffn_Wqkv"][i], "wo": inputs["L1_ffn_Wo"][i],
            "lng": inputs["L1_ffn_ln_g"][i], "lnb": inputs["L1_ffn_ln_b"][i],
            "w1a": w1a, "w1g": w1g, "w2": w2, "b1a": b1a, "b1g": b1g,
            "b2": inputs["L1_ffn_b2"][i],
            "tabq": np.ascontiguousarray(tabq), "tabk": np.ascontiguousarray(tabk),
            "wmq": host_l1f_masks(h),
            "rolec": np.full((128, 1), 0.0 if h == 0 else 1.0, np.float32)}


def kernel(**inputs):
    import ml_dtypes
    BF = ml_dtypes.bfloat16
    cores = list(range(NCORES))
    Sn = 2 * T
    xs = shard_tm(inputs["L0_attn_x"])

    nc_prep = build_prep_ln()
    r = run_bass_kernel_spmd(nc_prep, [
        {"x": xs[c], "g": inputs["L0_attn_ln_g"], "b": inputs["L0_attn_ln_b"]}
        for c in cores], core_ids=cores)
    prep = r.results

    xcm = [np.ascontiguousarray(xs[c].T) for c in cores]

    def gau_in(c):
        b, h = c // 2, c % 2
        na_slot = np.concatenate([prep[2 * b + (1 - h)]["na"], prep[c]["na"]], 0)
        return {"na": np.ascontiguousarray(na_slot.T), "x": xcm[c],
                "mur": prep[c]["mu"][None, :], "r1r": prep[c]["r1"][None, :],
                "wh": inputs["L0_attn_Wh"], "bh": inputs["L0_attn_bh"],
                "wqk": inputs["L0_attn_Wqk"], "bqk": inputs["L0_attn_bqk"],
                "gamma": inputs["L0_attn_gamma"], "beta": inputs["L0_attn_beta"],
                "wo": inputs["L0_attn_Wo"], "bo": inputs["L0_attn_bo"],
                "mwide": host_l0_masks(h, Sn)}
    r = run_bass_kernel_spmd(build_l0_attn(), [gau_in(c) for c in cores], core_ids=cores)
    x1cm = [r.results[c]["xo"] for c in cores]
    x1tm = [np.ascontiguousarray(v.T) for v in x1cm]

    r = run_bass_kernel_spmd(nc_prep, [
        {"x": x1tm[c], "g": inputs["L0_ffn_ln1_w"], "b": inputs["L0_ffn_ln1_b"]}
        for c in cores], core_ids=cores)
    prep2 = r.results

    def moa_in(c):
        b, h = c // 2, c % 2
        na_g = np.concatenate([prep2[2 * b]["na"], prep2[2 * b + 1]["na"]], 0)
        order, bandm, maskK, rl, rr = host_moa_layout(h, T, Sn)
        SL = Sn + 128
        slot = np.zeros((SL, D), BF)
        col = 0
        for rng_ in order:
            if rng_ is None:
                col += 128
                continue
            a, bnd = rng_
            slot[col:col + (bnd - a)] = na_g[a:bnd]
            col += bnd - a
        return {"xn": np.ascontiguousarray(slot.T),
                "xnlo": np.ascontiguousarray(prep2[c]["nalo"].T),
                "x": x1cm[c],
                "mur": prep2[c]["mu"][None, :], "r1r": prep2[c]["r1"][None, :],
                "gate_w": inputs["L0_ffn_attn_gate_w"],
                "q_w": inputs["L0_ffn_q_w"],
                "q_b": np.ascontiguousarray(inputs["L0_ffn_q_b"].T),
                "k_w": inputs["L0_ffn_k_w"], "k_b": inputs["L0_ffn_k_b"],
                "v_w": inputs["L0_ffn_v_w"], "v_b": inputs["L0_ffn_v_b"],
                "o_w": inputs["L0_ffn_o_w"], "o_b": inputs["L0_ffn_o_b"],
                "rel_t": np.ascontiguousarray(inputs["L0_ffn_rel_table"].T),
                "bandm": bandm, "maskK": maskK, "roleL": rl, "roleR": rr}
    r = run_bass_kernel_spmd(build_moa(), [moa_in(c) for c in cores], core_ids=cores)
    x2cm = [r.results[c]["x2o"] for c in cores]

    r = run_bass_kernel_spmd(build_moe(fuse_glue=True), [
        {"x2": x2cm[c], "x": x1cm[c],
         "ln_w": inputs["L0_ffn_ln2_w"], "ln_b": inputs["L0_ffn_ln2_b"],
         "gate_w": inputs["L0_ffn_ff_gate_w"], "w1": inputs["L0_ffn_ff_w1"],
         "b1": inputs["L0_ffn_ff_b1"], "w2": inputs["L0_ffn_ff_w2"],
         "b2": inputs["L0_ffn_ff_b2"]} for c in cores], core_ids=cores)
    x2res = [r.results[c]["xo"] for c in cores]
    xi_bf = [r.results[c]["ybf"] for c in cores]

    maa = np.ascontiguousarray(np.stack(
        [inputs["L1_attn_time_maa_x"].reshape(D), inputs["L1_attn_time_maa_w"].reshape(D),
         inputs["L1_attn_time_maa_k"].reshape(D), inputs["L1_attn_time_maa_v"].reshape(D),
         inputs["L1_attn_time_maa_r"].reshape(D)], 0))
    seg8 = host_seg8()

    def rwkv_a_in(c):
        b, h = c // 2, c % 2
        halo = np.zeros((D, 1), BF) if h == 0 else xi_bf[2 * b][:, T - 1:T]
        xin = np.ascontiguousarray(np.concatenate([halo, xi_bf[c]], axis=1))
        return {"xin": xin, "maa": maa, "tmw1": inputs["L1_attn_time_maa_w1"],
                "tmw2": inputs["L1_attn_time_maa_w2"],
                "decay_u": inputs["L1_attn_time_decay"].reshape(D),
                "dw1": inputs["L1_attn_time_decay_w1"], "dw2": inputs["L1_attn_time_decay_w2"],
                "wr": inputs["L1_attn_W_r"], "wk": inputs["L1_attn_W_k"],
                "wv": inputs["L1_attn_W_v"],
                "u_in": np.ascontiguousarray(inputs["L1_attn_time_faaaa"].reshape(D)),
                "seg8i": seg8}
    r = run_bass_kernel_spmd(build_rwkv_a(), [rwkv_a_in(c) for c in cores], core_ids=cores)
    ra = r.results

    def rwkv_b_in(c):
        b, h = c // 2, c % 2
        sprev = np.zeros((D, 16), np.float32) if h == 0 else ra[2 * b]["se"]
        return {"y1": ra[c]["y1"], "rc": ra[c]["rc"], "sprev": sprev, "x": x2res[c],
                "ln_g": inputs["L1_attn_ln_g"], "ln_b": inputs["L1_attn_ln_b"],
                "wo": inputs["L1_attn_W_o"], "seg8i": seg8}
    r = run_bass_kernel_spmd(build_rwkv_b(fuse_glue=True), [rwkv_b_in(c) for c in cores],
                             core_ids=cores)
    x3cm = [r.results[c]["xo"] for c in cores]
    prev = [r.results[c]["gxo"] for c in cores]

    nc_l1f = build_l1f_layer()
    for i in range(6):
        r = run_bass_kernel_spmd(nc_l1f, [_l1f_ins(c, prev, inputs, i) for c in cores],
                                 core_ids=cores)
        prev = [r.results[c]["xo2"] for c in cores]

    r = run_bass_kernel_spmd(build_glue(add_base=True), [
        {"y": prev[c], "x": x3cm[c]} for c in cores], core_ids=cores)
    out = np.empty((4, Sn, D), np.float32)
    for c in cores:
        out[c // 2, (c % 2) * T:(c % 2 + 1) * T, :] = r.results[c]["xo"].T
    return out



def build_prep_ln(Tn=T, Dn=D):
    c = Ctx("prep_ln")
    x = c.inp("x", [Tn, Dn])
    g = c.inp("g", [Dn])
    b = c.inp("b", [Dn])
    na_o = c.outp("na", [Tn, Dn], BF16)
    nalo_o = c.outp("nalo", [Tn, Dn], BF16)
    mu_o = c.outp("mu", [Tn])
    r1_o = c.outp("r1", [Tn])

    nt = Tn // 128
    xt = c.sbuf("xt", [128, nt * Dn], F32)
    sq = c.sbuf("sq", [128, Dn], F32)
    mu = c.sbuf("mu_s", [128, nt], F32)
    ssq = c.sbuf("ssq", [128, nt], F32)
    var = c.sbuf("var", [128, nt], F32)
    r1 = c.sbuf("r1_s", [128, nt], F32)
    rho = c.sbuf("rho", [128, nt], F32)
    gb = c.sbuf("gb", [128, Dn], F32)
    bb = c.sbuf("bb", [128, Dn], F32)
    grow = c.sbuf("grow", [1, Dn], F32)
    brow = c.sbuf("brow", [1, Dn], F32)
    ones_row = c.sbuf("ones_row", [1, 128], F32)
    na_ab = [c.sbuf("na_a", [128, Dn], BF16), c.sbuf("na_b", [128, Dn], BF16)]
    lo_ab = [c.sbuf("lo_a", [128, Dn], BF16), c.sbuf("lo_b", [128, Dn], BF16)]
    tmpf = c.sbuf("tmpf", [128, Dn], F32)
    tmp = c.sbuf("tmp", [128, Dn], F32)
    psb = [c.psum(f"psb{i}", [128, 512]) for i in range(2)]

    c.dma("sync", grow[:], g.rearrange("(one d) -> one d", one=1), group="gi")
    c.dma("sync", brow[:], b.rearrange("(one d) -> one d", one=1), group="gi")
    c.dma("sync", xt[:].rearrange("p (n d) -> p n d", d=Dn), x.rearrange("(n p) d -> p n d", p=128), group="gi")
    t_one = c.op("gpsimd", lambda e: e.memset(ones_row[:], 1.0))
    gi = c.gtok("gi")
    bco = []
    prev = [t_one, gi]
    for j in range(Dn // 512):
        tk1 = c.op("tensor", lambda t, j=j: t.matmul(psb[0][:, :], ones_row[:], grow[:, j * 512:(j + 1) * 512],
                                                     start=True, stop=True), deps=prev)
        v1 = c.op("vector", lambda v, j=j: v.tensor_copy(gb[:, j * 512:(j + 1) * 512], psb[0][:]), deps=[tk1])
        tk2 = c.op("tensor", lambda t, j=j: t.matmul(psb[1][:, :], ones_row[:], brow[:, j * 512:(j + 1) * 512],
                                                     start=True, stop=True), deps=prev)
        v2 = c.op("vector", lambda v, j=j: v.tensor_copy(bb[:, j * 512:(j + 1) * 512], psb[1][:]), deps=[tk2])
        prev = [v1, v2, gi, t_one]
        bco = [v1, v2]

    st_toks = []
    prev_sq = None
    prev_w = []
    for i in range(nt):
        xi = xt[:, i * Dn:(i + 1) * Dn]
        rs = c.op("vector", lambda v, xi=xi, i=i: v.reduce_sum(mu[:, i:i + 1], xi, axis=AX.X), deps=[gi])
        sm = c.op("vector", lambda v, i=i: v.tensor_scalar(mu[:, i:i + 1], mu[:, i:i + 1], 1.0 / Dn, None, ALU.mult), deps=[rs])
        qq = c.op("scalar", lambda s, xi=xi, i=i: s.activation(sq[:], xi, AF.Square, accum_out=ssq[:, i:i + 1]),
                  deps=[gi] + ([prev_sq] if prev_sq else []))
        prev_sq = qq
        v1 = c.op("vector", lambda v, i=i: v.tensor_tensor(var[:, i:i + 1], mu[:, i:i + 1], mu[:, i:i + 1], ALU.mult), deps=[sm])
        v2 = c.op("vector", lambda v, i=i: v.scalar_tensor_tensor(var[:, i:i + 1], ssq[:, i:i + 1], 1.0 / Dn,
                                                                  var[:, i:i + 1], ALU.mult, ALU.subtract), deps=[qq, v1])
        epsc, epst = c.constcol(EPS)
        eps2c, eps2t = c.constcol(EPS * (1.0 + EPS))
        s1 = c.op("scalar", lambda s, i=i, epsc=epsc: s.activation(r1[:, i:i + 1], var[:, i:i + 1], AF.Sqrt, bias=epsc[:, 0:1]), deps=[v2, epst])
        s2 = c.op("vector", lambda v, i=i: v.reciprocal(r1[:, i:i + 1], r1[:, i:i + 1]), deps=[s1])
        s3 = c.op("scalar", lambda s, i=i, eps2c=eps2c: s.activation(rho[:, i:i + 1], var[:, i:i + 1], AF.Sqrt,
                                                        scale=float(1.0 + EPS), bias=eps2c[:, 0:1]), deps=[v2, eps2t])
        s4 = c.op("vector", lambda v, i=i: v.reciprocal(rho[:, i:i + 1], rho[:, i:i + 1]), deps=[s3])
        w1 = c.op("vector", lambda v, xi=xi, i=i: v.tensor_scalar(tmp[:], xi, mu[:, i:i + 1], rho[:, i:i + 1],
                                                                  ALU.subtract, ALU.mult), deps=[s4, sm] + prev_w)
        w2 = c.op("vector", lambda v: v.tensor_tensor(tmp[:], tmp[:], gb[:], ALU.mult), deps=[w1] + bco)
        nai = na_ab[i % 2]
        loi = lo_ab[i % 2]
        grp = "stA" if i % 2 == 0 else "stB"
        war = [c.gtok(grp)] if i >= 2 else []
        w3f = c.op("vector", lambda v: v.tensor_tensor(tmpf[:], tmp[:], bb[:], ALU.add), deps=[w2] + war + prev_w)
        w3 = c.op("vector", lambda v, nai=nai: v.tensor_copy(nai[:], tmpf[:]), deps=[w3f])
        w4 = c.op("vector", lambda v, nai=nai, loi=loi: v.tensor_tensor(loi[:], tmpf[:], nai[:], ALU.subtract),
                  deps=[w3])
        prev_w = [w4]
        c.dma("gpsimd", na_o[i * 128:(i + 1) * 128, :], nai[:], deps=[w3], group=grp)
        c.dma("gpsimd", nalo_o[i * 128:(i + 1) * 128, :], loi[:], deps=[w4], group=grp)
        st_toks.append((sm, s2, i))
    for (sm, s2, i) in st_toks:
        c.dma("gpsimd", mu_o[i * 128:(i + 1) * 128].rearrange("(p one) -> p one", one=1),
              mu[:, i:i + 1], deps=[sm], group="st")
        c.dma("gpsimd", r1_o[i * 128:(i + 1) * 128].rearrange("(p one) -> p one", one=1),
              r1[:, i:i + 1], deps=[s2], group="st")
    c.wait_all("gpsimd", [c.gtok("st"), c.gtok("stA"), c.gtok("stB")])
    return c.finish()



def build_l0_attn(Tn=T, Sn=2 * T, nglob=8192):
    HID = 2048
    Jn = 5
    NQQ = Tn // 512
    NKB = Sn // 128
    HS = Sn // 2
    assert HS == Tn

    c = Ctx("l0attn")
    na = c.inp("na", [D, Sn], BF16)
    x = c.inp("x", [D, Tn])
    mur = c.inp("mur", [1, Tn])
    r1r = c.inp("r1r", [1, Tn])
    wh = c.inp("wh", [D, 2 * HID])
    bh = c.inp("bh", [2 * HID])
    wqk = c.inp("wqk", [D, 128])
    bqk = c.inp("bqk", [128])
    gamma = c.inp("gamma", [2, 128])
    beta = c.inp("beta", [2, 128])
    wo = c.inp("wo", [HID, D])
    bo = c.inp("bo", [D])
    mwide = c.inp("mwide", [Jn, 128, 512], BF16)
    xo = c.outp("xo", [D, Tn])

    v_hbm = c.dram("v_hbm", [Sn, HID], BF16)
    g_hbm = c.dram("g_hbm", [Tn, HID], BF16)
    z_hbm = c.dram("z_hbm", [Tn, HID], BF16)

    na_sb = c.sbuf("na_sb", [128, 8 * HS], BF16)
    whv = c.sbuf("whv", [128, 8 * HID], BF16)
    bh_vb = c.sbuf("bh_vb", [128, HID], F32)
    bh_row = c.sbuf("bh_row", [1, 2 * HID], F32)
    onesr = c.sbuf("onesr", [1, 128], F32, side="right")
    onesc = c.sbuf("onesc", [128, 1], F32, side="right")
    sc1 = c.sbuf("sc1", [128, 2048], F32, side="right")
    sc2 = c.sbuf("sc2", [128, 2048], F32, side="right")
    wqk_sb = c.sbuf("wqk_sb", [128, 8 * 128], BF16)
    bqk_c = c.sbuf("bqk_c", [128, 1], F32)
    gb_c = c.sbuf("gb_c", [128, 4], F32)
    q_sb = c.sbuf("q_sb", [128, Sn], BF16, side="right")
    k_sb = c.sbuf("k_sb", [128, Sn], BF16, side="right")
    vt_sb = [c.sbuf(f"vt{i}", [128, HID], BF16) for i in range(2)]
    m_sb = c.sbuf("m_sb", [128, Jn * 512], BF16, side="right")
    PA = c.psum("PA", [128, 2048])
    PB = c.psum("PB", [128, 2048])
    ps_v = [PA, PB]
    ps_qk = PB[:, 0:512]
    ps_b = PB[:, 512:1024]

    t_o1 = c.op("gpsimd", lambda g: g.memset(onesr[:], 1.0))
    t_o2 = c.op("gpsimd", lambda g: g.memset(onesc[:], 1.0))

    c.dma("sync", bh_row[:], bh.rearrange("(one n) -> one n", one=1), group="small")
    c.dma("sync", bqk_c[:], bqk.rearrange("(p one) -> p one", one=1), group="small")
    c.dma("sync", gb_c[:, 0:1], gamma[0:1, :].rearrange("one p -> p one"), group="small")
    c.dma("sync", gb_c[:, 1:2], beta[0:1, :].rearrange("one p -> p one"), group="small")
    c.dma("sync", gb_c[:, 2:3], gamma[1:2, :].rearrange("one p -> p one"), group="small")
    c.dma("sync", gb_c[:, 3:4], beta[1:2, :].rearrange("one p -> p one"), group="small")
    c.dma("sync", m_sb[:].rearrange("p (j n) -> p j n", n=512), mwide.rearrange("j p n -> p j n"), group="small")
    for cc in range(8):
        c.dma("gpsimd", wqk_sb[:, cc * 128:(cc + 1) * 128], wqk[cc * 128:(cc + 1) * 128, :], group="small2")
    sm = [c.gtok("small"), c.gtok("small2")]

    def bh_bcast(part):
        toks = []
        for j in range(HID // 512):
            tk = c.op("tensor", lambda t, j=j, part=part: t.matmul(
                ps_b, onesr[:], bh_row[:, part * HID + j * 512: part * HID + (j + 1) * 512],
                start=True, stop=True), deps=[t_o1] + sm + toks)
            toks.append(c.op("vector", lambda v, j=j: v.tensor_copy(bh_vb[:, j * 512:(j + 1) * 512], ps_b),
                             deps=[tk]))
        return toks

    def phase_a_half(half, wsel, out_hbm, out_off, bh_dep, qk=False):
        c.barrier()
        c.dma("sync", na_sb[:].rearrange("p (cc t) -> p cc t", t=HS),
              na[:, half * HS:(half + 1) * HS].rearrange("(cc p) t -> p cc t", p=128),
              group="lna")
        lna = [c.gtok("lna")]
        c.dma("gpsimd", whv[:].rearrange("p (cc n) -> p cc n", n=HID),
              wh[:, wsel * HID:(wsel + 1) * HID].rearrange("(cc p) n -> p cc n", p=128),
              group="lwh")
        lw = [c.gtok("lwh")]
        prev_evac = {0: [], 1: []}
        for tt in range(HS // 128):
            pv = ps_v[tt % 2][:]
            mmdep = lna + lw + bh_dep + prev_evac[tt % 2]
            mm = None
            for cc in range(8):
                st = na_sb[:, cc * HS + tt * 128: cc * HS + (tt + 1) * 128]
                c.raw("tensor", lambda t, st=st: t.ldweights(st), deps=mmdep if cc == 0 else [])
                for nn in range(HID // 512):
                    mm = c.op("tensor", lambda t, st=st, cc=cc, nn=nn, pv=pv: t.matmul(
                        pv[:, nn * 512:(nn + 1) * 512], st,
                        whv[:, cc * HID + nn * 512: cc * HID + (nn + 1) * 512],
                        start=(cc == 0), stop=(cc == 7)))
            vt = vt_sb[tt % 2]
            scp = sc1 if tt % 2 == 0 else sc2
            e1 = c.op("vector", lambda v, pv=pv, scp=scp: v.tensor_tensor(scp[:, 0:HID], pv, bh_vb[:], ALU.add),
                      deps=[mm] + bh_dep + prev_evac[tt % 2])
            e2 = c.op("scalar", lambda s, vt=vt, scp=scp: s.activation(vt[:], scp[:, 0:HID], AF.Silu),
                      deps=[e1] + prev_evac[tt % 2])
            c.dma("gpsimd", out_hbm[out_off + tt * 128: out_off + (tt + 1) * 128, :], vt[:],
                  deps=[e2], group=f"vst{half}{wsel}{tt % 2}")
            prev_evac[tt % 2] = [e1, e2, c.gtok(f"vst{half}{wsel}{tt % 2}")]
        if qk:
            c.barrier()
            prev = []
            for tcol in range(HS // 512):
                mm = None
                for cc in range(8):
                    wst = wqk_sb[:, cc * 128:(cc + 1) * 128]
                    c.raw("tensor", lambda t, wst=wst: t.ldweights(wst),
                          deps=(lna + sm + prev) if cc == 0 else [])
                    mm = c.op("tensor", lambda t, wst=wst, cc=cc, tcol=tcol: t.matmul(
                        ps_qk, wst, na_sb[:, cc * HS + tcol * 512: cc * HS + (tcol + 1) * 512],
                        start=(cc == 0), stop=(cc == 7)))
                sil = c.op("scalar", lambda s: s.activation(sc2[:, 0:512], ps_qk, AF.Silu,
                                                            bias=bqk_c[:, 0:1]), deps=[mm] + prev + sm)
                col0 = half * HS + tcol * 512
                kaf = c.op("vector", lambda v, col0=col0: v.tensor_scalar(
                    k_sb[:, col0:col0 + 512], sc2[:, 0:512], gb_c[:, 2:3], gb_c[:, 3:4], ALU.mult, ALU.add),
                    deps=[sil] + sm)
                qaf = c.op("vector", lambda v, col0=col0: v.tensor_scalar(
                    q_sb[:, col0:col0 + 512], sc2[:, 0:512], gb_c[:, 0:1], gb_c[:, 1:2], ALU.mult, ALU.add),
                    deps=[sil])
                prev = [sil, kaf, qaf]

    bhd = bh_bcast(0)
    phase_a_half(0, 0, v_hbm, 0, bhd, qk=True)
    phase_a_half(1, 0, v_hbm, HS, bhd, qk=True)
    c.barrier()
    bhd_g = bh_bcast(1)
    phase_a_half(1, 1, g_hbm, 0, bhd_g)
    c.barrier()

    c.sbuf_free("na_sb", "whv", "bh_vb", "bh_row", "wqk_sb", "bqk_c", "gb_c", "vt0", "vt1")
    at_sb = c.sbuf("at_sb", [128, NKB * 512], BF16)
    vblk = [c.sbuf(f"vblk{i}", [128, 8 * 1024], BF16) for i in range(2)]
    gate_q = c.sbuf("gate_q", [128, 4 * HID], BF16)
    zin_t = c.sbuf("zin_t", [128, 1024], BF16)
    ps_s = PA[:, 0:512]
    ps_att = [PA[:, 0:1024], PA[:, 1024:2048], PB[:, 0:1024], PB[:, 1024:2048]]
    kb_end = lambda qq: Sn // 256 + 4 * qq + 4

    for qq in range(NQQ):
        c.barrier()
        prev_s = []
        at_toks = []
        for kb in range(kb_end(qq)):
            lhs = k_sb[:, kb * 128:(kb + 1) * 128]
            rhs = q_sb[:, HS + qq * 512: HS + (qq + 1) * 512]
            mm = c.op("tensor", lambda t, lhs=lhs, rhs=rhs: (
                t.ldweights(lhs), t.matmul(ps_s, lhs, rhs, start=True, stop=True))[1],
                deps=prev_s)
            rl = c.op("scalar", lambda s: s.activation(sc1[:, 0:512], ps_s, AF.Relu,
                                                       scale=float(1.0 / nglob)), deps=[mm] + prev_s)
            sq = c.op("vector", lambda v: v.tensor_tensor(sc2[:, 0:512], sc1[:, 0:512], sc1[:, 0:512],
                                                          ALU.mult), deps=[rl] + prev_s)
            if kb < Sn // 256:
                Jc = 4
            else:
                dp = kb - Sn // 256 - 4 * qq
                Jc = dp if dp >= 0 else None
            if Jc is None:
                atk = c.op("vector", lambda v, kb=kb: v.tensor_copy(
                    at_sb[:, kb * 512:(kb + 1) * 512], sc2[:, 0:512]), deps=[sq])
            else:
                atk = c.op("vector", lambda v, kb=kb, Jc=Jc: v.tensor_tensor(
                    at_sb[:, kb * 512:(kb + 1) * 512], sc2[:, 0:512], m_sb[:, Jc * 512:(Jc + 1) * 512],
                    ALU.mult), deps=[sq])
            prev_s = [rl, sq, atk]
            at_toks.append(atk)
        lg = c.dma("sync", gate_q[:].rearrange("p (n m) -> p n m", m=HID),
                   g_hbm[qq * 512:(qq + 1) * 512, :].rearrange("(n p) m -> p n m", p=128),
                   group="gq")
        lgt = [c.gtok("gq")]
        lvb = {0: [], 1: []}
        for hh in range(2):
            ke = kb_end(qq)
            nblk = ceil_div(ke, 8)
            first = True
            mmtok = None
            for kbb in range(nblk):
                kb0, kb1 = kbb * 8, min(kbb * 8 + 8, ke)
                vb = vblk[kbb % 2]
                c.dma("sync", vb[:, 0:(kb1 - kb0) * 1024].rearrange("p (n m) -> p n m", m=1024),
                      v_hbm[kb0 * 128:kb1 * 128, hh * 1024:(hh + 1) * 1024].rearrange("(n p) m -> p n m", p=128),
                      deps=lvb[kbb % 2], group=f"vb{kbb % 2}")
                lvt = [c.gtok(f"vb{kbb % 2}")]
                for kb in range(kb0, kb1):
                    for qi in range(4):
                        lhs = at_sb[:, kb * 512 + qi * 128: kb * 512 + (qi + 1) * 128]
                        deps = (at_toks + lvt) if (kb == kb0 and qi == 0) else []
                        c.raw("tensor", lambda t, lhs=lhs: t.ldweights(lhs), deps=deps)
                        for h2 in range(2):
                            mmtok = c.op("tensor", lambda t, lhs=lhs, vb=vb, kb=kb, kb0=kb0, h2=h2, qi=qi, first=first, last=(kb == ke - 1): t.matmul(
                                ps_att[qi][:, h2 * 512:(h2 + 1) * 512], lhs,
                                vb[:, (kb - kb0) * 1024 + h2 * 512: (kb - kb0) * 1024 + h2 * 512 + 512],
                                start=first, stop=last))
                    first = False
                lvb[kbb % 2] = [mmtok]
            evs = []
            for qi in range(4):
                ev = c.op("vector", lambda v, qi=qi, hh=hh: v.tensor_tensor(
                    zin_t[:], ps_att[qi], gate_q[:, qi * HID + hh * 1024: qi * HID + hh * 1024 + 1024],
                    ALU.mult), deps=[mmtok] + lgt + [c.gtok("zst")])
                c.dma("gpsimd", z_hbm[qq * 512 + qi * 128: qq * 512 + (qi + 1) * 128,
                                      hh * 1024:(hh + 1) * 1024], zin_t[:], deps=[ev], group="zst")
                evs.append(ev)
            c.wait_all("tensor", evs)
        c.wait_all("vector", [c.gtok("zst")])
    c.barrier()

    c.sbuf_free("at_sb", "vblk0", "vblk1", "gate_q", "zin_t", "m_sb", "k_sb", "q_sb")
    wo_sb = c.sbuf("wo_sb", [128, 16 * 1024], BF16)
    Tn2 = Tn // 2
    zc_sb = c.sbuf("zc_sb", [128, 16 * Tn2], BF16)
    z_sb = c.sbuf("z_sb", [128, 8 * Tn], BF16, side="right")
    QW = min(512, Tn // 2)
    ps_z = [PA[:, i * QW:(i + 1) * QW] for i in range(4)]
    for cc16 in range(16):
        c.dma("gpsimd", wo_sb[:, cc16 * 1024:(cc16 + 1) * 1024],
              wo[cc16 * 128:(cc16 + 1) * 128, :], group="wol")
    wol = [c.gtok("wol")]
    for qh in range(2):
        for hc in range(16):
            c.dma("sync", zc_sb[:, hc * Tn2:(hc + 1) * Tn2],
                  z_hbm[qh * Tn2:(qh + 1) * Tn2, hc * 128:(hc + 1) * 128],
                  transpose=True, group=f"zc{qh}")
        lzt = [c.gtok(f"zc{qh}")]
        zdone = []
        for occ in range(8):
            mm = None
            for hc in range(16):
                wst = wo_sb[:, hc * 1024 + occ * 128: hc * 1024 + occ * 128 + 128]
                c.raw("tensor", lambda t, wst=wst: t.ldweights(wst),
                      deps=(lzt + wol + zdone) if hc == 0 else [])
                for qcol in range(Tn2 // QW):
                    mm = c.op("tensor", lambda t, wst=wst, hc=hc, qcol=qcol: t.matmul(
                        ps_z[qcol], wst, zc_sb[:, hc * Tn2 + qcol * QW: hc * Tn2 + (qcol + 1) * QW],
                        start=(hc == 0), stop=(hc == 15)))
            zdone = []
            for qcol in range(Tn2 // QW):
                zdone.append(c.op("vector", lambda v, occ=occ, qh=qh, qcol=qcol: v.tensor_copy(
                    z_sb[:, occ * Tn + qh * Tn2 + qcol * QW: occ * Tn + qh * Tn2 + qcol * QW + QW],
                    ps_z[qcol]), deps=[mm]))
        c.barrier()

    c.sbuf_free("zc_sb", "wo_sb")
    bo_c = c.sbuf("bo_c", [128, 8], F32)
    mu_row = c.sbuf("mu_rowD", [1, Tn], F32)
    r1_row = c.sbuf("r1_rowD", [1, Tn], F32)
    yrow = c.sbuf("yrow", [1, 512], F32)
    y2row = c.sbuf("y2row", [1, 512], F32)
    mb = c.sbuf("mb", [128, 512], F32)
    rb = c.sbuf("rb", [128, 512], F32)
    myb = c.sbuf("myb", [128, 512], F32)
    ryb = c.sbuf("ryb", [128, 512], F32)
    y0 = c.sbuf("y0", [128, 8 * 512], F32)
    xin_t = c.sbuf("xin_t", [128, 512], F32)
    xcc = c.sbuf("xcc", [128, 8 * 512], F32)
    xot = c.sbuf("xot", [128, 512], F32)
    sqq = c.sbuf("sqq", [128, 8 * 512], F32)
    ps_st = PB[0:1, 0:512]
    ps_st2 = PB[0:1, 512:1024]
    ps_bc = PA[:, 0:512]

    for occ in range(8):
        c.dma("sync", bo_c[:, occ:occ + 1], bo[occ * 128:(occ + 1) * 128].rearrange("(p one) -> p one", one=1),
              group="sm3")
    c.dma("sync", mu_row[:], mur[:], group="sm3")
    c.dma("sync", r1_row[:], r1r[:], group="sm3")
    sm3 = [c.gtok("sm3")]

    for qcol in range(Tn // 512):
        qs = qcol * 512
        c.barrier()
        lx = [c.dma("sync", xcc[:].rearrange("p (cc n) -> p cc n", n=512),
                    x[:, qs:qs + 512].rearrange("(cc p) n -> p cc n", p=128), group="lxe")]
        lxt = [c.gtok("lxe")]
        b1 = c.op("tensor", lambda t, qs=qs: t.matmul(ps_bc, onesr[:], mu_row[:, qs:qs + 512],
                                                      start=True, stop=True), deps=sm3 + [t_o1])
        b1c = c.op("vector", lambda v: v.tensor_copy(mb[:], ps_bc), deps=[b1])
        b2 = c.op("tensor", lambda t, qs=qs: t.matmul(ps_bc, onesr[:], r1_row[:, qs:qs + 512],
                                                      start=True, stop=True), deps=[b1c])
        b2c = c.op("vector", lambda v: v.tensor_copy(rb[:], ps_bc), deps=[b2])
        ytoks = []
        for cc in range(8):
            t1 = c.op("vector", lambda v, cc=cc: v.tensor_tensor(
                xin_t[:], xcc[:, cc * 512:(cc + 1) * 512], mb[:], ALU.subtract), deps=lxt + [b1c] + ytoks[-1:])
            t2 = c.op("vector", lambda v: v.tensor_tensor(xin_t[:], xin_t[:], rb[:], ALU.mult), deps=[t1, b2c])
            t3 = c.op("vector", lambda v, cc=cc, qs=qs: v.scalar_tensor_tensor(
                y0[:, cc * 512:(cc + 1) * 512], z_sb[:, cc * Tn + qs: cc * Tn + qs + 512], bo_c[:, cc:cc + 1],
                xin_t[:], ALU.add, ALU.add), deps=[t2] + sm3)
            ytoks.append(t3)
        sqs = []
        for cc in range(8):
            sqs.append(c.op("scalar", lambda s, cc=cc: s.activation(sqq[:, cc * 512:(cc + 1) * 512],
                                                                    y0[:, cc * 512:(cc + 1) * 512], AF.Square),
                            deps=[ytoks[cc]]))
        st1 = None
        st2 = None
        for cc in range(8):
            st1 = c.op("tensor", lambda t, cc=cc: t.matmul(ps_st, onesc[:], y0[:, cc * 512:(cc + 1) * 512],
                                                           start=(cc == 0), stop=(cc == 7)),
                       deps=[ytoks[cc], t_o2] + ([st1] if st1 else []))
            st2 = c.op("tensor", lambda t, cc=cc: t.matmul(ps_st2, onesc[:], sqq[:, cc * 512:(cc + 1) * 512],
                                                           start=(cc == 0), stop=(cc == 7)),
                       deps=[sqs[cc]] + ([st2] if st2 else []))
        epsc, epst = c.constcol(EPS)
        e1 = c.op("vector", lambda v: v.tensor_scalar(y2row[:], ps_st2, 1.0 / D, None, ALU.mult), deps=[st2])
        e2 = c.op("vector", lambda v: v.tensor_scalar(yrow[:], ps_st, 1.0 / D, None, ALU.mult), deps=[st1])
        e3 = c.op("vector", lambda v: v.tensor_tensor(myb[0:1, 0:512], yrow[:], yrow[:], ALU.mult), deps=[e2])
        e4 = c.op("vector", lambda v: v.tensor_tensor(y2row[:], y2row[:], myb[0:1, 0:512], ALU.subtract),
                  deps=[e1, e3])
        e5 = c.op("scalar", lambda s: s.activation(y2row[:], y2row[:], AF.Sqrt, bias=epsc[0:1, 0:1]),
                  deps=[e4, epst])
        e6 = c.op("vector", lambda v: v.reciprocal(y2row[:], y2row[:]), deps=[e5])
        b3 = c.op("tensor", lambda t: t.matmul(ps_bc, onesr[:], yrow[:], start=True, stop=True),
                  deps=[e6, b2c])
        b3c = c.op("vector", lambda v: v.tensor_copy(myb[:], ps_bc), deps=[b3])
        b4 = c.op("tensor", lambda t: t.matmul(ps_bc, onesr[:], y2row[:], start=True, stop=True), deps=[b3c])
        b4c = c.op("vector", lambda v: v.tensor_copy(ryb[:], ps_bc), deps=[b4])
        sto = []
        for cc in range(8):
            u1 = c.op("vector", lambda v, cc=cc: v.tensor_tensor(xot[:], y0[:, cc * 512:(cc + 1) * 512],
                                                                 myb[:], ALU.subtract),
                      deps=[b3c, b4c] + sto[-1:])
            u2 = c.op("vector", lambda v: v.tensor_tensor(xot[:], xot[:], ryb[:], ALU.mult), deps=[u1])
            u3 = c.op("vector", lambda v, cc=cc: v.tensor_tensor(xot[:], xot[:], xcc[:, cc * 512:(cc + 1) * 512],
                                                                 ALU.add), deps=[u2])
            sto.append(c.dma("gpsimd", xo[cc * 128:(cc + 1) * 128, qs:qs + 512], xot[:], deps=[u3], group="xst"))
    c.wait_all("gpsimd", [c.gtok("xst")])
    return c.finish()


def host_l0_masks(h, Sn):
    import ml_dtypes
    m = np.zeros((5, 128, 512), dtype=np.float32)
    tril = (np.arange(128)[:, None] <= np.arange(128)[None, :]).astype(np.float32)
    for dp in range(4):
        for sub in range(4):
            dd = dp - sub
            blk = np.ones((128, 128), np.float32) if dd < 0 else (tril if dd == 0 else 0.0)
            m[dp, :, sub * 128:(sub + 1) * 128] = blk
    m[4] = 1.0 if h == 1 else 0.0
    return m.astype(ml_dtypes.bfloat16)



def build_moa(Tn=T, Sn=2 * T, debug=False):
    EA, HD = 8, 128
    NQT = Tn // 128
    NQQ = Tn // 512
    SL = Sn + 128
    NKB = SL // 128
    SC = float(HD ** -0.5)
    MB = 30.0
    KPOST = Tn // 128 + 1

    c = Ctx("moa")
    xn = c.inp("xn", [D, SL], BF16)
    x = c.inp("x", [D, Tn])
    mur = c.inp("mur", [1, Tn])
    r1r = c.inp("r1r", [1, Tn])
    gate_w = c.inp("gate_w", [D, EA])
    q_w = c.inp("q_w", [EA, D, HD])
    q_b = c.inp("q_b", [HD, EA])
    k_w = c.inp("k_w", [D, HD])
    k_b = c.inp("k_b", [HD])
    v_w = c.inp("v_w", [D, HD])
    v_b = c.inp("v_b", [HD])
    o_w = c.inp("o_w", [EA, HD, D])
    o_b = c.inp("o_b", [EA, D])
    rel_t = c.inp("rel_t", [HD, 129])
    xnlo = c.inp("xnlo", [D, Tn], BF16)
    bandm = c.inp("bandm", [3, 4, 128, 768], BF16)
    maskK = c.inp("maskK", [2, 4, 128, 768], BF16)
    roleL = c.inp("roleL", [1, 1])
    roleR = c.inp("roleR", [1, 1])
    x2o = c.outp("x2o", [D, Tn])

    rel_hbm = c.dram("rel_hbm", [EA, (Tn + 256) * 129 + 2048], BF16)
    st_hbm = c.dram("st_hbm", [Tn, 128], BF16)

    onesr = c.sbuf("onesr", [1, 128], F32, side="right")
    onescb = c.sbuf("onescb", [128, 1], BF16, side="right")
    k_cm = c.sbuf("k_cm", [128, SL], BF16, side="right")
    v_tm = c.sbuf("v_tm", [128, NKB * 128], BF16, side="right")
    clcr = c.sbuf("clcr", [128, EA * NQT * 2], F32, side="right")
    strows = c.sbuf("strows", [128, Tn], BF16, side="right")
    grows = c.sbuf("grows", [8, Tn], BF16, side="right")
    rlc = c.sbuf("rlc", [128, 2], F32, side="right")
    q_hbm = c.dram("q_hbm", [EA, 128, Tn], BF16)
    ow_sb = c.sbuf("ow_sb", [128, EA * D], BF16, side="right")
    ob_sb = c.sbuf("ob_sb", [8, D], BF16, side="right")
    sc1 = c.sbuf("sc1", [128, 768], F32, side="right")
    sc2 = c.sbuf("sc2", [128, 768], F32, side="right")
    scE = c.sbuf("scE", [128, 512], BF16, side="right")
    PA = c.psum("PA", [128, 2048])
    PB = c.psum("PB", [128, 2048])

    onesrb = c.sbuf("onesrb", [1, 128], BF16, side="right")
    t_o1 = c.op("gpsimd", lambda g: g.memset(onesr[:], 1.0))
    t_o2 = c.op("gpsimd", lambda g: g.memset(onescb[:], 1.0))
    t_o3 = c.op("gpsimd", lambda g: g.memset(onesrb[:], 1.0))
    negMc, negMt = c.constcol(-MB)

    xn_sb = c.sbuf("xn_sb", [128, 8 * (Tn + 128)], BF16)
    wsmall = c.sbuf("wsmall", [128, 8 * 264], BF16)
    kvb = c.sbuf("kvb", [128, 2], F32)
    qwb = c.sbuf("qwb", [128, 8 * EA * HD], BF16)
    qbb = c.sbuf("qbb", [128, EA], F32)
    relT = c.sbuf("relT", [128, 132], BF16)
    zro = c.sbuf("zro", [128, 132], BF16)
    for cc in range(8):
        c.dma("gpsimd", wsmall[:, cc * 264:cc * 264 + 128], k_w[cc * 128:(cc + 1) * 128, :], group="wl")
        c.dma("gpsimd", wsmall[:, cc * 264 + 128:cc * 264 + 256], v_w[cc * 128:(cc + 1) * 128, :], group="wl")
        c.dma("gpsimd", wsmall[:, cc * 264 + 256:cc * 264 + 264], gate_w[cc * 128:(cc + 1) * 128, :], group="wl")
        for e in range(EA):
            c.dma("gpsimd", qwb[:, (e * 8 + cc) * 128:(e * 8 + cc + 1) * 128],
                  q_w[e, cc * 128:(cc + 1) * 128, :], group="wl")
    for e in range(EA):
        c.dma("gpsimd", ow_sb[:, e * D:(e + 1) * D], o_w[e], group="wl")
    c.dma("gpsimd", ob_sb[:], o_b[:], group="wl")
    c.dma("gpsimd", kvb[:, 0:1], k_b.rearrange("(p one) -> p one", one=1), group="wl2")
    vbrow = c.sbuf("vbrow", [1, 128], F32, side="right")
    vbb = c.sbuf("vbb", [128, 128], F32, side="right")
    c.dma("sync", vbrow[:], v_b.rearrange("(one n) -> one n", one=1), group="wl2f")
    c.dma("gpsimd", qbb[:], q_b[:], group="wl2")
    c.dma("gpsimd", relT[:, 0:129], rel_t[:], group="wl2")
    wl = [c.gtok("wl"), c.gtok("wl2"), c.gtok("wl2f")]
    zt = c.op("vector", lambda v: v.memset(zro[:], 0.0))
    vb1 = c.op("tensor", lambda t: t.matmul(PB[:, 1024:1152], onesr[:], vbrow[:], start=True, stop=True),
               deps=[t_o1] + wl)
    vb2 = c.op("vector", lambda v: v.tensor_copy(vbb[:], PB[:, 1024:1152]), deps=[vb1])

    def colchunks(width):
        out, off = [], 0
        while off < width:
            w = min(512, width - off)
            out.append((off, w))
            off += w
        return out

    for half in range(2):
        h0 = 0 if half == 0 else Tn
        hw = Tn if half == 0 else SL - Tn
        c.barrier()
        c.dma("sync", xn_sb[:, 0:8 * hw].rearrange("p (cc t) -> p cc t", t=hw),
              xn[:, h0:h0 + hw].rearrange("(cc p) t -> p cc t", p=128), group="lxn")
        prev = []
        for (off, w) in colchunks(hw):
            mm = None
            for cc in range(8):
                wst = wsmall[:, cc * 264:cc * 264 + 128]
                c.raw("tensor", lambda t, wst=wst: t.ldweights(wst),
                      deps=(wl + [c.gtok("lxn")] + prev) if cc == 0 else [])
                mm = c.op("tensor", lambda t, wst=wst, cc=cc, off=off, w=w, hw=hw: t.matmul(
                    PA[:, 0:w], wst, xn_sb[:, cc * hw + off: cc * hw + off + w],
                    start=(cc == 0), stop=(cc == 7)))
            prev = [c.op("vector", lambda v, h0=h0, off=off, w=w: v.tensor_scalar(
                k_cm[:, h0 + off: h0 + off + w], PA[:, 0:w], kvb[:, 0:1], None, ALU.add),
                deps=[mm] + wl + prev)]
        prev = []
        for tt in range(hw // 128):
            mm = None
            for cc in range(8):
                st = xn_sb[:, cc * hw + tt * 128: cc * hw + (tt + 1) * 128]
                c.raw("tensor", lambda t, st=st: t.ldweights(st), deps=prev if cc == 0 else [])
                mm = c.op("tensor", lambda t, st=st, cc=cc: t.matmul(
                    PA[:, 512:640], st, wsmall[:, cc * 264 + 128:cc * 264 + 256],
                    start=(cc == 0), stop=(cc == 7)))
            ti = h0 // 128 + tt
            prev = [c.op("vector", lambda v, ti=ti: v.tensor_tensor(
                v_tm[:, ti * 128:(ti + 1) * 128], PA[:, 512:640], vbb[:], ALU.add),
                deps=[mm, vb2] + prev)]
    c.barrier()
    c.dma("sync", xn_sb[:, 0:8 * Tn].rearrange("p (cc t) -> p cc t", t=Tn),
          xn[:, 128:128 + Tn].rearrange("(cc p) t -> p cc t", p=128), group="lxo")
    lxo = [c.gtok("lxo")]
    for e in range(EA):
        prev = lxo
        for tcol in range(Tn // 512):
            mm = None
            for cc in range(8):
                wst = qwb[:, (e * 8 + cc) * 128:(e * 8 + cc + 1) * 128]
                c.raw("tensor", lambda t, wst=wst: t.ldweights(wst), deps=prev if cc == 0 else [])
                mm = c.op("tensor", lambda t, wst=wst, cc=cc, tcol=tcol: t.matmul(
                    PA[:, 0:512], wst, xn_sb[:, cc * Tn + tcol * 512: cc * Tn + (tcol + 1) * 512],
                    start=(cc == 0), stop=(cc == 7)))
            ev = c.op("vector", lambda v, e=e, tcol=tcol: v.tensor_scalar(
                scE[:, 0:512], PA[:, 0:512],
                qbb[:, e:e + 1], SC, ALU.add, ALU.mult), deps=[mm] + prev)
            c.dma("gpsimd", q_hbm[e, :, tcol * 512:(tcol + 1) * 512], scE[:, 0:512], deps=[ev], group="qst")
            prev = [ev, c.gtok("qst")]
        c.barrier()
    for e in range(EA):
        c.dma("gpsimd", rel_hbm[e, 0:128 * 129].rearrange("(p n) -> p n", n=129), zro[:, 0:129],
              deps=[zt], group="rz")
        c.dma("gpsimd", rel_hbm[e, (Tn + 128) * 129:(Tn + 256) * 129].rearrange("(p n) -> p n", n=129),
              zro[:, 0:129], deps=[zt], group="rz")
    qtile = c.sbuf("qtile", [128, 512], BF16, side="right")
    for e in range(EA):
        for qt in range(NQT):
            c.barrier()
            c.dma("sync", qtile[:, 0:128], q_hbm[e, :, qt * 128:(qt + 1) * 128], group="qld")
            lhs = qtile[:, 0:128]
            mm = c.op("tensor", lambda t, lhs=lhs: (t.ldweights(lhs), t.matmul(
                PB[:, 0:129], lhs, relT[:, 0:129], start=True, stop=True))[1], deps=[c.gtok("qld")])
            rv = c.op("vector", lambda v: v.tensor_copy(sc1[:, 0:129], PB[:, 0:129]), deps=[mm])
            bb = c.op("vector", lambda v: v.tensor_copy(scE[:, 0:129], sc1[:, 0:129]), deps=[rv])
            c.dma("gpsimd", rel_hbm[e, (128 + qt * 128) * 129:(128 + (qt + 1) * 128) * 129]
                  .rearrange("(p n) -> p n", n=129), scE[:, 0:129], deps=[bb], group="rst")
            c.op("vector", lambda v, e=e, qt=qt: v.tensor_copy(
                clcr[:, (e * NQT + qt) * 2:(e * NQT + qt) * 2 + 1], sc1[:, 0:1]), deps=[rv])
            c.op("vector", lambda v, e=e, qt=qt: v.tensor_copy(
                clcr[:, (e * NQT + qt) * 2 + 1:(e * NQT + qt) * 2 + 2], sc1[:, 128:129]), deps=[rv])
    c.barrier()
    rl_row = c.sbuf("rl_row", [1, 2], F32, side="right")
    c.dma("sync", rl_row[:, 0:1], roleL[:], group="rlr")
    c.dma("sync", rl_row[:, 1:2], roleR[:], group="rlr")
    rlrt = [c.gtok("rlr")]
    rb1 = c.op("tensor", lambda t: t.matmul(PB[:, 1152:1153], onesr[:], rl_row[:, 0:1], start=True, stop=True),
               deps=[t_o1] + rlrt)
    rb2 = c.op("vector", lambda v: v.tensor_copy(rlc[:, 0:1], PB[:, 1152:1153]), deps=[rb1])
    rb3 = c.op("tensor", lambda t: t.matmul(PB[:, 1152:1153], onesr[:], rl_row[:, 1:2], start=True, stop=True),
               deps=[rb2])
    rb4 = c.op("vector", lambda v: v.tensor_copy(rlc[:, 1:2], PB[:, 1152:1153]), deps=[rb3])
    logit_s = c.sbuf("logit_s", [128, EA], F32)
    gwf = c.sbuf("gwf", [128, 8 * EA], F32)
    gwh = c.sbuf("gwh", [128, 8 * EA], BF16)
    gwl = c.sbuf("gwl", [128, 8 * EA], BF16)
    gwt = c.sbuf("gwt", [128, 8 * EA], F32)
    lo_t = c.sbuf("lo_t", [128, 8 * 128], BF16)
    for cc in range(8):
        c.dma("sync", gwf[:, cc * EA:(cc + 1) * EA], gate_w[cc * 128:(cc + 1) * 128, :], group="gwl")
    gwtk = [c.gtok("gwl")]
    sA = c.op("vector", lambda v: v.tensor_copy(gwh[:], gwf[:]), deps=gwtk)
    sB = c.op("vector", lambda v: v.tensor_copy(gwt[:], gwh[:]), deps=[sA])
    sC = c.op("vector", lambda v: v.tensor_tensor(gwt[:], gwf[:], gwt[:], ALU.subtract), deps=[sB])
    sD = c.op("vector", lambda v: v.tensor_copy(gwl[:], gwt[:]), deps=[sC])
    m1c = c.sbuf("m1c", [128, 1], F32)
    m2c = c.sbuf("m2c", [128, 1], F32)
    msk1 = c.sbuf("msk1", [128, EA], F32)
    msk2 = c.sbuf("msk2", [128, EA], F32)
    g1c = c.sbuf("g1c", [128, 1], F32)
    stt_sb = c.sbuf("stt_sb", [128, 48], F32)
    for qt in range(NQT):
        c.barrier()
        c.dma("sync", lo_t[:].rearrange("p (cc n) -> p cc n", n=128),
              xnlo[:, qt * 128:(qt + 1) * 128].rearrange("(cc p) n -> p cc n", p=128), group="lo")
        lot = [c.gtok("lo")]
        mm = None
        for cc in range(8):
            st = xn_sb[:, cc * Tn + qt * 128: cc * Tn + (qt + 1) * 128]
            stl = lo_t[:, cc * 128:(cc + 1) * 128]
            c.raw("tensor", lambda t, st=st: t.ldweights(st), deps=[sD] + lot if cc == 0 else [])
            c.op("tensor", lambda t, st=st, cc=cc: t.matmul(
                PA[:, 0:EA], st, gwh[:, cc * EA:(cc + 1) * EA],
                start=(cc == 0), stop=False))
            c.op("tensor", lambda t, st=st, cc=cc: t.matmul(
                PA[:, 0:EA], st, gwl[:, cc * EA:(cc + 1) * EA], start=False, stop=False))
            c.raw("tensor", lambda t, stl=stl: t.ldweights(stl))
            mm = c.op("tensor", lambda t, stl=stl, cc=cc: t.matmul(
                PA[:, 0:EA], stl, gwh[:, cc * EA:(cc + 1) * EA],
                start=False, stop=(cc == 7)))
        lg = c.op("vector", lambda v: v.tensor_copy(logit_s[:], PA[:, 0:EA]), deps=[mm])
        r1m = c.op("vector", lambda v: v.reduce_max(m1c[:], logit_s[:], axis=AX.X), deps=[lg])
        q1 = c.op("vector", lambda v: v.tensor_scalar(msk1[:], logit_s[:], m1c[:, 0:1], None, ALU.is_equal),
                  deps=[r1m])
        q2 = c.op("vector", lambda v: v.scalar_tensor_tensor(sc1[:, 0:EA], msk1[:], -1e30, logit_s[:],
                                                             ALU.mult, ALU.add), deps=[q1])
        r2m = c.op("vector", lambda v: v.reduce_max(m2c[:], sc1[:, 0:EA], axis=AX.X), deps=[q2])
        q3 = c.op("vector", lambda v: v.tensor_scalar(msk2[:], sc1[:, 0:EA], m2c[:, 0:1], None, ALU.is_equal),
                  deps=[r2m])
        d12 = c.op("vector", lambda v: v.tensor_tensor(g1c[:], m1c[:], m2c[:], ALU.subtract), deps=[r2m, q1])
        sg = c.op("scalar", lambda s: s.activation(g1c[:], g1c[:], AF.Sigmoid), deps=[d12])
        z0 = c.op("vector", lambda v: v.memset(stt_sb[:, 8:48], 0.0), deps=[lg])
        gA = c.op("vector", lambda v: v.tensor_scalar(stt_sb[:, 0:EA], msk1[:], g1c[:, 0:1], None, ALU.mult),
                  deps=[sg, q3, z0])
        gB = c.op("vector", lambda v: v.tensor_scalar(sc1[:, 0:EA], msk2[:], g1c[:, 0:1], None, ALU.mult),
                  deps=[sg, gA])
        gC = c.op("vector", lambda v: v.tensor_tensor(sc1[:, 0:EA], msk2[:], sc1[:, 0:EA], ALU.subtract),
                  deps=[gB])
        gD = c.op("vector", lambda v: v.tensor_tensor(stt_sb[:, 0:EA], stt_sb[:, 0:EA], sc1[:, 0:EA], ALU.add),
                  deps=[gC])
        eo = [gD]
        for e in range(EA):
            eo.append(c.op("scalar", lambda s, e=e, qt=qt: s.activation(
                stt_sb[:, 16 + e:17 + e], clcr[:, (e * NQT + qt) * 2:(e * NQT + qt) * 2 + 1],
                AF.Exp), deps=[sg, z0]))
            eo.append(c.op("scalar", lambda s, e=e, qt=qt: s.activation(
                stt_sb[:, 24 + e:25 + e], clcr[:, (e * NQT + qt) * 2 + 1:(e * NQT + qt) * 2 + 2],
                AF.Exp), deps=[sg, z0]))
        p1 = c.op("vector", lambda v: v.tensor_scalar(stt_sb[:, 32:40], stt_sb[:, 16:24], rlc[:, 0:1],
                                                      None, ALU.mult), deps=eo + [rb2])
        p2 = c.op("vector", lambda v: v.tensor_scalar(stt_sb[:, 40:48], stt_sb[:, 24:32], rlc[:, 1:2],
                                                      None, ALU.mult), deps=[p1, rb4])
        p3 = c.op("vector", lambda v: v.tensor_tensor(stt_sb[:, 8:16], stt_sb[:, 32:40], stt_sb[:, 40:48],
                                                      ALU.add), deps=[p2])
        cv = c.op("vector", lambda v: v.tensor_copy(scE[:, 0:48], stt_sb[:, 0:48]), deps=[p3])
        c.dma("gpsimd", st_hbm[qt * 128:(qt + 1) * 128, 0:48], scE[:, 0:48], deps=[cv], group="sst")
    c.barrier()
    c.dma("sync", strows[:], st_hbm[:, :], transpose=True, group="str")
    strt = [c.gtok("str")]
    for e in range(EA):
        c.dma("sync", grows[e:e + 1, :], strows[e:e + 1, :], deps=strt, group="rd")
    rd = [c.gtok("rd")]
    c.barrier()
    c.sbuf_free("stt_sb", "g1c", "msk2", "msk1", "m2c", "m1c", "logit_s",
                "zro", "relT", "qbb", "qwb", "kvb", "wsmall", "xn_sb",
                "gwf", "gwh", "gwl", "gwt", "lo_t")

    Bsb = c.sbuf("Bsb", [128, 4 * 768], F32)
    stage0 = c.sbuf("stage0", [1, 6 * 512], BF16)
    q_sl = c.sbuf("q_sl", [128, 512], BF16)
    bm_sb = c.sbuf("bm_sb", [128, 12 * 768], BF16)
    km_sb = c.sbuf("km_sb", [128, 8 * 768], BF16)
    fct = c.sbuf("fct", [128, 5 * 512], F32)
    gzb = c.sbuf("gzb", [128, 512], F32)
    zr = c.sbuf("zr", [1, 512], F32)
    ctxs = c.sbuf("ctxs", [128, EA * 512], BF16)
    xcc = c.sbuf("xcc", [128, 8 * 512], F32)
    mb = c.sbuf("mb", [128, 512], F32)
    rb = c.sbuf("rb", [128, 512], F32)
    x2t = c.sbuf("x2t", [128, 512], F32)
    scF = c.sbuf("scF", [128, 512], F32)
    c.dma("sync", bm_sb[:].rearrange("p (k s n) -> p k s n", s=4, n=768),
          bandm.rearrange("k s p n -> p k s n"), group="bm")
    c.dma("sync", km_sb[:].rearrange("p (k s n) -> p k s n", s=4, n=768),
          maskK.rearrange("k s p n -> p k s n"), group="bm")
    bmt = [c.gtok("bm")]
    ident_f = c.sbuf("ident_f", [128, 128], F32)
    ii1 = c.op("gpsimd", lambda g: g.iota(sc1[:, 0:128], [[1, 128]], channel_multiplier=0,
                                          allow_small_or_imprecise_dtypes=True))
    ii2 = c.op("gpsimd", lambda g: g.iota(sc2[:, 0:128], [[0, 128]], channel_multiplier=1,
                                          allow_small_or_imprecise_dtypes=True))
    c.op("vector", lambda v: v.tensor_tensor(ident_f[:], sc1[:, 0:128], sc2[:, 0:128], ALU.is_equal),
         deps=[ii1, ii2])
    mu_row = c.sbuf("mu_rowM", [1, Tn], F32)
    r1_row = c.sbuf("r1_rowM", [1, Tn], F32)
    c.dma("sync", mu_row[:], mur[:], group="smu")
    c.dma("sync", r1_row[:], r1r[:], group="smu")
    smu = [c.gtok("smu")]
    ps_sc = PA[:, 0:512]
    ps_ctx = PA[:, 512:1024]
    ps_Z = PB[0:1, 0:512]
    ps_bc = PB[:, 512:1024]
    c.barrier()

    for qq in range(NQQ):
        q0 = 128 + qq * 512
        zone0 = q0 - 128
        kz0 = zone0 // 128
        kz1 = kz0 + 6
        for e in range(EA):
            c.barrier()
            for j, slot in enumerate([e, 8 + e, 16 + e, 24 + e, 32 + e, 40 + e]):
                c.dma("sync", stage0[:, j * 512:(j + 1) * 512],
                      strows[slot:slot + 1, qq * 512:(qq + 1) * 512], group="stg")
            c.dma("sync", q_sl[:], q_hbm[e, :, qq * 512:(qq + 1) * 512], group="stg")
            stg = [c.gtok("stg")]
            fprev = []
            for j in range(5):
                bmm = c.op("tensor", lambda t, j=j: t.matmul(
                    ps_bc, onesrb[:], stage0[:, (j + 1) * 512:(j + 2) * 512],
                    start=True, stop=True), deps=[t_o3] + stg + fprev)
                fprev = [c.op("vector", lambda v, j=j: v.tensor_copy(fct[:, j * 512:(j + 1) * 512], ps_bc),
                              deps=[bmm])]
            btoks = []
            for sub in range(4):
                qa = q0 + sub * 128
                off = (qa - 128 + 128) * 129 + (zone0 - qa) + 64
                bl = c.dma("gpsimd", sc1[:, 0:768], mkap(rel_hbm[e], off, [[128, 128], [1, 768]]),
                           deps=btoks[-1:], group="bld")
                b1 = c.op("vector", lambda v, sub=sub: v.tensor_tensor(
                    sc2[:, 0:768], sc1[:, 0:768], bm_sb[:, (0 * 4 + sub) * 768:(0 * 4 + sub + 1) * 768],
                    ALU.mult), deps=[c.gtok("bld")] + bmt)
                qt = qq * 4 + sub
                b2 = c.op("vector", lambda v, sub=sub, e=e, qt=qt: v.scalar_tensor_tensor(
                    sc2[:, 0:768], bm_sb[:, (1 * 4 + sub) * 768:(1 * 4 + sub + 1) * 768],
                    clcr[:, (e * NQT + qt) * 2:(e * NQT + qt) * 2 + 1],
                    sc2[:, 0:768], ALU.mult, ALU.add), deps=[b1])
                b3 = c.op("vector", lambda v, sub=sub, e=e, qt=qt: v.scalar_tensor_tensor(
                    Bsb[:, sub * 768:(sub + 1) * 768], bm_sb[:, (2 * 4 + sub) * 768:(2 * 4 + sub + 1) * 768],
                    clcr[:, (e * NQT + qt) * 2 + 1:(e * NQT + qt) * 2 + 2],
                    sc2[:, 0:768], ALU.mult, ALU.add), deps=[b2])
                if qq == 0:
                    b3 = c.op("vector", lambda v, sub=sub: v.scalar_tensor_tensor(
                        Bsb[:, sub * 768:(sub + 1) * 768], km_sb[:, (0 * 4 + sub) * 768:(0 * 4 + sub + 1) * 768],
                        -1e30, Bsb[:, sub * 768:(sub + 1) * 768], ALU.mult, ALU.add), deps=[b3] + bmt)
                if qq == NQQ - 1:
                    b3 = c.op("vector", lambda v, sub=sub: v.scalar_tensor_tensor(
                        Bsb[:, sub * 768:(sub + 1) * 768], km_sb[:, (1 * 4 + sub) * 768:(1 * 4 + sub + 1) * 768],
                        -1e30, Bsb[:, sub * 768:(sub + 1) * 768], ALU.mult, ALU.add), deps=[b3] + bmt)
                btoks.append(b3)
            prevE = []
            for kb in range(NKB):
                lhs = k_cm[:, kb * 128:(kb + 1) * 128]
                inzone = kz0 <= kb < kz1
                mm = c.op("tensor", lambda t, lhs=lhs, inzone=inzone: (
                    t.ldweights(lhs), t.matmul(
                        ps_sc, lhs, q_sl[:],
                        start=True, stop=(not inzone)))[1], deps=prevE + btoks + fprev + stg)
                if inzone:
                    for sub in range(4):
                        mm = c.op("tensor", lambda t, kb=kb, sub=sub, kz0=kz0: t.matmul(
                            ps_sc[:, sub * 128:(sub + 1) * 128],
                            Bsb[:, sub * 768 + (kb - kz0) * 128: sub * 768 + (kb - kz0) * 128 + 128],
                            ident_f[:], start=False, stop=(sub == 3), is_transpose=True))
                    Etok = c.op("scalar", lambda s: s.activation(scE[:], ps_sc, AF.Exp, bias=negMc[:, 0:1]),
                                deps=[mm, negMt])
                else:
                    ex = c.op("scalar", lambda s: s.activation(scF[:], ps_sc, AF.Exp, bias=negMc[:, 0:1]),
                              deps=[mm, negMt])
                    if kb == 0:
                        fsel = 3
                    elif kb == KPOST:
                        fsel = 4
                    elif kb < kz0:
                        fsel = 1
                    elif kb <= KPOST:
                        fsel = 2
                    else:
                        fsel = 0
                    Etok = c.op("vector", lambda v, fsel=fsel: v.tensor_tensor(
                        scE[:], scF[:], fct[:, fsel * 512:(fsel + 1) * 512], ALU.mult), deps=[ex])
                cm = c.op("tensor", lambda t, kb=kb: (t.ldweights(v_tm[:, kb * 128:(kb + 1) * 128]), t.matmul(
                    ps_ctx, v_tm[:, kb * 128:(kb + 1) * 128], scE[:], start=(kb == 0), stop=(kb == NKB - 1)))[1],
                    deps=[Etok])
                zm = c.op("tensor", lambda t, kb=kb: (t.ldweights(onescb[:]), t.matmul(
                    ps_Z, onescb[:], scE[:], start=(kb == 0), stop=(kb == NKB - 1)))[1], deps=[t_o2])
                prevE = [cm, zm]
            rz = c.op("vector", lambda v: v.reciprocal(zr[:], ps_Z), deps=prevE)
            gz = c.op("vector", lambda v: v.tensor_tensor(
                zr[:], zr[:], stage0[:, 0:512], ALU.mult), deps=[rz])
            bz = c.op("tensor", lambda t: t.matmul(ps_bc, onesr[:], zr[:], start=True, stop=True), deps=[gz])
            bzc = c.op("vector", lambda v: v.tensor_copy(gzb[:], ps_bc), deps=[bz])
            c.op("vector", lambda v, e=e: v.tensor_tensor(ctxs[:, e * 512:(e + 1) * 512], ps_ctx, gzb[:],
                                                          ALU.mult), deps=[bzc])
        c.barrier()
        b1 = c.op("tensor", lambda t, qq=qq: t.matmul(ps_bc, onesr[:],
                  mu_row[:, qq * 512:(qq + 1) * 512], start=True, stop=True), deps=smu)
        b1c = c.op("vector", lambda v: v.tensor_copy(mb[:], ps_bc), deps=[b1])
        b2 = c.op("tensor", lambda t, qq=qq: t.matmul(ps_bc, onesr[:],
                  r1_row[:, qq * 512:(qq + 1) * 512], start=True, stop=True), deps=[b1c])
        b2c = c.op("vector", lambda v: v.tensor_copy(rb[:], ps_bc), deps=[b2])
        c.dma("sync", xcc[:].rearrange("p (cc n) -> p cc n", n=512),
              x[:, qq * 512:(qq + 1) * 512].rearrange("(cc p) n -> p cc n", p=128), group="lxq")
        lxt = [c.gtok("lxq")]
        evtoks = []
        for ccg in range(2):
            mms = []
            for cc4 in range(4):
                cc = ccg * 4 + cc4
                mm = None
                for e in range(EA):
                    lhs = ow_sb[:, e * D + cc * 128: e * D + cc * 128 + 128]
                    c.raw("tensor", lambda t, lhs=lhs: t.ldweights(lhs),
                          deps=evtoks if (cc4 == 0 and e == 0) else [])
                    mm = c.op("tensor", lambda t, lhs=lhs, e=e, cc4=cc4: t.matmul(
                        PA[:, cc4 * 512:(cc4 + 1) * 512], lhs, ctxs[:, e * 512:(e + 1) * 512],
                        start=(e == 0), stop=False))
                obl = ob_sb[:, cc * 128:(cc + 1) * 128]
                c.raw("tensor", lambda t, obl=obl: t.ldweights(obl))
                mm = c.op("tensor", lambda t, obl=obl, cc4=cc4, qq=qq: t.matmul(
                    PA[:, cc4 * 512:(cc4 + 1) * 512], obl, grows[:, qq * 512:(qq + 1) * 512],
                    start=False, stop=True))
                mms.append(mm)
            for cc4 in range(4):
                cc = ccg * 4 + cc4
                t1 = c.op("vector", lambda v, cc=cc: v.tensor_tensor(
                    x2t[:], xcc[:, cc * 512:(cc + 1) * 512], mb[:], ALU.subtract),
                    deps=[mms[cc4], b1c, b2c] + lxt + [c.gtok("x2s")])
                t2 = c.op("vector", lambda v: v.tensor_tensor(x2t[:], x2t[:], rb[:], ALU.mult), deps=[t1])
                t3 = c.op("vector", lambda v, cc4=cc4: v.tensor_tensor(
                    x2t[:], x2t[:], PA[:, cc4 * 512:(cc4 + 1) * 512], ALU.add), deps=[t2])
                evtoks = [t3]
                c.dma("gpsimd", x2o[cc * 128:(cc + 1) * 128, qq * 512:(qq + 1) * 512], x2t[:],
                      deps=[t3], group="x2s")
        c.barrier()
    if debug:
        dq = c.outp("dbg_q", [128, EA * Tn], BF16)
        dk = c.outp("dbg_k", [128, SL], BF16)
        dv = c.outp("dbg_v", [128, NKB * 128], BF16)
        dc = c.outp("dbg_ctx", [128, EA * 512], BF16)
        dg = c.outp("dbg_rows", [1, 48 * Tn], BF16)
        c.dma("gpsimd", dq, q_all[:], group="x2s")
        c.dma("gpsimd", dk, k_cm[:], group="x2s")
        c.dma("gpsimd", dv, v_tm[:], group="x2s")
        c.dma("gpsimd", dc, ctxs[:], group="x2s")
        c.dma("gpsimd", dg, rows0[:], group="x2s")
    c.wait_all("gpsimd", [c.gtok("x2s")])
    return c.finish()


def host_moa_layout(h, Tn, Sn):
    import ml_dtypes
    HSn = Sn // 2
    own0 = h * HSn
    par0 = (1 - h) * HSn
    if h == 0:
        pre = None
        post = (par0, par0 + 128)
        rest = (par0 + 128, par0 + HSn)
    else:
        pre = (par0 + HSn - 128, par0 + HSn)
        post = None
        rest = (par0, par0 + HSn - 128)
    j = np.arange(768)[None, :]
    qr = np.arange(128)[:, None]
    bandm = np.zeros((3, 4, 128, 768), np.float32)
    for sub in range(4):
        d = j - 128 - 128 * sub - qr
        bandm[0, sub] = (np.abs(d) <= 64)
        bandm[1, sub] = (d < -64)
        bandm[2, sub] = (d > 64)
    maskK = np.zeros((2, 4, 128, 768), np.float32)
    if h == 0:
        maskK[0, :, :, 0:128] = 1.0
    if h == 1:
        maskK[1, :, :, 640:768] = 1.0
    bf = ml_dtypes.bfloat16
    return ((pre, (own0, own0 + Tn), post, rest),
            bandm.astype(bf), maskK.astype(bf),
            np.array([[1.0 if h == 1 else 0.0]], np.float32),
            np.array([[1.0 if h == 0 else 0.0]], np.float32))



def build_moe(Tn=T, fuse_glue=False):
    EF, FH = 16, 512
    NQT = Tn // 128
    NQQ = Tn // 512
    c = Ctx("moe")
    x2 = c.inp("x2", [D, Tn])
    x = c.inp("x", [D, Tn])
    ln_w = c.inp("ln_w", [D])
    ln_b = c.inp("ln_b", [D])
    gate_w = c.inp("gate_w", [D, EF])
    w1 = c.inp("w1", [EF, D, FH])
    b1 = c.inp("b1", [EF, FH])
    w2 = c.inp("w2", [EF, FH, D])
    b2 = c.inp("b2", [EF, D])
    xo = c.outp("xo", [D, Tn])

    w1bf = c.dram("w1bf", [EF, D, FH], BF16)
    w2bf = c.dram("w2bf", [EF, FH, D], BF16)
    st_hbm = c.dram("st2_hbm", [Tn, 128], BF16)

    onesr = c.sbuf("onesr", [1, 128], F32, side="right")
    onesc = c.sbuf("onesc", [128, 1], F32, side="right")
    xn2h = c.sbuf("xn2h", [128, 8 * Tn], BF16, side="right")
    strows = c.sbuf("strows", [128, Tn], BF16, side="right")
    g16 = c.sbuf("g16", [16, Tn], BF16, side="right")
    b2sb = c.sbuf("b2sb", [16, D], BF16, side="right")
    b1c = c.sbuf("b1c", [128, EF * 4], F32, side="right")
    wbc = c.sbuf("wbc", [128, 16], F32, side="right")
    sc1 = c.sbuf("sc1", [128, 512], F32, side="right")
    sc2 = c.sbuf("sc2", [128, 512], F32, side="right")
    scE = c.sbuf("scE", [128, 512], BF16, side="right")
    mb = c.sbuf("mb", [128, 512], F32, side="right")
    rb = c.sbuf("rb", [128, 512], F32, side="right")
    stage0 = c.sbuf("stage0", [1, 512], BF16, side="right")
    PA = c.psum("PA", [128, 2048])
    PB = c.psum("PB", [128, 2048])
    t_o1 = c.op("gpsimd", lambda g: g.memset(onesr[:], 1.0))
    t_o2 = c.op("gpsimd", lambda g: g.memset(onesc[:], 1.0))
    epsc, epst = c.constcol(EPS)

    for e in range(EF):
        c.dma("gpsimd", w1bf[e], w1[e], group="wcast")
        c.dma("gpsimd", w2bf[e], w2[e], group="wcast")
    c.dma("gpsimd", b2sb[:], b2[:], group="wl")
    for e in range(EF):
        for hc in range(4):
            c.dma("sync", b1c[:, e * 4 + hc:e * 4 + hc + 1],
                  b1[e, hc * 128:(hc + 1) * 128].rearrange("(p one) -> p one", one=1), group="wl2")
    for cc in range(8):
        c.dma("sync", wbc[:, cc:cc + 1], ln_w[cc * 128:(cc + 1) * 128].rearrange("(p one) -> p one", one=1),
              group="wl2")
        c.dma("sync", wbc[:, 8 + cc:9 + cc], ln_b[cc * 128:(cc + 1) * 128].rearrange("(p one) -> p one", one=1),
              group="wl2")
    wl = [c.gtok("wl"), c.gtok("wl2")]

    xn2l_hbm = c.dram("xn2l_hbm", [D, Tn], BF16)
    lo_t2 = c.sbuf("lo_t2", [128, 512], BF16)
    x2cc = c.sbuf("x2cc", [128, 8 * 512], F32)
    murow = c.sbuf("murow", [1, Tn], F32)
    rrow = c.sbuf("rrow", [1, Tn], F32)
    sqrow = c.sbuf("sqrow", [1, Tn], F32)
    tf = c.sbuf("tf", [128, 512], F32)
    for qq in range(NQQ):
        c.barrier()
        c.dma("sync", x2cc[:].rearrange("p (cc n) -> p cc n", n=512),
              x2[:, qq * 512:(qq + 1) * 512].rearrange("(cc p) n -> p cc n", p=128), group="lx2")
        lxt = [c.gtok("lx2")]
        s1 = None
        s2 = None
        sqs = []
        for cc in range(8):
            q1 = c.op("scalar", lambda s, cc=cc: s.activation(sc1[:], x2cc[:, cc * 512:(cc + 1) * 512],
                                                              AF.Square),
                      deps=lxt + sqs[-1:] + ([s2] if s2 else []))
            sq = c.op("vector", lambda v: v.tensor_copy(scE[:], sc1[:]), deps=[q1])
            s1 = c.op("tensor", lambda t, cc=cc: t.matmul(PB[0:1, 0:512], onesc[:],
                                                          x2cc[:, cc * 512:(cc + 1) * 512],
                                                          start=(cc == 0), stop=(cc == 7)),
                      deps=[t_o2] + lxt + ([s1] if s1 else []))
            s2 = c.op("tensor", lambda t, cc=cc: t.matmul(PB[0:1, 512:1024], onesc[:], sc1[:],
                                                          start=(cc == 0), stop=(cc == 7)),
                      deps=[q1] + ([s2] if s2 else []))
            sqs = [sq]
        e1 = c.op("vector", lambda v, qq=qq: v.tensor_scalar(murow[:, qq * 512:(qq + 1) * 512],
                                                             PB[0:1, 0:512], 1.0 / D, None, ALU.mult), deps=[s1])
        e2 = c.op("vector", lambda v, qq=qq: v.tensor_scalar(sqrow[:, qq * 512:(qq + 1) * 512],
                                                             PB[0:1, 512:1024], 1.0 / D, None, ALU.mult), deps=[s2])
        e3 = c.op("vector", lambda v, qq=qq: v.tensor_tensor(sc1[0:1, 0:512],
                                                             murow[:, qq * 512:(qq + 1) * 512],
                                                             murow[:, qq * 512:(qq + 1) * 512], ALU.mult),
                  deps=[e1, e2])
        e4 = c.op("vector", lambda v, qq=qq: v.tensor_tensor(sqrow[:, qq * 512:(qq + 1) * 512],
                                                             sqrow[:, qq * 512:(qq + 1) * 512],
                                                             sc1[0:1, 0:512], ALU.subtract), deps=[e3])
        e5 = c.op("scalar", lambda s, qq=qq: s.activation(sqrow[:, qq * 512:(qq + 1) * 512],
                                                          sqrow[:, qq * 512:(qq + 1) * 512], AF.Sqrt,
                                                          bias=epsc[0:1, 0:1]), deps=[e4, epst])
        e6 = c.op("vector", lambda v, qq=qq: v.reciprocal(rrow[:, qq * 512:(qq + 1) * 512],
                                                          sqrow[:, qq * 512:(qq + 1) * 512]), deps=[e5])
        bm1 = c.op("tensor", lambda t, qq=qq: t.matmul(PB[:, 1024:1536], onesr[:],
                                                       murow[:, qq * 512:(qq + 1) * 512], start=True, stop=True),
                   deps=[e6, t_o1])
        bc1 = c.op("vector", lambda v: v.tensor_copy(mb[:], PB[:, 1024:1536]), deps=[bm1])
        bm2 = c.op("tensor", lambda t, qq=qq: t.matmul(PB[:, 1024:1536], onesr[:],
                                                       rrow[:, qq * 512:(qq + 1) * 512], start=True, stop=True),
                   deps=[bc1])
        bc2 = c.op("vector", lambda v: v.tensor_copy(rb[:], PB[:, 1024:1536]), deps=[bm2])
        prevf = []
        for cc in range(8):
            f1 = c.op("vector", lambda v, cc=cc: v.tensor_tensor(tf[:], x2cc[:, cc * 512:(cc + 1) * 512],
                                                                 mb[:], ALU.subtract), deps=[bc1, bc2] + prevf)
            f2 = c.op("vector", lambda v: v.tensor_tensor(tf[:], tf[:], rb[:], ALU.mult), deps=[f1])
            f3 = c.op("vector", lambda v, cc=cc: v.tensor_scalar(tf[:], tf[:], wbc[:, cc:cc + 1],
                                                          wbc[:, 8 + cc:9 + cc],
                                                          ALU.mult, ALU.add), deps=[f2] + wl)
            f4 = c.op("vector", lambda v, cc=cc, qq=qq: v.tensor_copy(
                xn2h[:, cc * Tn + qq * 512: cc * Tn + qq * 512 + 512], tf[:]), deps=[f3])
            f5 = c.op("vector", lambda v, cc=cc, qq=qq: v.tensor_tensor(
                lo_t2[:], tf[:],
                xn2h[:, cc * Tn + qq * 512: cc * Tn + qq * 512 + 512], ALU.subtract),
                deps=[f4] + [c.gtok("lo2s")])
            c.dma("gpsimd", xn2l_hbm[cc * 128:(cc + 1) * 128, qq * 512:(qq + 1) * 512], lo_t2[:],
                  deps=[f5], group="lo2s")
            prevf = [f5]
    c.barrier()

    gwf = c.sbuf("gwf", [128, 8 * EF], F32)
    gwh = c.sbuf("gwh", [128, 8 * EF], BF16)
    gwl = c.sbuf("gwl", [128, 8 * EF], BF16)
    gwt = c.sbuf("gwt", [128, 8 * EF], F32)
    logit_s = c.sbuf("logit_s", [128, EF], F32)
    m1c = c.sbuf("m1c", [128, 1], F32)
    m2c = c.sbuf("m2c", [128, 1], F32)
    msk1 = c.sbuf("msk1", [128, EF], F32)
    msk2 = c.sbuf("msk2", [128, EF], F32)
    g1c = c.sbuf("g1c", [128, 1], F32)
    stt_sb = c.sbuf("stt_sb", [128, EF], F32)
    for cc in range(8):
        c.dma("sync", gwf[:, cc * EF:(cc + 1) * EF], gate_w[cc * 128:(cc + 1) * 128, :], group="gwl")
    gwtk = [c.gtok("gwl")]
    sA = c.op("vector", lambda v: v.tensor_copy(gwh[:], gwf[:]), deps=gwtk)
    sB = c.op("vector", lambda v: v.tensor_copy(gwt[:], gwh[:]), deps=[sA])
    sC = c.op("vector", lambda v: v.tensor_tensor(gwt[:], gwf[:], gwt[:], ALU.subtract), deps=[sB])
    sD = c.op("vector", lambda v: v.tensor_copy(gwl[:], gwt[:]), deps=[sC])
    lo_g = c.sbuf("lo_g", [128, 8 * 128], BF16)
    for qt in range(NQT):
        c.barrier()
        c.dma("sync", lo_g[:].rearrange("p (cc n) -> p cc n", n=128),
              xn2l_hbm[:, qt * 128:(qt + 1) * 128].rearrange("(cc p) n -> p cc n", p=128), group="lo2l")
        lo2t = [c.gtok("lo2l")]
        mm = None
        for cc in range(8):
            sh = xn2h[:, cc * Tn + qt * 128: cc * Tn + (qt + 1) * 128]
            sl = lo_g[:, cc * 128:(cc + 1) * 128]
            c.raw("tensor", lambda t, sh=sh: t.ldweights(sh), deps=([sD] + lo2t) if cc == 0 else [])
            c.op("tensor", lambda t, sh=sh, cc=cc: t.matmul(
                PA[:, 0:EF], sh, gwh[:, cc * EF:(cc + 1) * EF], start=(cc == 0), stop=False))
            c.op("tensor", lambda t, sh=sh, cc=cc: t.matmul(
                PA[:, 0:EF], sh, gwl[:, cc * EF:(cc + 1) * EF], start=False, stop=False))
            c.raw("tensor", lambda t, sl=sl: t.ldweights(sl))
            mm = c.op("tensor", lambda t, sl=sl, cc=cc: t.matmul(
                PA[:, 0:EF], sl, gwh[:, cc * EF:(cc + 1) * EF], start=False, stop=(cc == 7)))
        lg = c.op("vector", lambda v: v.tensor_copy(logit_s[:], PA[:, 0:EF]), deps=[mm])
        r1m = c.op("vector", lambda v: v.reduce_max(m1c[:], logit_s[:], axis=AX.X), deps=[lg])
        q1 = c.op("vector", lambda v: v.tensor_scalar(msk1[:], logit_s[:], m1c[:, 0:1], None, ALU.is_equal),
                  deps=[r1m])
        q2 = c.op("vector", lambda v: v.scalar_tensor_tensor(sc1[:, 0:EF], msk1[:], -1e30, logit_s[:],
                                                             ALU.mult, ALU.add), deps=[q1])
        r2m = c.op("vector", lambda v: v.reduce_max(m2c[:], sc1[:, 0:EF], axis=AX.X), deps=[q2])
        q3 = c.op("vector", lambda v: v.tensor_scalar(msk2[:], sc1[:, 0:EF], m2c[:, 0:1], None, ALU.is_equal),
                  deps=[r2m])
        d12 = c.op("vector", lambda v: v.tensor_tensor(g1c[:], m1c[:], m2c[:], ALU.subtract), deps=[r2m, q1])
        sg = c.op("scalar", lambda s: s.activation(g1c[:], g1c[:], AF.Sigmoid), deps=[d12])
        gA = c.op("vector", lambda v: v.tensor_scalar(stt_sb[:], msk1[:], g1c[:, 0:1], None, ALU.mult),
                  deps=[sg, q3])
        gB = c.op("vector", lambda v: v.tensor_scalar(sc1[:, 0:EF], msk2[:], g1c[:, 0:1], None, ALU.mult),
                  deps=[sg, gA])
        gC = c.op("vector", lambda v: v.tensor_tensor(sc1[:, 0:EF], msk2[:], sc1[:, 0:EF], ALU.subtract),
                  deps=[gB])
        gD = c.op("vector", lambda v: v.tensor_tensor(stt_sb[:], stt_sb[:], sc1[:, 0:EF], ALU.add),
                  deps=[gC])
        cv = c.op("vector", lambda v: v.tensor_copy(scE[:, 0:EF], stt_sb[:]), deps=[gD])
        c.dma("gpsimd", st_hbm[qt * 128:(qt + 1) * 128, 0:EF], scE[:, 0:EF], deps=[cv], group="sst")
    c.barrier()
    c.dma("sync", strows[:], st_hbm[:, :], transpose=True, group="str")
    strt = [c.gtok("str")]
    c.dma("sync", g16[:], strows[0:16, :], deps=strt, group="g16l")
    g16t = [c.gtok("g16l")]
    c.sbuf_free("stt_sb", "g1c", "msk2", "msk1", "m2c", "m1c", "logit_s", "gwt", "gwl", "gwh", "gwf",
                "lo_g", "lo_t2", "x2cc", "murow", "rrow", "sqrow", "tf")

    x2cc2 = c.sbuf("x2cc2", [128, 8 * 512], F32)
    w1sb = [c.sbuf(f"w1sb{i}", [128, 8 * FH], BF16) for i in range(2)]
    w2sb = [c.sbuf(f"w2sb{i}", [128, 4 * D], BF16) for i in range(2)]
    h_hbm = c.dram("h_hbm", [EF, 4 * 128, 512], BF16)
    hsb = c.sbuf("hsb", [128, 4 * 512], BF16)
    hld = [c.sbuf(f"hld{i}", [128, 4 * 512], BF16) for i in range(2)]
    onesrb = c.sbuf("onesrb", [1, 128], BF16)
    g2b = c.sbuf("g2b", [128, 512], F32)
    y0 = c.sbuf("y0", [128, 8 * 512], F32)
    xcc = c.sbuf("xcc", [128, 8 * 512], F32)
    sqq = c.sbuf("sqq", [128, 8 * 512], BF16)
    onescb2 = c.sbuf("onescb2", [128, 1], BF16)
    yrow = c.sbuf("yrow", [1, 512], F32)
    y2row = c.sbuf("y2row", [1, 512], F32)
    myb = c.sbuf("myb", [128, 512], F32)
    ryb = c.sbuf("ryb", [128, 512], F32)
    xot = c.sbuf("xot", [128, 512], F32)
    t_o4 = c.op("gpsimd", lambda g: g.memset(onesrb[:], 1.0))
    t_o5 = c.op("gpsimd", lambda g: g.memset(onescb2[:], 1.0))
    wcast = [c.gtok("wcast")]
    w1war = [[], []]
    w2war = [[], []]
    gbwar = []
    hwar = []
    for qq in range(NQQ):
        c.barrier()
        for e in range(EF):
            c.dma("sync", w1sb[e % 2][:].rearrange("p (cc n) -> p cc n", n=FH),
                  w1bf[e].rearrange("(cc p) n -> p cc n", p=128),
                  deps=wcast + w1war[e % 2], group=f"w1l{e % 2}")
            lwt = [c.gtok(f"w1l{e % 2}")]
            c.dma("sync", stage0[:], strows[e:e + 1, qq * 512:(qq + 1) * 512],
                  deps=strt + [("Ptensor", c._cnt.get("Ptensor", 0))], group="stg")
            sgt = [c.gtok("stg")]
            gb1 = c.op("tensor", lambda t: t.matmul(PB[:, 1024:1536], onesrb[:], stage0[:],
                                                    start=True, stop=True), deps=sgt + [t_o4] + gbwar)
            gb2 = c.op("vector", lambda v: v.tensor_copy(g2b[:], PB[:, 1024:1536]), deps=[gb1] + hwar)
            gbwar = [gb2]
            mm = None
            for hc in range(4):
                for cc in range(8):
                    wst = w1sb[e % 2][:, cc * FH + hc * 128: cc * FH + (hc + 1) * 128]
                    c.raw("tensor", lambda t, wst=wst: t.ldweights(wst),
                          deps=(lwt + hwar) if (hc == 0 and cc == 0) else [])
                    mm = c.op("tensor", lambda t, wst=wst, cc=cc, hc=hc, qq=qq: t.matmul(
                        PA[:, hc * 512:(hc + 1) * 512], wst,
                        xn2h[:, cc * Tn + qq * 512: cc * Tn + qq * 512 + 512],
                        start=(cc == 0), stop=(cc == 7)))
            w1war[e % 2] = [mm]
            prevh = []
            for hc in range(4):
                r1 = c.op("scalar", lambda s, hc=hc, e=e: s.activation(
                    sc1[:], PA[:, hc * 512:(hc + 1) * 512], AF.Relu,
                    bias=b1c[:, e * 4 + hc:e * 4 + hc + 1]), deps=[mm] + wl + prevh)
                h1 = c.op("vector", lambda v, e=e, hc=hc: v.tensor_tensor(
                    hsb[:, hc * 512:(hc + 1) * 512], sc1[:], g2b[:], ALU.mult),
                    deps=[r1, gb2] + ([c.gtok("hst")] if hc == 0 else []))
                prevh = [r1, h1]
                w1war[e % 2] = w1war[e % 2] + [h1]
            hwar = prevh
            c.dma("gpsimd", h_hbm[e].rearrange("(hc p) n -> p hc n", p=128),
                  hsb[:].rearrange("p (hc n) -> p hc n", n=512), deps=prevh, group="hst")
        c.barrier()
        lx2 = c.dma("sync", x2cc2[:].rearrange("p (cc n) -> p cc n", n=512),
                    x2[:, qq * 512:(qq + 1) * 512].rearrange("(cc p) n -> p cc n", p=128), group="lx2b")
        lx = c.dma("sync", xcc[:].rearrange("p (cc n) -> p cc n", n=512),
                   x[:, qq * 512:(qq + 1) * 512].rearrange("(cc p) n -> p cc n", p=128), group="lx2b")
        lxt = [c.gtok("lx2b")]
        PBANK = [PA[:, 0:512], PA[:, 512:1024], PA[:, 1024:1536], PA[:, 1536:2048],
                 PB[:, 0:512], PB[:, 512:1024], PB[:, 1024:1536], PB[:, 1536:2048]]
        mm = None
        hwl = [c.gtok("hst")]
        for e in range(EF):
            c.dma("sync", w2sb[e % 2][:].rearrange("p (hc n) -> p hc n", n=D),
                  w2bf[e].rearrange("(hc p) n -> p hc n", p=128),
                  deps=wcast + w2war[e % 2], group=f"w2l{e % 2}")
            lw2t = [c.gtok(f"w2l{e % 2}")]
            c.dma("sync", hld[e % 2][:].rearrange("p (hc n) -> p hc n", n=512),
                  h_hbm[e].rearrange("(hc p) n -> p hc n", p=128),
                  deps=hwl + w2war[e % 2], group=f"hl{e % 2}")
            hlt = [c.gtok(f"hl{e % 2}")]
            for cc in range(8):
                for hc in range(4):
                    wst = w2sb[e % 2][:, hc * D + cc * 128: hc * D + cc * 128 + 128]
                    c.raw("tensor", lambda t, wst=wst: t.ldweights(wst),
                          deps=(lw2t + hlt) if (cc == 0 and hc == 0) else [])
                    mm = c.op("tensor", lambda t, wst=wst, e=e, hc=hc, cc=cc: t.matmul(
                        PBANK[cc], wst, hld[e % 2][:, hc * 512:(hc + 1) * 512],
                        start=(e == 0 and hc == 0), stop=False))
            w2war[e % 2] = [mm]
        mms = []
        for cc in range(8):
            bb = b2sb[:, cc * 128:(cc + 1) * 128]
            c.raw("tensor", lambda t, bb=bb: t.ldweights(bb))
            mm = c.op("tensor", lambda t, bb=bb, cc=cc, qq=qq: t.matmul(
                PBANK[cc], bb, g16[:, qq * 512:(qq + 1) * 512], start=False, stop=True), deps=g16t)
            mms.append(mm)
        ytoks = []
        for cc in range(8):
            ytoks.append(c.op("vector", lambda v, cc=cc: v.tensor_tensor(
                y0[:, cc * 512:(cc + 1) * 512], x2cc2[:, cc * 512:(cc + 1) * 512],
                PBANK[cc], ALU.add), deps=[mms[cc]] + lxt))
        sqs = []
        st1 = None
        st2 = None
        for cc in range(8):
            sqs.append(c.op("scalar", lambda s, cc=cc: s.activation(sqq[:, cc * 512:(cc + 1) * 512],
                                                                    y0[:, cc * 512:(cc + 1) * 512], AF.Square),
                            deps=[ytoks[cc]]))
        for cc in range(8):
            st1 = c.op("tensor", lambda t, cc=cc: t.matmul(PA[0:1, 0:512], onesc[:],
                                                           y0[:, cc * 512:(cc + 1) * 512],
                                                           start=(cc == 0), stop=(cc == 7)),
                       deps=[t_o2] + ytoks + ([st1] if st1 else []))
            st2 = c.op("tensor", lambda t, cc=cc: t.matmul(PA[0:1, 512:1024], onescb2[:],
                                                           sqq[:, cc * 512:(cc + 1) * 512],
                                                           start=(cc == 0), stop=(cc == 7)),
                       deps=[sqs[cc], t_o5] + ([st2] if st2 else []))
        e1 = c.op("vector", lambda v: v.tensor_scalar(y2row[:], PA[0:1, 512:1024], 1.0 / D, None, ALU.mult),
                  deps=[st2])
        e2 = c.op("vector", lambda v: v.tensor_scalar(yrow[:], PA[0:1, 0:512], 1.0 / D, None, ALU.mult),
                  deps=[st1])
        e3 = c.op("vector", lambda v: v.tensor_tensor(myb[0:1, 0:512], yrow[:], yrow[:], ALU.mult), deps=[e2])
        e4 = c.op("vector", lambda v: v.tensor_tensor(y2row[:], y2row[:], myb[0:1, 0:512], ALU.subtract),
                  deps=[e1, e3])
        e5 = c.op("scalar", lambda s: s.activation(y2row[:], y2row[:], AF.Sqrt, bias=epsc[0:1, 0:1]),
                  deps=[e4, epst])
        e6 = c.op("vector", lambda v: v.reciprocal(y2row[:], y2row[:]), deps=[e5])
        b3 = c.op("tensor", lambda t: t.matmul(PA[:, 1024:1536], onesr[:], yrow[:], start=True, stop=True),
                  deps=[e6, t_o1] + ytoks)
        b3c = c.op("vector", lambda v: v.tensor_copy(myb[:], PA[:, 1024:1536]), deps=[b3])
        b4 = c.op("tensor", lambda t: t.matmul(PA[:, 1024:1536], onesr[:], y2row[:], start=True, stop=True),
                  deps=[b3c])
        b4c = c.op("vector", lambda v: v.tensor_copy(ryb[:], PA[:, 1024:1536]), deps=[b4])
        sto = []
        for cc in range(8):
            u1 = c.op("vector", lambda v, cc=cc: v.tensor_tensor(xot[:], y0[:, cc * 512:(cc + 1) * 512],
                                                                 myb[:], ALU.subtract),
                      deps=[b3c, b4c] + sto + [c.gtok("xst")])
            u2 = c.op("vector", lambda v: v.tensor_tensor(xot[:], xot[:], ryb[:], ALU.mult), deps=[u1])
            u3 = c.op("vector", lambda v, cc=cc: v.tensor_tensor(xot[:], xot[:], xcc[:, cc * 512:(cc + 1) * 512],
                                                                 ALU.add), deps=[u2])
            sto = [u3]
            c.dma("gpsimd", xo[cc * 128:(cc + 1) * 128, qq * 512:(qq + 1) * 512], xot[:], deps=[u3], group="xst")
    c.wait_all("gpsimd", [c.gtok("xst")])
    if fuse_glue:
        ybf = c.outp("ybf", [D, Tn], BF16)
        append_glue(c, PA[:, 0:1024], xo, Tn,
                    (xcc, sqq, yrow, y2row, myb, ryb, xot, onesr, onesc, onescb2),
                    bf16_out=ybf)
    return c.finish()



def build_rwkv_a(Tn=T):
    HS = 16
    c = Ctx("rwkva")
    xin = c.inp("xin", [D, Tn + 1], BF16)
    maa = c.inp("maa", [5, D])
    tmw1 = c.inp("tmw1", [D, 128])
    tmw2 = c.inp("tmw2", [4, 32, D])
    decay_u = c.inp("decay_u", [D])
    dw1 = c.inp("dw1", [D, 64])
    dw2 = c.inp("dw2", [64, D])
    wr = c.inp("wr", [D, D])
    wk = c.inp("wk", [D, D])
    wv = c.inp("wv", [D, D])
    u_in = c.inp("u_in", [D])
    y1o = c.outp("y1", [D, Tn], BF16)
    rco = c.outp("rc", [D, Tn], BF16)
    seo = c.outp("se", [D, HS])

    mix_hbm = c.dram("mix_hbm", [4, D, Tn], BF16)
    r_hbm = c.dram("r_hbm", [D, Tn], BF16)
    k_hbm = c.dram("k_hbm", [D, Tn], BF16)
    v_hbm = c.dram("v_hbm", [D, Tn], BF16)
    wd_hbm = c.dram("wd_hbm", [D, Tn], F32)

    onesr = c.sbuf("onesr", [1, 128], F32, side="right")
    PA = c.psum("PA", [128, 2048])
    PB = c.psum("PB", [128, 2048])
    t_o1 = c.op("gpsimd", lambda g: g.memset(onesr[:], 1.0))

    xin_sb = c.sbuf("xin_sb", [128, 8 * (Tn + 1)], BF16)
    xx_sb = c.sbuf("xx_sb", [128, 8 * Tn], BF16)
    maac = c.sbuf("maac", [128, 5 * 8], F32)
    mg = [c.sbuf(f"mg{f}", [32, Tn], BF16) for f in range(4)]
    w1s = c.sbuf("w1s", [128, 8 * 128], BF16)
    w2s = c.sbuf("w2s", [32, 4 * D], BF16)
    TT1 = Tn + 1
    c.dma("sync", xin_sb[:].rearrange("p (cc t) -> p cc t", t=TT1),
          xin.rearrange("(cc p) t -> p cc t", p=128), group="li")
    for j in range(5):
        for cc in range(8):
            c.dma("sync", maac[:, j * 8 + cc:j * 8 + cc + 1],
                  maa[j, cc * 128:(cc + 1) * 128].rearrange("(p one) -> p one", one=1), group="li")
    for cc in range(8):
        c.dma("gpsimd", w1s[:, cc * 128:(cc + 1) * 128], tmw1[cc * 128:(cc + 1) * 128, :], group="li2")
    for f in range(4):
        c.dma("gpsimd", w2s[:, f * D:(f + 1) * D], tmw2[f], group="li2")
    li = [c.gtok("li"), c.gtok("li2")]
    xxt = []
    for cc in range(8):
        xxt.append(c.op("vector", lambda v, cc=cc: v.tensor_tensor(
            xx_sb[:, cc * Tn:(cc + 1) * Tn], xin_sb[:, cc * TT1: cc * TT1 + Tn],
            xin_sb[:, cc * TT1 + 1: cc * TT1 + Tn + 1], ALU.subtract), deps=li))
    xxx_t = c.sbuf("xxx_t", [128, 8 * 512], BF16)
    prevm = []
    for tcol in range(Tn // 512):
        xts = []
        for cc in range(8):
            xts.append(c.op("vector", lambda v, cc=cc, tcol=tcol: v.scalar_tensor_tensor(
                xxx_t[:, cc * 512:(cc + 1) * 512], xx_sb[:, cc * Tn + tcol * 512: cc * Tn + tcol * 512 + 512],
                maac[:, cc:cc + 1], xin_sb[:, cc * TT1 + tcol * 512 + 1: cc * TT1 + tcol * 512 + 513],
                ALU.mult, ALU.add), deps=[xxt[cc]] + prevm))
        mmk = None
        for cc in range(8):
            wst = w1s[:, cc * 128:(cc + 1) * 128]
            c.raw("tensor", lambda t, wst=wst: t.ldweights(wst), deps=(xts + li) if cc == 0 else [])
            mmk = c.op("tensor", lambda t, wst=wst, cc=cc: t.matmul(
                PA[:, 0:512], wst, xxx_t[:, cc * 512:(cc + 1) * 512], start=(cc == 0), stop=(cc == 7)))
        newm = []
        for f in range(4):
            newm.append(c.op("scalar", lambda s, tcol=tcol, f=f: s.activation(
                mg[f][:, tcol * 512:(tcol + 1) * 512], PA[32 * f:32 * f + 32, 0:512], AF.Tanh),
                deps=[mmk] + prevm))
        prevm = newm
    c.barrier()
    mixt = c.sbuf("mixt", [128, 512], F32)
    mixb = c.sbuf("mixb", [128, 512], BF16)
    for j in range(4):
        for tcol in range(Tn // 512):
            for cc in range(8):
                wst = w2s[0:32, j * D + cc * 128: j * D + (cc + 1) * 128]
                mmk = c.op("tensor", lambda t, wst=wst, j=j, tcol=tcol: (t.ldweights(wst), t.matmul(
                    PA[:, 512:1024], wst, mg[j][0:32, tcol * 512:(tcol + 1) * 512],
                    start=True, stop=True))[1], deps=prevm + [c.gtok("mxs")])
                a1 = c.op("vector", lambda v, j=j, cc=cc: v.tensor_scalar(
                    mixt[:], PA[:, 512:1024], maac[:, (j + 1) * 8 + cc:(j + 1) * 8 + cc + 1], None, ALU.add),
                    deps=[mmk])
                a2 = c.op("vector", lambda v, cc=cc, tcol=tcol: v.tensor_tensor(
                    mixt[:], mixt[:], xx_sb[:, cc * Tn + tcol * 512: cc * Tn + tcol * 512 + 512], ALU.mult),
                    deps=[a1])
                a3 = c.op("vector", lambda v, cc=cc, tcol=tcol: v.tensor_tensor(
                    mixb[:], mixt[:], xin_sb[:, cc * TT1 + tcol * 512 + 1: cc * TT1 + tcol * 512 + 513],
                    ALU.add), deps=[a2])
                c.dma("gpsimd", mix_hbm[j, cc * 128:(cc + 1) * 128, tcol * 512:(tcol + 1) * 512],
                      mixb[:], deps=[a3], group="mxs")
    c.barrier()
    c.sbuf_free("mixb", "mixt", "xxx_t", "w2s", "w1s", "mg3", "mg2", "mg1", "mg0", "xx_sb", "xin_sb", "maac")

    sc1 = c.sbuf("sc1", [128, 1024], F32, side="right")
    sc2 = c.sbuf("sc2", [128, Tn], F32, side="right")
    scb = c.sbuf("scb", [128, 1024], BF16, side="right")
    mx_sb = c.sbuf("mx_sb", [128, 8 * Tn], BF16)
    wbig = [c.sbuf(f"wbig{i}", [128, 8 * 512], BF16) for i in range(2)]
    mxl = [c.gtok("mxs")]

    def proj(mix_idx, w_in, out_h):
        c.barrier()
        c.dma("sync", mx_sb[:].rearrange("p (cc t) -> p cc t", t=Tn),
              mix_hbm[mix_idx].rearrange("(cc p) t -> p cc t", p=128), deps=mxl, group="lmx")
        lmx = [c.gtok("lmx")]
        wwar = {0: [], 1: []}
        prev = []
        for oc in range(2):
            c.dma("gpsimd", wbig[oc][:].rearrange("p (cc n) -> p cc n", n=512),
                  w_in[:, oc * 512:(oc + 1) * 512].rearrange("(cc p) n -> p cc n", p=128),
                  deps=wwar[oc], group=f"wld{oc}")
            lw = [c.gtok(f"wld{oc}")]
            for ob in range(4):
                for tcol in range(Tn // 512):
                    mmk = None
                    for cc in range(8):
                        wst = wbig[oc][:, cc * 512 + ob * 128: cc * 512 + (ob + 1) * 128]
                        c.raw("tensor", lambda t, wst=wst: t.ldweights(wst),
                              deps=(lw + lmx + prev) if cc == 0 else [])
                        mmk = c.op("tensor", lambda t, wst=wst, cc=cc, tcol=tcol: t.matmul(
                            PA[:, 0:512], wst, mx_sb[:, cc * Tn + tcol * 512: cc * Tn + tcol * 512 + 512],
                            start=(cc == 0), stop=(cc == 7)))
                    e1 = c.op("vector", lambda v: v.tensor_copy(scb[:, 0:512], PA[:, 0:512]),
                              deps=[mmk] + prev)
                    c.dma("gpsimd", out_h[(oc * 4 + ob) * 128:(oc * 4 + ob + 1) * 128,
                                          tcol * 512:(tcol + 1) * 512], scb[:, 0:512], deps=[e1], group="pst")
                    prev = [e1, c.gtok("pst")]
    proj(3, wr, r_hbm)
    proj(1, wk, k_hbm)
    proj(2, wv, v_hbm)
    t1cb = c.sbuf("t1cb", [64, Tn], BF16)
    dw1s = c.sbuf("dw1s", [128, 8 * 64], BF16)
    dw2us = c.sbuf("dw2us", [64, D], BF16)
    dcolu = c.sbuf("dcolu", [128, 8], F32)
    for cc in range(8):
        c.dma("gpsimd", dw1s[:, cc * 64:(cc + 1) * 64], dw1[cc * 128:(cc + 1) * 128, :], group="li4")
        c.dma("sync", dcolu[:, cc:cc + 1], decay_u[cc * 128:(cc + 1) * 128].rearrange("(p one) -> p one", one=1),
              group="li4b")
    c.dma("gpsimd", dw2us[:], dw2[:], group="li4")
    li4 = [c.gtok("li4"), c.gtok("li4b")]
    c.barrier()
    c.dma("sync", mx_sb[:].rearrange("p (cc t) -> p cc t", t=Tn),
          mix_hbm[0].rearrange("(cc p) t -> p cc t", p=128), group="lmx")
    lmx = [c.gtok("lmx")]
    prev = []
    for tcol in range(Tn // 512):
        mmk = None
        for cc in range(8):
            wst = dw1s[:, cc * 64:(cc + 1) * 64]
            c.raw("tensor", lambda t, wst=wst: t.ldweights(wst), deps=(li4 + lmx + prev) if cc == 0 else [])
            mmk = c.op("tensor", lambda t, wst=wst, cc=cc, tcol=tcol: t.matmul(
                PA[0:64, 0:512], wst, mx_sb[:, cc * Tn + tcol * 512: cc * Tn + tcol * 512 + 512],
                start=(cc == 0), stop=(cc == 7)))
        a1 = c.op("scalar", lambda s, tcol=tcol: s.activation(sc1[0:64, 0:512],
                                                              PA[0:64, 0:512], AF.Tanh), deps=[mmk] + prev)
        prev = [c.op("vector", lambda v, tcol=tcol: v.tensor_copy(t1cb[:, tcol * 512:(tcol + 1) * 512],
                                                                  sc1[0:64, 0:512]),
                     deps=[a1])]
    c.barrier()
    prevw = []
    for cc in range(8):
        for tcol in range(Tn // 512):
            mmk = c.op("tensor", lambda t, cc=cc, tcol=tcol: (t.ldweights(dw2us[:, cc * 128:(cc + 1) * 128]),
                       t.matmul(PA[:, 512:1024], dw2us[:, cc * 128:(cc + 1) * 128],
                                t1cb[0:64, tcol * 512:(tcol + 1) * 512], start=True, stop=True))[1],
                       deps=prev + prevw)
            a1 = c.op("scalar", lambda s, cc=cc: s.activation(sc2[:, 0:512], PA[:, 512:1024], AF.Exp,
                                                              bias=dcolu[:, cc:cc + 1]), deps=[mmk] + li4 + prevw)
            a2 = c.op("scalar", lambda s: s.activation(sc1[:, 0:512], sc2[:, 0:512], AF.Exp, scale=-1.0),
                      deps=[a1])
            c.dma("gpsimd", wd_hbm[cc * 128:(cc + 1) * 128, tcol * 512:(tcol + 1) * 512], sc1[:, 0:512],
                  deps=[a2], group="wds")
            prevw = [a2, c.gtok("wds")]
    c.barrier()
    c.sbuf_free("dcolu", "dw2us", "dw1s", "t1cb", "wbig1", "wbig0", "mx_sb")

    r_sb = c.sbuf("r_sb", [128, Tn], BF16)
    k_sb = c.sbuf("k_sb", [128, Tn], BF16)
    wd_sb = c.sbuf("wd_sb", [128, Tn], F32)
    ucol = c.sbuf("ucol", [128, 8], F32)
    vjb = [c.sbuf(f"vjb{i}", [128, Tn], BF16) for i in range(2)]
    kv_sb = c.sbuf("kv_sb", [128, Tn], F32)
    Ssc = c.sbuf("Ssc", [128, Tn + 1], F32)
    yt = c.sbuf("yt", [128, Tn], BF16)
    seg8i = c.inp("seg8i", [128, 8], BF16)
    seg8 = c.sbuf("seg8", [128, 8], BF16)
    rowb = c.sbuf("rowb", [8, Tn], BF16)
    cumw = c.sbuf("cumw", [128, Tn + 1], F32)
    rcb = c.sbuf("rcb", [128, Tn], BF16)
    sebuf = c.sbuf("sebuf", [128, HS], F32)
    zeros_t = c.sbuf("zeros_t", [128, Tn], BF16)
    c.dma("sync", seg8[:], seg8i[:], group="li6")
    for cc in range(8):
        c.dma("sync", ucol[:, cc:cc + 1], u_in[cc * 128:(cc + 1) * 128].rearrange("(p one) -> p one", one=1),
              group="li6")
    li6 = [c.gtok("li6")]
    zz = c.op("vector", lambda v: v.memset(zeros_t[:], 0.0))
    for cc in range(8):
        c.barrier()
        c.dma("sync", r_sb[:], r_hbm[cc * 128:(cc + 1) * 128, :], deps=[c.gtok("pst")], group="lsc")
        c.dma("sync", k_sb[:], k_hbm[cc * 128:(cc + 1) * 128, :], group="lsc")
        c.dma("sync", wd_sb[:], wd_hbm[cc * 128:(cc + 1) * 128, :], deps=[c.gtok("wds")], group="lsc")
        lsc = [c.gtok("lsc")]
        zc1 = c.op("vector", lambda v: v.memset(cumw[:, 0:1], 1.0), deps=lsc)
        zc2 = c.op("vector", lambda v: v.tensor_tensor_scan(cumw[:, 1:Tn + 1], wd_sb[:], zeros_t[:],
                                                            1.0, ALU.mult, ALU.add), deps=[zc1, zz] + lsc)
        zc3 = c.op("vector", lambda v: v.tensor_tensor(rcb[:], r_sb[:], cumw[:, 0:Tn], ALU.mult),
                   deps=[zc2])
        c.dma("gpsimd", rco[cc * 128:(cc + 1) * 128, :], rcb[:], deps=[zc3], group="rcs")
        prevj = []
        for j in range(HS):
            vb = vjb[j % 2]
            c.dma("sync", vb[:],
                  mkap(v_hbm[0:1, 0:1], (cc * 128 + j) * Tn, [[16 * Tn, 8], [0, 16], [1, Tn]]),
                  deps=[c.gtok("pst")] + prevj, group="lvj")
            lvt = [c.gtok("lvj")]
            kv1 = c.op("vector", lambda v, vb=vb: v.tensor_tensor(kv_sb[:], k_sb[:], vb[:], ALU.mult),
                       deps=lvt + lsc + prevj)
            zs = c.op("vector", lambda v: v.memset(Ssc[:, 0:1], 0.0), deps=prevj)
            sc = c.op("vector", lambda v: v.tensor_tensor_scan(Ssc[:, 1:Tn + 1], wd_sb[:], kv_sb[:],
                                                               0.0, ALU.mult, ALU.add), deps=[kv1, zs])
            tt1 = c.op("vector", lambda v, cc=cc: v.scalar_tensor_tensor(sc2[:], kv_sb[:], ucol[:, cc:cc + 1],
                                                                         Ssc[:, 0:Tn], ALU.mult, ALU.add),
                       deps=[sc] + li6)
            tt2 = c.op("vector", lambda v: v.tensor_tensor(yt[:], sc2[:], r_sb[:], ALU.mult), deps=[tt1])
            mms = []
            for tcol in range(Tn // 512):
                mk = c.op("tensor", lambda t, tcol=tcol: (t.ldweights(seg8[:]), t.matmul(
                    PB[0:8, 0:512], seg8[:], yt[:, tcol * 512:(tcol + 1) * 512], start=True, stop=True))[1],
                    deps=[tt2] + li6 + mms)
                mms.append(c.op("vector", lambda v, tcol=tcol: v.tensor_copy(
                    rowb[0:8, tcol * 512:(tcol + 1) * 512], PB[0:8, 0:512]), deps=[mk]))
            c.dma("gpsimd", mkap(y1o[0:1, 0:1], (cc * 128 + j) * Tn, [[16 * Tn, 8], [1, Tn]]),
                  rowb[0:8, :], deps=mms, group="yst")
            se1 = c.op("vector", lambda v, j=j: v.tensor_copy(sebuf[:, j:j + 1], Ssc[:, Tn:Tn + 1]),
                       deps=[sc])
            prevj = mms + [se1, c.gtok("yst"), kv1, tt2]
        c.dma("gpsimd", seo[cc * 128:(cc + 1) * 128, :], sebuf[:], deps=prevj, group="ses")
    c.wait_all("gpsimd", [c.gtok("yst"), c.gtok("ses"), c.gtok("rcs")])
    return c.finish()



def build_rwkv_b(Tn=T, fuse_glue=False):
    HS = 16
    c = Ctx("rwkvb")
    y1 = c.inp("y1", [D, Tn], BF16)
    rc = c.inp("rc", [D, Tn], BF16)
    sprev = c.inp("sprev", [D, HS])
    x = c.inp("x", [D, Tn])
    ln_g = c.inp("ln_g", [D])
    ln_b = c.inp("ln_b", [D])
    wo = c.inp("wo", [D, D])
    seg8i = c.inp("seg8i", [128, 8], BF16)
    xo = c.outp("xo", [D, Tn])
    corr_hbm = c.dram("corr_hbm", [D, Tn], BF16)

    onesr = c.sbuf("onesr", [1, 128], F32, side="right")
    onesc = c.sbuf("onesc", [128, 1], F32, side="right")
    onescb = c.sbuf("onescb", [128, 1], BF16, side="right")
    sc1 = c.sbuf("sc1", [128, 512], F32, side="right")
    PA = c.psum("PA", [128, 2048])
    PB = c.psum("PB", [128, 2048])
    t_o1 = c.op("gpsimd", lambda g: g.memset(onesr[:], 1.0))
    t_o2 = c.op("gpsimd", lambda g: g.memset(onesc[:], 1.0))
    t_o3 = c.op("gpsimd", lambda g: g.memset(onescb[:], 1.0))
    epsc, epst = c.constcol(EPS)

    rc_sb = c.sbuf("rc_sb", [128, Tn], BF16)
    sp_sb = c.sbuf("sp_sb", [128, HS], F32)
    seg8 = c.sbuf("seg8", [128, 8], BF16)
    rowb = c.sbuf("rowb", [8, Tn], BF16)
    tmpb = c.sbuf("tmpb", [128, Tn], BF16)
    c.dma("sync", seg8[:], seg8i[:], group="li")
    li = [c.gtok("li")]
    for cc in range(8):
        c.barrier()
        c.dma("sync", rc_sb[:], rc[cc * 128:(cc + 1) * 128, :], group="lrc")
        c.dma("sync", sp_sb[:], sprev[cc * 128:(cc + 1) * 128, :], group="lrc")
        lrc = [c.gtok("lrc")]
        for j in range(HS):
            t1 = c.op("vector", lambda v, j=j: v.tensor_scalar(tmpb[:], rc_sb[:], sp_sb[:, j:j + 1], None,
                                                               ALU.mult), deps=lrc + [c.gtok("cst")])
            mms = []
            for tcol in range(Tn // 512):
                mk = c.op("tensor", lambda t, tcol=tcol: (t.ldweights(seg8[:]), t.matmul(
                    PB[0:8, 0:512], seg8[:], tmpb[:, tcol * 512:(tcol + 1) * 512], start=True, stop=True))[1],
                    deps=[t1] + li + mms)
                mms.append(c.op("vector", lambda v, tcol=tcol: v.tensor_copy(
                    rowb[0:8, tcol * 512:(tcol + 1) * 512], PB[0:8, 0:512]), deps=[mk]))
            c.dma("gpsimd", mkap(corr_hbm[0:1, 0:1], (cc * 128 + j) * Tn, [[16 * Tn, 8], [1, Tn]]),
                  rowb[0:8, :], deps=mms, group="cst")
    c.barrier()
    c.sbuf_free("tmpb", "rowb", "seg8", "sp_sb", "rc_sb")

    wosb = c.sbuf("wosb", [128, 8 * D], BF16)
    gbcol = c.sbuf("gbcol", [128, 16], F32)
    ycc = c.sbuf("ycc", [128, 8 * 512], F32)
    ynb = c.sbuf("ynb", [128, 8 * 512], BF16)
    zcc = c.sbuf("zcc", [128, 8 * 512], F32)
    xcc = c.sbuf("xcc", [128, 8 * 512], F32)
    sqq = c.sbuf("sqq", [128, 8 * 512], BF16)
    t1b = c.sbuf("t1b", [128, 512], BF16)
    t2b = c.sbuf("t2b", [128, 512], BF16)
    yrow = c.sbuf("yrow", [1, 512], F32)
    y2row = c.sbuf("y2row", [1, 512], F32)
    myb = c.sbuf("myb", [128, 512], F32)
    ryb = c.sbuf("ryb", [128, 512], F32)
    xot = c.sbuf("xot", [128, 512], F32)
    c.dma("gpsimd", wosb[:].rearrange("p (cc n) -> p cc n", n=D),
          wo.rearrange("(cc p) n -> p cc n", p=128), group="lwo")
    for cc in range(8):
        c.dma("sync", gbcol[:, cc:cc + 1], ln_g[cc * 128:(cc + 1) * 128].rearrange("(p one) -> p one", one=1),
              group="lwo2")
        c.dma("sync", gbcol[:, 8 + cc:9 + cc], ln_b[cc * 128:(cc + 1) * 128].rearrange("(p one) -> p one", one=1),
              group="lwo2")
    lwo = [c.gtok("lwo"), c.gtok("lwo2")]

    def ln_stats_rows(src_f32, srcq_bf, deps):
        st1 = None
        st2 = None
        for cc in range(8):
            st1 = c.op("tensor", lambda t, cc=cc: t.matmul(PB[0:1, 0:512], onesc[:],
                                                           src_f32[:, cc * 512:(cc + 1) * 512],
                                                           start=(cc == 0), stop=(cc == 7)),
                       deps=deps + [t_o2] + ([st1] if st1 else []))
            st2 = c.op("tensor", lambda t, cc=cc: t.matmul(PB[0:1, 512:1024], onescb[:],
                                                           srcq_bf[:, cc * 512:(cc + 1) * 512],
                                                           start=(cc == 0), stop=(cc == 7)),
                       deps=deps + [t_o3] + ([st2] if st2 else []))
        e1 = c.op("vector", lambda v: v.tensor_scalar(y2row[:], PB[0:1, 512:1024], 1.0 / D, None, ALU.mult),
                  deps=[st2])
        e2 = c.op("vector", lambda v: v.tensor_scalar(yrow[:], PB[0:1, 0:512], 1.0 / D, None, ALU.mult),
                  deps=[st1])
        e3 = c.op("vector", lambda v: v.tensor_tensor(myb[0:1, 0:512], yrow[:], yrow[:], ALU.mult), deps=[e2])
        e4 = c.op("vector", lambda v: v.tensor_tensor(y2row[:], y2row[:], myb[0:1, 0:512], ALU.subtract),
                  deps=[e1, e3])
        e5 = c.op("scalar", lambda s: s.activation(y2row[:], y2row[:], AF.Sqrt, bias=epsc[0:1, 0:1]),
                  deps=[e4, epst])
        e6 = c.op("vector", lambda v: v.reciprocal(y2row[:], y2row[:]), deps=[e5])
        b3 = c.op("tensor", lambda t: t.matmul(PB[:, 1024:1536], onesr[:], yrow[:], start=True, stop=True),
                  deps=[e6, t_o1])
        b3c = c.op("vector", lambda v: v.tensor_copy(myb[:], PB[:, 1024:1536]), deps=[b3])
        b4 = c.op("tensor", lambda t: t.matmul(PB[:, 1024:1536], onesr[:], y2row[:], start=True, stop=True),
                  deps=[b3c])
        b4c = c.op("vector", lambda v: v.tensor_copy(ryb[:], PB[:, 1024:1536]), deps=[b4])
        return [b3c, b4c]

    for qcol in range(Tn // 512):
        c.barrier()
        qs = qcol * 512
        adds = []
        for cc in range(8):
            c.dma("sync", t1b[:], y1[cc * 128:(cc + 1) * 128, qs:qs + 512], deps=adds, group="lyq")
            c.dma("sync", t2b[:], corr_hbm[cc * 128:(cc + 1) * 128, qs:qs + 512],
                  deps=[c.gtok("cst")], group="lyq")
            ly = [c.gtok("lyq")]
            a1 = c.op("vector", lambda v, cc=cc: v.tensor_tensor(ycc[:, cc * 512:(cc + 1) * 512],
                                                                 t1b[:], t2b[:], ALU.add), deps=ly)
            a2 = c.op("scalar", lambda s, cc=cc: s.activation(sqq[:, cc * 512:(cc + 1) * 512],
                                                              ycc[:, cc * 512:(cc + 1) * 512], AF.Square),
                      deps=[a1])
            adds += [a1, a2]
        bt = ln_stats_rows(ycc, sqq, adds)
        yns = []
        for cc in range(8):
            n1 = c.op("vector", lambda v, cc=cc: v.tensor_tensor(sc1[:, 0:512],
                                                                 ycc[:, cc * 512:(cc + 1) * 512], myb[:],
                                                                 ALU.subtract), deps=bt + yns[-1:])
            n2 = c.op("vector", lambda v: v.tensor_tensor(sc1[:, 0:512], sc1[:, 0:512], ryb[:], ALU.mult),
                      deps=[n1])
            yns.append(c.op("vector", lambda v, cc=cc: v.tensor_scalar(
                ynb[:, cc * 512:(cc + 1) * 512], sc1[:, 0:512], gbcol[:, cc:cc + 1],
                gbcol[:, 8 + cc:9 + cc], ALU.mult, ALU.add), deps=[n2] + lwo))
        PBANK = [PA[:, 0:512], PA[:, 512:1024], PA[:, 1024:1536], PA[:, 1536:2048],
                 PB[:, 0:512], PB[:, 512:1024], PB[:, 1024:1536], PB[:, 1536:2048]]
        mm = None
        for occ in range(8):
            for cc in range(8):
                wst = wosb[:, cc * D + occ * 128: cc * D + occ * 128 + 128]
                c.raw("tensor", lambda t, wst=wst: t.ldweights(wst),
                      deps=(yns + lwo + bt) if (occ == 0 and cc == 0) else [])
                mm = c.op("tensor", lambda t, wst=wst, occ=occ, cc=cc: t.matmul(
                    PBANK[occ], wst, ynb[:, cc * 512:(cc + 1) * 512], start=(cc == 0), stop=(cc == 7)))
        c.dma("sync", xcc[:].rearrange("p (cc n) -> p cc n", n=512),
              x[:, qs:qs + 512].rearrange("(cc p) n -> p cc n", p=128), group="lxq")
        lxt = [c.gtok("lxq")]
        ztoks = []
        sqs = []
        for occ in range(8):
            z1 = c.op("vector", lambda v, occ=occ: v.tensor_copy(zcc[:, occ * 512:(occ + 1) * 512],
                                                                 PBANK[occ]), deps=[mm] + sqs[-1:])
            ztoks.append(z1)
            sqs.append(c.op("scalar", lambda s, occ=occ: s.activation(sqq[:, occ * 512:(occ + 1) * 512],
                                                                      zcc[:, occ * 512:(occ + 1) * 512],
                                                                      AF.Square), deps=[z1]))
        bt2 = ln_stats_rows(zcc, sqq, ztoks + sqs)
        sto = []
        for cc in range(8):
            u1 = c.op("vector", lambda v, cc=cc: v.tensor_tensor(xot[:], zcc[:, cc * 512:(cc + 1) * 512],
                                                                 myb[:], ALU.subtract),
                      deps=bt2 + sto + [c.gtok("xst")])
            u2 = c.op("vector", lambda v: v.tensor_tensor(xot[:], xot[:], ryb[:], ALU.mult), deps=[u1])
            u3 = c.op("vector", lambda v, cc=cc: v.tensor_tensor(xot[:], xot[:], xcc[:, cc * 512:(cc + 1) * 512],
                                                                 ALU.add), deps=[u2] + lxt)
            sto = [u3]
            c.dma("gpsimd", xo[cc * 128:(cc + 1) * 128, qs:qs + 512], xot[:], deps=[u3], group="xst")
    c.wait_all("gpsimd", [c.gtok("xst")])
    if fuse_glue:
        gxo = c.outp("gxo", [D, Tn])
        append_glue(c, PA[:, 0:1024], xo, Tn,
                    (ycc, sqq, yrow, y2row, myb, ryb, xot, onesr, onesc, onescb),
                    f32_out=gxo)
    return c.finish()



def build_l1f_layer(Tn=T):
    A = 513 + Tn
    KV = 512 + Tn
    NW = Tn // 256 + 1
    NT = KV // 128
    ATN = Tn + 128
    FH2 = 2816
    NHC = FH2 // 128
    c = Ctx("l1f")
    xa = c.inp("xa", [D, A])
    wqkv = c.inp("wqkv", [D, 3072])
    wo = c.inp("wo", [D, D])
    lng = c.inp("lng", [D])
    lnb = c.inp("lnb", [D])
    w1a = c.inp("w1a", [D, FH2])
    w1g = c.inp("w1g", [D, FH2])
    w2 = c.inp("w2", [FH2, D])
    b1a = c.inp("b1a", [FH2])
    b1g = c.inp("b1g", [FH2])
    b2 = c.inp("b2", [D])
    tabq = c.inp("tabq", [2, KV, 64])
    tabk = c.inp("tabk", [2, KV, 64])
    wmq = c.inp("wmq", [3, 256, 512])
    rolec = c.inp("rolec", [128, 1])
    xo2 = c.outp("xo2", [D, Tn])

    qn_h = c.dram("qn_h", [8, KV, 128], BF16)
    kn_h = c.dram("kn_h", [8, KV, 128], BF16)
    v_h = c.dram("v_h", [KV, D], BF16)
    qcm_h = c.dram("qcm_h", [D, KV], BF16)
    kcm_h = c.dram("kcm_h", [D, KV], BF16)
    at_h = c.dram("at_h", [8, ATN, 128], BF16)
    x2_h = c.dram("x2_h", [D, Tn + 1], F32)
    w1g_h = c.dram("w1g_h", [D, FH2], BF16)
    w2_h = c.dram("w2_h", [FH2, D], BF16)

    onesr = c.sbuf("onesr", [1, 128], F32, side="right")
    onesc = c.sbuf("onesc", [128, 1], F32, side="right")
    onescb = c.sbuf("onescb", [128, 1], BF16, side="right")
    idb = c.sbuf("idb", [128, 128], BF16, side="right")
    wmq_sb = c.sbuf("wmq_sb", [128, 3 * 1024], F32, side="right")
    role_sb = c.sbuf("role_sb", [128, 1], F32, side="right")
    t_o1 = c.op("gpsimd", lambda g: g.memset(onesr[:], 1.0))
    t_o2 = c.op("gpsimd", lambda g: g.memset(onesc[:], 1.0))
    t_o3 = c.op("gpsimd", lambda g: g.memset(onescb[:], 1.0))
    epsc, epst = c.constcol(EPS)
    sid1 = c.sbuf("sid1", [128, 128], F32, side="right")
    sid2 = c.sbuf("sid2", [128, 128], F32, side="right")
    ii1 = c.op("gpsimd", lambda g: g.iota(sid1[:], [[1, 128]], channel_multiplier=0,
                                          allow_small_or_imprecise_dtypes=True))
    ii2 = c.op("gpsimd", lambda g: g.iota(sid2[:], [[0, 128]], channel_multiplier=1,
                                          allow_small_or_imprecise_dtypes=True))
    idt = c.op("vector", lambda v: v.tensor_tensor(idb[:], sid1[:], sid2[:], ALU.is_equal),
               deps=[ii1, ii2])
    for s in range(3):
        for qsub in range(2):
            c.dma("sync", wmq_sb[:, s * 1024 + qsub * 512:(s * 1024 + qsub * 512) + 512],
                  wmq[s, qsub * 128:(qsub + 1) * 128, :], group="lwm")
    c.dma("sync", role_sb[:], rolec[:], group="lwm")
    lwm = [c.gtok("lwm")]

    c.dma("gpsimd", w1g_h[:, :], w1g[:, :], group="wcast")
    c.dma("gpsimd", w2_h[:, :], w2[:, :], group="wcast")

    wqs = c.sbuf("wqs", [128, 8 * 3072], BF16)
    HC = (NT // 2) * 128
    xab = c.sbuf("xab", [128, 8 * (HC + 1)], BF16)
    tbt = c.sbuf("tbt", [128, 4 * 64], F32)
    sqt = c.sbuf("sqt", [128, 1024], F32)
    ssq = c.sbuf("ssq", [128, 32], F32)
    qnt = c.sbuf("qnt", [128, 1024], F32)
    rot = c.sbuf("rot", [128, 1024], F32)
    tt1 = c.sbuf("tt1", [128, 1024], F32)
    qfb = c.sbuf("qfb", [128, 1024], BF16)
    kfb = c.sbuf("kfb", [128, 1024], BF16)
    vfb = c.sbuf("vfb", [128, 1024], BF16)
    PA = c.psum("PA", [128, 2048])
    PB = c.psum("PB", [128, 1024])
    PT = c.psum("PT", [128, 512], BF16)
    for cc in range(8):
        c.dma("gpsimd", wqs[:, cc * 3072:(cc + 1) * 3072], wqkv[cc * 128:(cc + 1) * 128, :],
              group="lw")
    lw = [c.gtok("lw")]

    def qk_post(src_ps, tb_base, outb, deps):
        v3 = src_ps.rearrange("p (h d) -> p h d", d=64)
        s1 = c.op("scalar", lambda s: s.activation(sqt[:], src_ps, AF.Square), deps=deps)
        s2 = c.op("vector", lambda v: v.reduce_sum(ssq[:, 0:16], sqt[:].rearrange(
            "p (h d) -> p h d", d=64), axis=AX.X), deps=[s1])
        s3 = c.op("scalar", lambda s: s.activation(ssq[:, 16:32], ssq[:, 0:16], AF.Sqrt), deps=[s2])
        s4 = c.op("vector", lambda v: v.tensor_scalar(ssq[:, 16:32], ssq[:, 16:32], 1e-12, None,
                                                      ALU.max), deps=[s3])
        s5 = c.op("vector", lambda v: v.reciprocal(ssq[:, 16:32], ssq[:, 16:32]), deps=[s4])
        rq = ssq[:, 16:32].unsqueeze(2).broadcast_to([128, 16, 64])
        s6 = c.op("vector", lambda v: v.tensor_tensor(qnt[:].rearrange("p (h d) -> p h d", d=64),
                                                      v3, rq, ALU.mult), deps=[s5])
        cs = tbt[:, tb_base:tb_base + 64].unsqueeze(1).broadcast_to([128, 16, 64])
        snL = mkap(tbt[:, 0:1], tb_base + 64, [[4 * 64, 128], [0, 16], [1, 32]])
        snR = mkap(tbt[:, 0:1], tb_base + 64 + 32, [[4 * 64, 128], [0, 16], [1, 32]])
        qnL = mkap(qnt[:, 0:1], 0, [[1024, 128], [64, 16], [1, 32]])
        qnR = mkap(qnt[:, 0:1], 32, [[1024, 128], [64, 16], [1, 32]])
        rotL = mkap(rot[:, 0:1], 0, [[1024, 128], [64, 16], [1, 32]])
        rotR = mkap(rot[:, 0:1], 32, [[1024, 128], [64, 16], [1, 32]])
        r1 = c.op("gpsimd", lambda g: g.tensor_tensor(rotL, qnR, snL, ALU.mult),
                  deps=[s6, c.gtok("tbl")])
        r2 = c.op("gpsimd", lambda g: g.tensor_tensor(rotR, qnL, snR, ALU.mult), deps=[s6])
        r3 = c.op("vector", lambda v: v.tensor_tensor(tt1[:].rearrange("p (h d) -> p h d", d=64),
                                                      qnt[:].rearrange("p (h d) -> p h d", d=64),
                                                      cs, ALU.mult), deps=[s6, c.gtok("tbl")])
        r4 = c.op("gpsimd", lambda g: g.tensor_tensor(outb[:], tt1[:], rot[:], ALU.add),
                  deps=[r1, r2, r3])
        return [r4, s1, s6]

    pa_free = []
    for half in range(2):
        c.barrier()
        cb = half * HC
        for cc in range(8):
            c.dma("gpsimd", xab[:, cc * (HC + 1):(cc + 1) * (HC + 1)],
                  xa[cc * 128:(cc + 1) * 128, cb:cb + HC + 1], group="lxa")
        lxa = [c.gtok("lxa")]
        for tt in range(half * (NT // 2), (half + 1) * (NT // 2)):
            kk = tt * 128
            c.dma("sync", tbt[:, 0:64], tabq[0, kk:kk + 128, :],
                  deps=[c.gtok("Pgpsimd"), c.gtok("Pvector")], group="tbl")
            c.dma("sync", tbt[:, 64:128], tabq[1, kk:kk + 128, :], group="tbl")
            c.dma("sync", tbt[:, 128:192], tabk[0, kk:kk + 128, :], group="tbl")
            c.dma("sync", tbt[:, 192:256], tabk[1, kk:kk + 128, :], group="tbl")
            tl = c.gtok("tbl")
            mmdep = list(pa_free)
            mm = None
            for cc in range(8):
                loc = cc * (HC + 1) + (kk - cb) + (1 if cc < 4 else 0)
                st = xab[:, loc:loc + 128]
                c.raw("tensor", lambda t, st=st: t.ldweights(st),
                      deps=(lxa + lw + mmdep) if cc == 0 else [])
                for nn in range(6):
                    dst = PA[:, nn * 512:(nn + 1) * 512] if nn < 4 else PB[:, (nn - 4) * 512:(nn - 3) * 512]
                    mm = c.op("tensor", lambda t, st=st, cc=cc, nn=nn, dst=dst: t.matmul(
                        dst, st, wqs[:, cc * 3072 + nn * 512:cc * 3072 + (nn + 1) * 512],
                        start=(cc == 0), stop=(cc == 7)))
            qd = qk_post(PA[:, 0:1024], 0, qfb, [mm, c.gtok("sq"), tl])
            kd = qk_post(PA[:, 1024:2048], 128, kfb, qd + [c.gtok("sk")])
            vd = c.op("scalar", lambda s: s.activation(vfb[:], PB[:, 0:1024], AF.Copy, bias=0.0),
                      deps=[mm, c.gtok("sv")] + kd)
            pa_free = [qd[1], qd[2], kd[1], kd[2], vd]
            c.dma("sync", mkap(qn_h[0:1, 0:1, 0:1], kk * 128,
                               [[128, 128], [KV * 128, 8], [1, 128]]),
                  qfb[:].rearrange("p (cc n) -> p cc n", n=128), deps=qd, group="sq")
            c.dma("sync", mkap(kn_h[0:1, 0:1, 0:1], kk * 128,
                               [[128, 128], [KV * 128, 8], [1, 128]]),
                  kfb[:].rearrange("p (cc n) -> p cc n", n=128), deps=kd, group="sk")
            c.dma("sync", v_h[kk:kk + 128, :], vfb[:], deps=[vd], group="sv")
    c.barrier()
    c.sbuf_free("xab", "tbt", "sqt", "ssq", "qnt", "rot", "tt1", "qfb", "kfb", "vfb", "wqs")

    tch = c.sbuf("tch", [128, KV], BF16)
    tch2 = c.sbuf("tch2", [128, KV], BF16)
    for cc2 in range(8):
        buf = tch if cc2 % 2 == 0 else tch2
        gq = "tq%d" % (cc2 % 2)
        c.dma("sync", buf[:], qn_h[cc2].rearrange("t p -> t p"), deps=[c.gtok(gq)],
              transpose=True, group=gq)
        c.dma("sync", qcm_h[cc2 * 128:(cc2 + 1) * 128, :], buf[:], deps=[c.gtok(gq)], group=gq)
    for cc2 in range(8):
        buf = tch if cc2 % 2 == 0 else tch2
        gq = "tq%d" % (cc2 % 2)
        c.dma("sync", buf[:], kn_h[cc2].rearrange("t p -> t p"), deps=[c.gtok(gq)],
              transpose=True, group=gq)
        c.dma("sync", kcm_h[cc2 * 128:(cc2 + 1) * 128, :], buf[:], deps=[c.gtok(gq)], group=gq)
    c.barrier()
    c.sbuf_free("tch", "tch2")

    qwin2 = c.sbuf("qwin2", [128, 2 * 8 * 256], BF16)
    kwin2 = c.sbuf("kwin2", [128, 2 * 8 * 512], BF16)
    vwin2 = c.sbuf("vwin2", [128, 2 * 4 * 1024], BF16)
    scq = c.sbuf("scq", [128, 2 * 512], F32)
    Eq = c.sbuf("Eq", [128, 2 * 512], BF16)
    ET = c.sbuf("ET", [128, 2 * 512], BF16)
    mcol = c.sbuf("mcol", [128, 4], F32)
    zcol = c.sbuf("zcol", [128, 4], F32)
    atb = c.sbuf("atb", [128, 2 * 1024], BF16)
    c.barrier()
    prevA = []
    jmm = {}
    for j in range(1, NW + 1):
        k0 = 256 * (j - 1)
        par = j % 2
        qwin = qwin2[:, par * 2048:(par + 1) * 2048]
        kwin = kwin2[:, par * 4096:(par + 1) * 4096]
        vwin = vwin2[:, par * 4096:(par + 1) * 4096]
        ldep = [jmm[j - 2]] if (j - 2) in jmm else []
        c.dma("sync", qwin.rearrange("p (hp n) -> p hp n", n=256),
              mkap(qcm_h[0:1, 0:1], k0 + 256, [[KV, 128], [128 * KV, 8], [1, 256]]),
              deps=ldep, group="lq%d" % par)
        c.dma("sync", kwin.rearrange("p (hp n) -> p hp n", n=512),
              mkap(kcm_h[0:1, 0:1], k0, [[KV, 128], [128 * KV, 8], [1, 512]]),
              group="lq%d" % par)
        c.dma("sync", vwin.rearrange("p (kt n) -> p kt n", n=1024),
              mkap(v_h[0:1, 0:1], k0 * 1024, [[1024, 128], [128 * 1024, 4], [1, 1024]]),
              group="lq%d" % par)
        lq = [c.gtok("lq%d" % par)]
        msel = 0 if j == 1 else (1 if j == 2 else 2)
        for h in range(16):
            hp, hr = h // 2, 64 * (h % 2)
            mmq = None
            for qsub in range(2):
                lhs = qwin[hr:hr + 64, hp * 256 + qsub * 128:hp * 256 + qsub * 128 + 128]
                c.raw("tensor", lambda t, lhs=lhs, hr=hr: t.ldweights(lhs, tile_position=(hr, 0)),
                      deps=lq + prevA)
                mmq = c.op("tensor", lambda t, lhs=lhs, qsub=qsub, hr=hr, hp=hp, kwin=kwin:
                           t.matmul(
                    PA[:, qsub * 512:(qsub + 1) * 512], lhs,
                    kwin[hr:hr + 64, hp * 512:(hp + 1) * 512], start=True, stop=True,
                    tile_position=(hr, 0)))
            prevA = []
            for qsub in range(2):
                a1 = c.op("vector", lambda v, qsub=qsub, msel=msel: v.scalar_tensor_tensor(
                    scq[:, qsub * 512:(qsub + 1) * 512], PA[:, qsub * 512:(qsub + 1) * 512], 8.0,
                    wmq_sb[:, msel * 1024 + qsub * 512:msel * 1024 + qsub * 512 + 512],
                    ALU.mult, ALU.add), deps=[mmq] + lwm)
                prevA.append(a1)
                a2 = c.op("vector", lambda v, qsub=qsub: v.reduce_max(
                    mcol[:, qsub:qsub + 1], scq[:, qsub * 512:(qsub + 1) * 512], axis=AX.X),
                    deps=[a1])
                a3 = c.op("vector", lambda v, qsub=qsub: v.tensor_scalar(
                    mcol[:, 2 + qsub:3 + qsub], mcol[:, qsub:qsub + 1], -1.0, None, ALU.mult),
                    deps=[a2])
                a4 = c.op("scalar", lambda s, qsub=qsub: s.activation(
                    Eq[:, qsub * 512:(qsub + 1) * 512], scq[:, qsub * 512:(qsub + 1) * 512],
                    AF.Exp, bias=mcol[:, 2 + qsub:3 + qsub],
                    accum_out=zcol[:, qsub:qsub + 1]), deps=[a3, c.gtok("Ptensor")])
                mmt = None
                for kt in range(4):
                    src = Eq[:, qsub * 512 + kt * 128:qsub * 512 + kt * 128 + 128]
                    c.raw("tensor", lambda t, src=src: t.ldweights(src),
                          deps=[a4, idt, c.gtok("Pscalar")])
                    mmt = c.op("tensor", lambda t, src=src, kt=kt: t.matmul(
                        PT[:, kt * 128:(kt + 1) * 128], src, idb[:], start=True, stop=True,
                        is_transpose=True))
                et = c.op("scalar", lambda s, qsub=qsub: s.activation(
                    ET[:, qsub * 512:(qsub + 1) * 512], PT[:], AF.Copy, bias=0.0), deps=[mmt])
                mva = None
                for kt in range(4):
                    lhs = ET[:, qsub * 512 + kt * 128:qsub * 512 + kt * 128 + 128]
                    c.raw("tensor", lambda t, lhs=lhs: t.ldweights(lhs), deps=[et])
                    mva = c.op("tensor", lambda t, lhs=lhs, kt=kt, qsub=qsub, h=h, vwin=vwin:
                               t.matmul(
                        PB[:, qsub * 512:qsub * 512 + 64], lhs,
                        vwin[:, kt * 1024 + h * 64:kt * 1024 + h * 64 + 64],
                        start=(kt == 0), stop=(kt == 3)))
                rz = c.op("vector", lambda v, qsub=qsub: v.reciprocal(
                    zcol[:, 2 + qsub:3 + qsub], zcol[:, qsub:qsub + 1]), deps=[a4])
                prevA.append(c.op("vector", lambda v, qsub=qsub, h=h: v.tensor_scalar(
                    atb[:, qsub * 1024 + h * 64:qsub * 1024 + h * 64 + 64],
                    PB[:, qsub * 512:qsub * 512 + 64], zcol[:, 2 + qsub:3 + qsub], None,
                    ALU.mult), deps=[mva, rz, c.gtok("sat")]))
        jmm[j] = c.gtok("Ptensor")
        row0 = 256 * j - 384
        for qsub in range(2):
            if j == 1 and qsub == 0:
                continue
            c.dma("sync", mkap(at_h[0:1, 0:1, 0:1], (row0 + qsub * 128) * 128,
                               [[128, 128], [ATN * 128, 8], [1, 128]]),
                  atb[:, qsub * 1024:(qsub + 1) * 1024].rearrange("p (cc n) -> p cc n", n=128),
                  deps=prevA, group="sat")
    c.barrier()
    c.sbuf_free("qwin2", "kwin2", "vwin2", "scq", "Eq", "ET", "mcol", "zcol", "atb")

    atcm = c.sbuf("atcm", [128, 8 * ATN], BF16)
    wos = c.sbuf("wos", [128, 8 * 1024], BF16)
    x2t = c.sbuf("x2t", [128, 512], F32)
    xrt = c.sbuf("xrt", [128, 512], F32)
    for cc2 in range(8):
        c.dma("sync", atcm[:, cc2 * ATN:(cc2 + 1) * ATN], at_h[cc2].rearrange("t p -> t p"),
              transpose=True, group="lat")
        c.dma("gpsimd", wos[:, cc2 * 1024:(cc2 + 1) * 1024], wo[cc2 * 128:(cc2 + 1) * 128, :],
              group="lwo")
    lat = [c.gtok("lat"), c.gtok("lwo")]
    PBANK = [PA[:, 0:512], PA[:, 512:1024], PA[:, 1024:1536], PA[:, 1536:2048],
             PB[:, 0:512], PB[:, 512:1024]]
    chunks = [(i * 512, 512) for i in range(ATN // 512)]
    if ATN % 512:
        chunks.append(((ATN // 512) * 512, ATN % 512))
    for (ts, tw) in chunks:
        c.barrier()
        mm = None
        etoks = []
        for occ in range(8):
            pb = PBANK[occ % 6]
            for cc in range(8):
                wst = wos[:, cc * 1024 + occ * 128:cc * 1024 + occ * 128 + 128]
                c.raw("tensor", lambda t, wst=wst: t.ldweights(wst),
                      deps=lat + ([etoks[occ - 6]] if occ >= 6 and cc == 0 else []))
                mm = c.op("tensor", lambda t, wst=wst, pb=pb, cc=cc, ts=ts, tw=tw: t.matmul(
                    pb[:, 0:tw], wst, atcm[:, cc * ATN + ts:cc * ATN + ts + tw],
                    start=(cc == 0), stop=(cc == 7)))
            lo = max(ts, 127)
            sh = lo - ts
            vw = ts + tw - lo
            c.dma("sync", xrt[:, 0:tw], xa[occ * 128:(occ + 1) * 128, ts + 385:ts + 385 + tw],
                  deps=[c.gtok("Pvector")], group="lxr")
            e1 = c.op("vector", lambda v, pb=pb, tw=tw: v.tensor_tensor(
                x2t[:, 0:tw], pb[:, 0:tw], xrt[:, 0:tw], ALU.add),
                deps=[mm, c.gtok("lxr"), c.gtok("sx2")])
            if ts == 0:
                e1 = c.op("vector", lambda v: v.tensor_scalar(
                    x2t[:, 127:128], x2t[:, 127:128], role_sb[:, 0:1], None, ALU.mult),
                    deps=[e1] + lwm)
            etoks.append(e1)
            c.dma("sync", x2_h[occ * 128:(occ + 1) * 128, lo - 127:lo - 127 + vw],
                  x2t[:, sh:sh + vw], deps=[e1], group="sx2")
    c.barrier()
    c.sbuf_free("atcm", "wos", "x2t", "xrt")

    w1as = c.sbuf("w1as", [128, 8 * FH2], BF16)
    w1gb = c.sbuf("w1gb", [128, 2 * 1024], BF16)
    w2s = c.sbuf("w2s", [128, NHC * 1024], BF16)
    bcols = c.sbuf("bcols", [128, 2 * NHC + 24], F32)
    xcc = c.sbuf("xcc", [128, 8 * 512], F32)
    xov4 = c.sbuf("xov4", [128, 4 * 512], F32)
    sqq = c.sbuf("sqq", [128, 8 * 512], BF16)
    xnb = c.sbuf("xnb", [128, 8 * 512], BF16)
    hbuf = c.sbuf("hbuf", [128, NHC * 512], BF16)
    sc1 = c.sbuf("sc1", [128, 512], F32)
    sc2 = c.sbuf("sc2", [128, 512], F32)
    yrow = c.sbuf("yrow", [1, 512], F32)
    y2row = c.sbuf("y2row", [1, 512], F32)
    myb = c.sbuf("myb", [128, 512], F32)
    ryb = c.sbuf("ryb", [128, 512], F32)
    xot = c.sbuf("xot", [128, 512], F32)
    for cc in range(8):
        c.dma("gpsimd", w1as[:, cc * FH2:(cc + 1) * FH2], w1a[cc * 128:(cc + 1) * 128, :],
              group="lw1")
    c.dma("sync", w2s[:].rearrange("p (hc n) -> p hc n", n=1024),
          w2_h.rearrange("(hc p) n -> p hc n", p=128), deps=[c.gtok("wcast")], group="lw2s")
    c.dma("sync", bcols[:, 0:NHC], b1a.rearrange("(hc p) -> p hc", p=128), group="lb", slow=True)
    c.dma("sync", bcols[:, NHC:2 * NHC], b1g.rearrange("(hc p) -> p hc", p=128), group="lb",
          slow=True)
    c.dma("sync", bcols[:, 2 * NHC:2 * NHC + 8], b2.rearrange("(cc p) -> p cc", p=128),
          group="lb", slow=True)
    c.dma("sync", bcols[:, 2 * NHC + 8:2 * NHC + 16], lng.rearrange("(cc p) -> p cc", p=128),
          group="lb", slow=True)
    c.dma("sync", bcols[:, 2 * NHC + 16:2 * NHC + 24], lnb.rearrange("(cc p) -> p cc", p=128),
          group="lb", slow=True)
    lb = [c.gtok("lb"), c.gtok("lw1"), c.gtok("wcast")]
    lw2s = c.gtok("lw2s")

    def ln_stats(deps):
        st1 = None
        st2 = None
        for cc in range(8):
            st1 = c.op("tensor", lambda t, cc=cc: t.matmul(
                PB[0:1, 0:512], onesc[:], xcc[:, cc * 512:(cc + 1) * 512],
                start=(cc == 0), stop=(cc == 7)), deps=deps + [t_o2] + ([st1] if st1 else []))
            st2 = c.op("tensor", lambda t, cc=cc: t.matmul(
                PB[0:1, 512:1024], onescb[:], sqq[:, cc * 512:(cc + 1) * 512],
                start=(cc == 0), stop=(cc == 7)), deps=deps + [t_o3] + ([st2] if st2 else []))
        e1 = c.op("vector", lambda v: v.tensor_scalar(y2row[:], PB[0:1, 512:1024], 1.0 / D, None,
                                                      ALU.mult), deps=[st2])
        e2 = c.op("vector", lambda v: v.tensor_scalar(yrow[:], PB[0:1, 0:512], 1.0 / D, None,
                                                      ALU.mult), deps=[st1])
        e3 = c.op("vector", lambda v: v.tensor_tensor(myb[0:1, 0:512], yrow[:], yrow[:], ALU.mult),
                  deps=[e2])
        e4 = c.op("vector", lambda v: v.tensor_tensor(y2row[:], y2row[:], myb[0:1, 0:512],
                                                      ALU.subtract), deps=[e1, e3])
        e5 = c.op("scalar", lambda s: s.activation(y2row[:], y2row[:], AF.Sqrt,
                                                   bias=epsc[0:1, 0:1]), deps=[e4, epst])
        e6 = c.op("vector", lambda v: v.reciprocal(y2row[:], y2row[:]), deps=[e5])
        b3 = c.op("tensor", lambda t: t.matmul(PB[:, 0:512], onesr[:], yrow[:], start=True,
                                               stop=True), deps=[e6, t_o1])
        b3c = c.op("vector", lambda v: v.tensor_copy(myb[:], PB[:, 0:512]), deps=[b3])
        b4 = c.op("tensor", lambda t: t.matmul(PB[:, 0:512], onesr[:], y2row[:], start=True,
                                               stop=True), deps=[b3c])
        b4c = c.op("vector", lambda v: v.tensor_copy(ryb[:], PB[:, 0:512]), deps=[b4])
        return [b3c, b4c]

    for qcol in range(Tn // 512):
        c.barrier()
        qs = qcol * 512
        lds = []
        for cc in range(8):
            off = (1 + qs) if cc < 4 else qs
            c.dma("sync", xcc[:, cc * 512:(cc + 1) * 512],
                  x2_h[cc * 128:(cc + 1) * 128, off:off + 512], group="lx2")
            if cc >= 4:
                c.dma("sync", xov4[:, (cc - 4) * 512:(cc - 3) * 512],
                      x2_h[cc * 128:(cc + 1) * 128, 1 + qs:1 + qs + 512], group="lx2")
        lx2 = [c.gtok("lx2")]
        sqs = []
        for cc in range(8):
            sqs.append(c.op("scalar", lambda s, cc=cc: s.activation(
                sqq[:, cc * 512:(cc + 1) * 512], xcc[:, cc * 512:(cc + 1) * 512], AF.Square),
                deps=lx2))
        bt = ln_stats(lx2 + sqs)
        yns = []
        for cc in range(8):
            n1 = c.op("vector", lambda v, cc=cc: v.tensor_tensor(
                sc1[:, 0:512], xcc[:, cc * 512:(cc + 1) * 512], myb[:], ALU.subtract),
                deps=bt + yns[-1:])
            n2 = c.op("vector", lambda v: v.tensor_tensor(sc1[:, 0:512], sc1[:, 0:512], ryb[:],
                                                          ALU.mult), deps=[n1])
            yns.append(c.op("vector", lambda v, cc=cc: v.tensor_scalar(
                xnb[:, cc * 512:(cc + 1) * 512], sc1[:, 0:512],
                bcols[:, 2 * NHC + 8 + cc:2 * NHC + 9 + cc],
                bcols[:, 2 * NHC + 16 + cc:2 * NHC + 17 + cc], ALU.mult, ALU.add),
                deps=[n2] + lb))
        hts = []
        hgmm = []
        for hc in range(NHC):
            par = hc % 2
            gld = c.dma("sync", w1gb[:, par * 1024:(par + 1) * 1024].rearrange(
                "p (cc n) -> p cc n", n=128),
                mkap(w1g_h[0:1, 0:1], hc * 128, [[FH2, 128], [128 * FH2, 8], [1, 128]]),
                deps=[hgmm[hc - 2]] if hc >= 2 else [], group="hg%d" % par)
            mm = None
            for cc in range(8):
                wst = w1as[:, cc * FH2 + hc * 128:cc * FH2 + hc * 128 + 128]
                c.raw("tensor", lambda t, wst=wst: t.ldweights(wst),
                      deps=(yns + lb + hts[-2:]) if cc == 0 else [])
                mm = c.op("tensor", lambda t, wst=wst, cc=cc: t.matmul(
                    PA[:, 0:512], wst, xnb[:, cc * 512:(cc + 1) * 512],
                    start=(cc == 0), stop=(cc == 7)))
            for cc in range(8):
                wst = w1gb[:, par * 1024 + cc * 128:par * 1024 + cc * 128 + 128]
                c.raw("tensor", lambda t, wst=wst: t.ldweights(wst),
                      deps=[gld] if cc == 0 else [])
                mm = c.op("tensor", lambda t, wst=wst, cc=cc: t.matmul(
                    PA[:, 512:1024], wst, xnb[:, cc * 512:(cc + 1) * 512],
                    start=(cc == 0), stop=(cc == 7)))
            g1 = c.op("scalar", lambda s, hc=hc: s.activation(
                sc2[:, 0:512], PA[:, 512:1024], AF.Gelu, bias=bcols[:, NHC + hc:NHC + hc + 1]),
                deps=[mm] + lb)
            g2 = c.op("vector", lambda v, hc=hc: v.tensor_scalar(
                sc1[:, 0:512], PA[:, 0:512], bcols[:, hc:hc + 1], None, ALU.add),
                deps=[mm] + lb)
            hgmm.append(mm)
            hts.append(c.op("gpsimd", lambda g, hc=hc: g.tensor_tensor(
                hbuf[:, hc * 512:(hc + 1) * 512], sc1[:, 0:512], sc2[:, 0:512], ALU.mult),
                deps=[g1, g2]))
        mm2 = None
        for hc in range(NHC):
            for occ in range(6):
                wst = w2s[:, hc * 1024 + occ * 128:hc * 1024 + occ * 128 + 128]
                c.raw("tensor", lambda t, wst=wst: t.ldweights(wst),
                      deps=[lw2s, hts[-1], c.gtok("Pscalar"), c.gtok("Pvector")]
                      if (hc == 0 and occ == 0) else [])
                mm2 = c.op("tensor", lambda t, wst=wst, occ=occ, hc=hc: t.matmul(
                    PA[:, occ * 512:(occ + 1) * 512] if occ < 4
                    else PB[:, (occ - 4) * 512:(occ - 3) * 512], wst,
                    hbuf[:, hc * 512:(hc + 1) * 512], start=(hc == 0), stop=(hc == NHC - 1)))
        sto = []
        for occ in range(8):
            if occ >= 6:
                continue
            pb = PA[:, occ * 512:(occ + 1) * 512] if occ < 4 else PB[:, (occ - 4) * 512:(occ - 3) * 512]
            res = xcc[:, occ * 512:(occ + 1) * 512] if occ < 4 else xov4[:, (occ - 4) * 512:(occ - 3) * 512]
            u1 = c.op("vector", lambda v, pb=pb, occ=occ: v.tensor_scalar(
                xot[:], pb, bcols[:, 2 * NHC + occ:2 * NHC + occ + 1], None, ALU.add),
                deps=[mm2] + sto + lb + [c.gtok("sxo")])
            u2 = c.op("vector", lambda v, res=res: v.tensor_tensor(xot[:], xot[:], res, ALU.add),
                      deps=[u1])
            sto = [u2]
            c.dma("gpsimd", xo2[occ * 128:(occ + 1) * 128, qs:qs + 512], xot[:], deps=[u2],
                  group="sxo")
        c.barrier()
        mm3 = None
        for hc in range(NHC):
            for oi, occ in enumerate((6, 7)):
                wst = w2s[:, hc * 1024 + occ * 128:hc * 1024 + occ * 128 + 128]
                c.raw("tensor", lambda t, wst=wst: t.ldweights(wst), deps=[])
                mm3 = c.op("tensor", lambda t, wst=wst, oi=oi, hc=hc: t.matmul(
                    PA[:, oi * 512:(oi + 1) * 512], wst, hbuf[:, hc * 512:(hc + 1) * 512],
                    start=(hc == 0), stop=(hc == NHC - 1)))
        sto = []
        for oi, occ in enumerate((6, 7)):
            pb = PA[:, oi * 512:(oi + 1) * 512]
            res = xov4[:, (occ - 4) * 512:(occ - 3) * 512]
            u1 = c.op("vector", lambda v, pb=pb, occ=occ: v.tensor_scalar(
                xot[:], pb, bcols[:, 2 * NHC + occ:2 * NHC + occ + 1], None, ALU.add),
                deps=[mm3] + sto + lb + [c.gtok("sxo")])
            u2 = c.op("vector", lambda v, res=res: v.tensor_tensor(xot[:], xot[:], res, ALU.add),
                      deps=[u1])
            sto = [u2]
            c.dma("gpsimd", xo2[occ * 128:(occ + 1) * 128, qs:qs + 512], xot[:], deps=[u2],
                  group="sxo")
    c.wait_all("gpsimd", [c.gtok("sxo")])
    return c.finish()


def host_l1f_tables(h, Tn, qs64, ks64, n=8192):
    KV = 512 + Tn
    own0 = h * Tn
    p = own0 - 512 + np.arange(KV, dtype=np.float64)
    half = np.arange(0, 64, 2, dtype=np.float64)
    invf = 1.0 / 10000.0 ** (half / 64.0)
    f = p[:, None] * invf[None, :]
    freqs = np.concatenate([f, f], 1)
    cos, sin = np.cos(freqs), np.sin(freqs)
    bs = (half + 0.4 * 64) / (1.4 * 64)
    power = (p - n // 2) / (256 // 2)
    sc = bs[None, :] ** power[:, None]
    scale = np.concatenate([sc, sc], 1)
    sign = np.concatenate([-np.ones(32), np.ones(32)])[None, :]
    roll = np.concatenate([np.arange(32, 64), np.arange(0, 32)]).astype(int)
    q64 = qs64.astype(np.float64)
    k64 = ks64.astype(np.float64)
    tabq = np.stack([q64[None, :] * cos * scale,
                     q64[roll][None, :] * sin * scale * sign]).astype(np.float32)
    tabk = np.stack([k64[None, :] * cos / scale,
                     k64[roll][None, :] * sin / scale * sign]).astype(np.float32)
    return tabq, tabk


def host_l1f_masks(h):
    qr = np.arange(256)[:, None]
    kr = np.arange(512)[None, :]
    causal = (qr <= kr) & (kr <= qr + 256)
    gen = np.where(causal, 0.0, -1e30).astype(np.float32)
    first = np.where(causal & (kr >= 256), 0.0, -1e30).astype(np.float32)
    zero = np.zeros((256, 512), np.float32)
    if h == 0:
        return np.ascontiguousarray(np.stack([zero, first, gen]))
    return np.ascontiguousarray(np.stack([gen, gen, gen]))




def append_glue(c, PBv, yap, Tn, bufs, f32_out=None, bf16_out=None):
    c.barrier()
    (ycc, sqq, yrow, y2row, myb, ryb, xot, g_or, g_oc, g_ob) = bufs
    t_o1 = c.op("gpsimd", lambda g: g.memset(g_or[0:1, 0:128], 1.0))
    t_o2 = c.op("gpsimd", lambda g: g.memset(g_oc[:, 0:1], 1.0))
    t_o3 = c.op("gpsimd", lambda g: g.memset(g_ob[:, 0:1], 1.0))
    epsc, epst = c.constcol(EPS)
    for qcol in range(Tn // 512):
        c.barrier()
        qs = qcol * 512
        c.dma("sync", ycc[:].rearrange("p (cc n) -> p cc n", n=512),
              yap[:, qs:qs + 512].rearrange("(cc p) n -> p cc n", p=128), group="g_ly")
        ly = [c.gtok("g_ly")]
        sqs = []
        for cc in range(8):
            sqs.append(c.op("scalar", lambda s, cc=cc: s.activation(
                sqq[:, cc * 512:(cc + 1) * 512], ycc[:, cc * 512:(cc + 1) * 512], AF.Square),
                deps=ly))
        st1 = None
        st2 = None
        for cc in range(8):
            st1 = c.op("tensor", lambda t, cc=cc: t.matmul(
                PBv[0:1, 0:512], g_oc[:, 0:1], ycc[:, cc * 512:(cc + 1) * 512],
                start=(cc == 0), stop=(cc == 7)), deps=ly + [t_o2] + ([st1] if st1 else []))
            st2 = c.op("tensor", lambda t, cc=cc: t.matmul(
                PBv[0:1, 512:1024], g_ob[:, 0:1], sqq[:, cc * 512:(cc + 1) * 512],
                start=(cc == 0), stop=(cc == 7)),
                deps=[sqs[cc], t_o3] + ([st2] if st2 else []))
        e1 = c.op("vector", lambda v: v.tensor_scalar(y2row[0:1, 0:512], PBv[0:1, 512:1024],
                                                      1.0 / D, None, ALU.mult), deps=[st2])
        e2 = c.op("vector", lambda v: v.tensor_scalar(yrow[0:1, 0:512], PBv[0:1, 0:512],
                                                      1.0 / D, None, ALU.mult), deps=[st1])
        e3 = c.op("vector", lambda v: v.tensor_tensor(myb[0:1, 0:512], yrow[0:1, 0:512],
                                                      yrow[0:1, 0:512], ALU.mult), deps=[e2])
        e4 = c.op("vector", lambda v: v.tensor_tensor(y2row[0:1, 0:512], y2row[0:1, 0:512],
                                                      myb[0:1, 0:512], ALU.subtract),
                  deps=[e1, e3])
        e5 = c.op("scalar", lambda s: s.activation(y2row[0:1, 0:512], y2row[0:1, 0:512], AF.Sqrt,
                                                   bias=epsc[0:1, 0:1]), deps=[e4, epst])
        e6 = c.op("vector", lambda v: v.reciprocal(y2row[0:1, 0:512], y2row[0:1, 0:512]),
                  deps=[e5])
        b3 = c.op("tensor", lambda t: t.matmul(PBv[:, 0:512], g_or[0:1, 0:128], yrow[0:1, 0:512],
                                               start=True, stop=True), deps=[e6, t_o1])
        b3c = c.op("vector", lambda v: v.tensor_copy(myb[:, 0:512], PBv[:, 0:512]), deps=[b3])
        b4 = c.op("tensor", lambda t: t.matmul(PBv[:, 0:512], g_or[0:1, 0:128],
                                               y2row[0:1, 0:512], start=True, stop=True),
                  deps=[b3c])
        b4c = c.op("vector", lambda v: v.tensor_copy(ryb[:, 0:512], PBv[:, 0:512]), deps=[b4])
        sto = []
        for cc in range(8):
            u1 = c.op("vector", lambda v, cc=cc: v.tensor_tensor(
                xot[:, 0:512], ycc[:, cc * 512:(cc + 1) * 512], myb[:, 0:512], ALU.subtract),
                deps=[b3c, b4c] + sto + [c.gtok("g_sxo")])
            u2 = c.op("vector", lambda v: v.tensor_tensor(xot[:, 0:512], xot[:, 0:512],
                                                          ryb[:, 0:512], ALU.mult), deps=[u1])
            sto = [u2]
            if f32_out is not None:
                c.dma("gpsimd", f32_out[cc * 128:(cc + 1) * 128, qs:qs + 512], xot[:, 0:512],
                      deps=[u2], group="g_sxo")
            if bf16_out is not None:
                c.dma("gpsimd", bf16_out[cc * 128:(cc + 1) * 128, qs:qs + 512], xot[:, 0:512],
                      deps=[u2], group="g_sxo")
    c.wait_all("gpsimd", [c.gtok("g_sxo")])


def build_glue(Tn=T, add_base=True, emit_bf16=False):
    c = Ctx("glue")
    y = c.inp("y", [D, Tn])
    x = c.inp("x", [D, Tn]) if add_base else None
    xo = c.outp("xo", [D, Tn])
    ybf = c.outp("ybf", [D, Tn], BF16) if emit_bf16 else None
    onesr = c.sbuf("onesr", [1, 128], F32, side="right")
    onesc = c.sbuf("onesc", [128, 1], F32, side="right")
    onescb = c.sbuf("onescb", [128, 1], BF16, side="right")
    t_o1 = c.op("gpsimd", lambda g: g.memset(onesr[:], 1.0))
    t_o2 = c.op("gpsimd", lambda g: g.memset(onesc[:], 1.0))
    t_o3 = c.op("gpsimd", lambda g: g.memset(onescb[:], 1.0))
    epsc, epst = c.constcol(EPS)
    PB = c.psum("PB", [128, 1024])
    ycc = c.sbuf("ycc", [128, 8 * 512], F32)
    xcc = c.sbuf("xcc", [128, 8 * 512], F32)
    sqq = c.sbuf("sqq", [128, 8 * 512], BF16)
    yrow = c.sbuf("yrow", [1, 512], F32)
    y2row = c.sbuf("y2row", [1, 512], F32)
    myb = c.sbuf("myb", [128, 512], F32)
    ryb = c.sbuf("ryb", [128, 512], F32)
    xot = c.sbuf("xot", [128, 512], F32)

    for qcol in range(Tn // 512):
        c.barrier()
        qs = qcol * 512
        c.dma("sync", ycc[:].rearrange("p (cc n) -> p cc n", n=512),
              y[:, qs:qs + 512].rearrange("(cc p) n -> p cc n", p=128), group="ly")
        if add_base:
            c.dma("sync", xcc[:].rearrange("p (cc n) -> p cc n", n=512),
                  x[:, qs:qs + 512].rearrange("(cc p) n -> p cc n", p=128), group="ly")
        ly = [c.gtok("ly")]
        sqs = []
        for cc in range(8):
            sqs.append(c.op("scalar", lambda s, cc=cc: s.activation(
                sqq[:, cc * 512:(cc + 1) * 512], ycc[:, cc * 512:(cc + 1) * 512], AF.Square),
                deps=ly))
        st1 = None
        st2 = None
        for cc in range(8):
            st1 = c.op("tensor", lambda t, cc=cc: t.matmul(
                PB[0:1, 0:512], onesc[:], ycc[:, cc * 512:(cc + 1) * 512],
                start=(cc == 0), stop=(cc == 7)), deps=ly + [t_o2] + ([st1] if st1 else []))
            st2 = c.op("tensor", lambda t, cc=cc: t.matmul(
                PB[0:1, 512:1024], onescb[:], sqq[:, cc * 512:(cc + 1) * 512],
                start=(cc == 0), stop=(cc == 7)),
                deps=[sqs[cc], t_o3] + ([st2] if st2 else []))
        e1 = c.op("vector", lambda v: v.tensor_scalar(y2row[:], PB[0:1, 512:1024], 1.0 / D, None,
                                                      ALU.mult), deps=[st2])
        e2 = c.op("vector", lambda v: v.tensor_scalar(yrow[:], PB[0:1, 0:512], 1.0 / D, None,
                                                      ALU.mult), deps=[st1])
        e3 = c.op("vector", lambda v: v.tensor_tensor(myb[0:1, 0:512], yrow[:], yrow[:], ALU.mult),
                  deps=[e2])
        e4 = c.op("vector", lambda v: v.tensor_tensor(y2row[:], y2row[:], myb[0:1, 0:512],
                                                      ALU.subtract), deps=[e1, e3])
        e5 = c.op("scalar", lambda s: s.activation(y2row[:], y2row[:], AF.Sqrt,
                                                   bias=epsc[0:1, 0:1]), deps=[e4, epst])
        e6 = c.op("vector", lambda v: v.reciprocal(y2row[:], y2row[:]), deps=[e5])
        b3 = c.op("tensor", lambda t: t.matmul(PB[:, 0:512], onesr[:], yrow[:], start=True,
                                               stop=True), deps=[e6, t_o1])
        b3c = c.op("vector", lambda v: v.tensor_copy(myb[:], PB[:, 0:512]), deps=[b3])
        b4 = c.op("tensor", lambda t: t.matmul(PB[:, 0:512], onesr[:], y2row[:], start=True,
                                               stop=True), deps=[b3c])
        b4c = c.op("vector", lambda v: v.tensor_copy(ryb[:], PB[:, 0:512]), deps=[b4])
        sto = []
        for cc in range(8):
            u1 = c.op("vector", lambda v, cc=cc: v.tensor_tensor(
                xot[:], ycc[:, cc * 512:(cc + 1) * 512], myb[:], ALU.subtract),
                deps=[b3c, b4c] + sto + [c.gtok("sxo")])
            u2 = c.op("vector", lambda v: v.tensor_tensor(xot[:], xot[:], ryb[:], ALU.mult),
                      deps=[u1])
            if add_base:
                u2 = c.op("vector", lambda v, cc=cc: v.tensor_tensor(
                    xot[:], xot[:], xcc[:, cc * 512:(cc + 1) * 512], ALU.add), deps=[u2])
            sto = [u2]
            c.dma("gpsimd", xo[cc * 128:(cc + 1) * 128, qs:qs + 512], xot[:], deps=[u2],
                  group="sxo")
            if emit_bf16:
                c.dma("gpsimd", ybf[cc * 128:(cc + 1) * 128, qs:qs + 512], xot[:], deps=[u2],
                      group="sxo")
    c.wait_all("gpsimd", [c.gtok("sxo")])
    return c.finish()
